# version 19
# baseline (speedup 1.0000x reference)
import numpy as np
import ml_dtypes
import concourse.bass as bass
import concourse.mybir as mybir
from concourse.bass_utils import run_bass_kernel_spmd

F32 = mybir.dt.float32
BF16 = mybir.dt.bfloat16
I32 = mybir.dt.int32
AF = mybir.ActivationFunctionType
ALU = mybir.AluOpType
AX = mybir.AxisListType
BF16NP = ml_dtypes.bfloat16

D = 1024
T = 2048
NCH = 8
HID0 = 3072
NH0 = HID0 // 128
QSC = 1.0 / 127.0
EPS = 1e-5


class Prog:
    ENGINES = ("sync", "vector", "scalar", "tensor", "gpsimd")

    def __init__(self, nc):
        self.nc = nc
        self.q = {e: [] for e in self.ENGINES}
        self.pool = []
        self.lmap = {}
        self._tensors = {}
        self._uid = 0
        self.nbar = 0
        self.bar = None
        self.arena0 = None
        self.arena_off = 0
        self.arena_hwm = 0

    def sb(self, name, shape, dtype=F32):
        if self.arena0 is None:
            self.arena0 = (self.nc.sbuf_base + 63) & ~63
        esz = {F32: 4, BF16: 2, I32: 4}[dtype]
        nbytes = int(shape[1]) * esz
        off = (self.arena0 + self.arena_off + 63) & ~63
        self._uid += 1
        t = self.nc.alloc_sbuf_tensor_at(f"{name}_u{self._uid}", list(shape), dtype,
                                         offset=off)
        self.arena_off = off + nbytes - self.arena0
        self.arena_hwm = max(self.arena_hwm, self.arena_off)
        assert self.arena0 + self.arena_off <= self.nc.sbuf_top - 64, \
            f"arena overflow at {name}: {self.arena0 + self.arena_off}"
        self._tensors[name] = t
        return t

    def ps(self, name, shape, dtype=F32):
        t = self.nc.alloc_psum_tensor(name, list(shape), dtype)
        self._tensors[name] = t
        return t

    def _sem(self, name):
        if name not in self.lmap:
            idx = len(self.lmap)
            if idx >= len(self.pool):
                self.pool.append([self.nc.alloc_semaphore(f"ps{len(self.pool)}"), 0])
            self.lmap[name] = (idx, self.pool[idx][1])
        idx, base = self.lmap[name]
        return self.pool[idx], base

    def count(self, name):
        s, base = self._sem(name)
        return s[1] - base

    def wait(self, eng, semname, val=None):
        s, base = self._sem(semname)
        v = (s[1] - base) if val is None else val
        if v > 0:
            self.q[eng].append(("w", s[0], v + base))

    def op(self, eng, fn, inc=None, n=1):
        if inc is not None:
            s, _ = self._sem(inc)
            s[1] += n
            self.q[eng].append(("o", fn, s[0], n))
        else:
            self.q[eng].append(("o", fn, None, 0))

    def dma(self, eng, fn, inc):
        self.op(eng, fn, inc=inc, n=16)

    def barrier(self):
        if self.bar is None:
            self.bar = self.nc.alloc_semaphore("bar")
        for h, c in self.pool:
            if c > 0:
                self.q["sync"].append(("w", h, c))
        self.nbar += 1
        for eng in self.ENGINES:
            self.q[eng].append(("o", (lambda e: e.nop()), self.bar, 1))
        for eng in self.ENGINES:
            self.q[eng].append(("w", self.bar, 5 * self.nbar))
        self.lmap = {}
        self.arena_off = 0

    def finish_and_emit(self):
        for h, c in self.pool:
            if c > 0:
                self.q["sync"].append(("w", h, c))
        nc = self.nc
        with nc.Block() as block:
            for eng_name, dec in (("sync", block.sync), ("vector", block.vector),
                                  ("scalar", block.scalar), ("tensor", block.tensor),
                                  ("gpsimd", block.gpsimd)):
                items = self.q[eng_name]
                if not items:
                    continue

                def body(e, items=items):
                    for it in items:
                        if it[0] == "w":
                            e.wait_ge(it[1], it[2])
                        else:
                            ins = it[1](e)
                            if it[2] is not None:
                                ins.then_inc(it[2], it[3])
                dec(body)


def make_ln_scratch(p, pre="lns"):
    rows1 = p.sb(pre + "_rows1", [33, T], F32)
    rows2 = p.sb(pre + "_rows2", [33, T], F32)
    return dict(
        ones_b=p.sb(pre + "_onesb", [128, 1], BF16),
        ones_f=p.sb(pre + "_onesf", [1, 128], F32),
        sq=[p.sb(pre + f"_sq{i}", [128, T], BF16) for i in range(2)],
        rowm=rows1[0:1, :],
        rowq=rows1[32:33, :],
        rowt=rows2[32:33, :],
        rowr=rows2[0:1, :],
        epst=p.sb(pre + "_eps", [1, 1], F32),
        init=[False],
    )


def emit_ln_fm(p, pre, x_ch, xn_ch, ps1, ps2, sc, lnw=None, lnb=None, x_ready=None,
               x_chunks=None, bc_sb=None, x_prod=None):
    nc = p.nc
    ones_b = sc["ones_b"]; ones_f = sc["ones_f"]; sq = sc["sq"]
    rowm = sc["rowm"]; rowq = sc["rowq"]; rowr = sc["rowr"]
    rowt = sc["rowt"]; epst = sc["epst"]
    s_te, s_act, s_dve = pre + "_te", pre + "_act", pre + "_dve"
    NS = T // 512
    in_bf = (x_ch[0].dtype == BF16)

    xb = x_ch if in_bf else xn_ch
    if not sc["init"][0]:
        sc["init"][0] = True
        p.op("vector", lambda e: e.memset(ones_b[:], 1.0), inc=s_dve)
        p.op("vector", lambda e: e.memset(ones_f[:], 1.0), inc=s_dve)
        p.op("vector", lambda e: e.memset(epst[:], EPS), inc=s_dve)
    cast_start = p.count(s_dve)
    if x_chunks is not None:
        xname, xbase = x_chunks
        if not in_bf:
            for c in range(NCH):
                p.wait("vector", xname, xbase + 16 * (c + 1))
                p.op("vector", lambda e, c=c: e.tensor_copy(xb[c], x_ch[c]), inc=s_dve)
    else:
        if x_ready is not None:
            p.wait("vector", *x_ready)
        if not in_bf:
            for c in range(NCH):
                p.op("vector", lambda e, c=c: e.tensor_copy(xb[c], x_ch[c]), inc=s_dve)
    casts_done = p.count(s_dve)

    if in_bf and x_ready is not None and x_prod is None:
        p.wait("tensor", *x_ready)
        p.wait("scalar", *x_ready)
    for s in range(NS):
        for c in range(NCH):
            if not in_bf:
                p.wait("tensor", s_dve, cast_start + c + 1)
            elif x_prod is not None and s == 0:
                p.wait("tensor", x_prod[0], x_prod[1] + x_prod[2] * (c + 1))
            p.op("tensor",
                 lambda e, s=s, c=c: e.matmul(ps1[0:1, s * 512:(s + 1) * 512],
                                              ones_b[:], xb[c][:, s * 512:(s + 1) * 512],
                                              start=(c == 0), stop=(c == NCH - 1)),
                 inc=(s_te if (s == NS - 1 and c == NCH - 1) else None))
    for c in range(NCH):
        if not in_bf:
            p.wait("scalar", s_dve, cast_start + c + 1)
        elif x_prod is not None:
            p.wait("scalar", x_prod[0], x_prod[1] + x_prod[2] * (c + 1))
        else:
            p.wait("scalar", s_dve, casts_done)
        if c >= 2:
            p.wait("scalar", s_te, 1 + (c - 1))
        p.op("scalar", lambda e, c=c: e.activation(sq[c % 2][:], xb[c], AF.Square),
             inc=s_act)
        p.wait("tensor", s_act, p.count(s_act))
        for s in range(NS):
            p.op("tensor",
                 lambda e, s=s, c=c: e.matmul(ps1[32:33, s * 512:(s + 1) * 512],
                                              ones_b[:], sq[c % 2][:, s * 512:(s + 1) * 512],
                                              start=(c == 0), stop=(c == NCH - 1)),
                 inc=(s_te if s == NS - 1 else None))
    p.wait("vector", s_te, p.count(s_te))
    p.op("vector", lambda e: e.tensor_scalar_mul(rowm, ps1[0:1, :], 1.0 / D), inc=s_dve)
    p.op("vector", lambda e: e.tensor_scalar_mul(rowq, ps1[32:33, :], 1.0 / D), inc=s_dve)
    p.wait("scalar", s_dve, p.count(s_dve))
    p.op("scalar", lambda e: e.activation(rowt, rowm, AF.Square), inc=s_act)
    p.wait("vector", s_act, p.count(s_act))
    p.op("vector", lambda e: e.tensor_sub(rowq, rowq, rowt), inc=s_dve)
    p.wait("scalar", s_dve, p.count(s_dve))
    p.op("scalar", lambda e: e.activation(rowt, rowq, AF.Sqrt, bias=epst[:]), inc=s_act)
    p.wait("vector", s_act, p.count(s_act))
    p.op("vector", lambda e: e.reciprocal(rowr, rowt), inc=s_dve)
    fin_done = p.count(s_dve)
    p.wait("tensor", s_dve, fin_done)
    for s in range(NS):
        p.op("tensor", lambda e, s=s: e.matmul(ps2[:, s * 512:(s + 1) * 512], ones_f[:],
                                               rowr[:, s * 512:(s + 1) * 512],
                                               start=True, stop=True),
             inc=(s_te if s == NS - 1 else None))
    for s in range(NS):
        p.op("tensor", lambda e, s=s: e.matmul(ps1[:, s * 512:(s + 1) * 512], ones_f[:],
                                               rowm[:, s * 512:(s + 1) * 512],
                                               start=True, stop=True),
             inc=(s_te if s == NS - 1 else None))
    bc_done = p.count(s_te)
    p.wait("vector", s_te, bc_done)
    if bc_sb is not None:
        bcm, bcr = bc_sb[:, 0:T], bc_sb[:, T:2 * T]
        p.op("vector", lambda e, bcm=bcm: e.tensor_copy(bcm, ps1[:]), inc=s_dve)
        p.op("vector", lambda e, bcr=bcr: e.tensor_copy(bcr, ps2[:]), inc=s_dve)
    else:
        bcm, bcr = ps1[:], ps2[:]
    for c in range(NCH):
        p.op("vector", lambda e, c=c, bcm=bcm: e.tensor_sub(xn_ch[c], x_ch[c], bcm), inc=s_dve)
        p.op("vector", lambda e, c=c, bcr=bcr: e.tensor_mul(xn_ch[c], xn_ch[c], bcr), inc=s_dve)
        if lnw is not None:
            p.op("vector", lambda e, c=c: e.tensor_scalar(xn_ch[c], xn_ch[c],
                                                          lnw[c], lnb[c],
                                                          ALU.mult, ALU.add), inc=s_dve)
    return p.count(s_dve)


def build_l1():
    nc = bass.Bass("TRN2", target_bir_lowering=False, debug=False)
    p = Prog(nc)
    x_d = nc.dram_tensor("x_fm", [D, T], F32, kind="ExternalInput")
    wgT_d = nc.dram_tensor("wgT", [D, HID0], I32, kind="ExternalInput")
    wuT_d = nc.dram_tensor("wuT", [D, HID0], I32, kind="ExternalInput")
    wdT_d = nc.dram_tensor("wdT", [HID0, D], I32, kind="ExternalInput")
    xn_d = nc.dram_tensor("xn_fm", [D, T], BF16, kind="ExternalOutput")
    acc_d = nc.dram_tensor("acc_fm", [D, T], F32, kind="ExternalOutput")

    pbig1 = p.ps("pbig1", [128, T], F32)
    pbig2 = p.ps("pbig2", [128, T], F32)
    x_sb = p.sb("x_sb", [128, NCH * T], F32)
    xn_sb = p.sb("xn_sb", [128, NCH * T], BF16)
    x_ch = [x_sb[:, c * T:(c + 1) * T] for c in range(NCH)]
    xn_ch = [xn_sb[:, c * T:(c + 1) * T] for c in range(NCH)]

    for c in range(NCH):
        p.dma("sync", lambda e, c=c: e.dma_start(x_ch[c], x_d.ap()[c * 128:(c + 1) * 128, :]),
              inc="ld_x")
    lnsc = make_ln_scratch(p)
    ln_done = emit_ln_fm(p, "ln0", x_ch, xn_ch, pbig1, pbig2, lnsc, x_ready=("ld_x", 8 * 16))

    wg_sb = [p.sb(f"wg{i}", [128, NCH * 128], BF16) for i in range(2)]
    wu_sb = [p.sb(f"wu{i}", [128, NCH * 128], BF16) for i in range(2)]
    wd_sb = [p.sb(f"wd{i}", [128, NH0 * 128], BF16) for i in range(2)]
    h_sb = p.sb("h_sb", [128, NH0 * 512], BF16)
    g_scr = p.sb("g_scr", [128, 512], BF16)
    ps_g = pbig1[:, 0:512]
    ps_u = pbig1[:, 512:1024]
    ps_d = [pbig1[:, 1024:1536], pbig1[:, 1536:2048]]

    n_wdma = 0

    for st in range(4):
        tq = slice(st * 512, (st + 1) * 512)
        for hc in range(NH0):
            buf = hc % 2
            src_g = bass.AP(wgT_d, hc * 128, [[HID0, 128], [128 * HID0, 8], [1, 128]])
            src_u = bass.AP(wuT_d, hc * 128, [[HID0, 128], [128 * HID0, 8], [1, 128]])
            dst_g = wg_sb[buf][:].rearrange("p (c f) -> p c f", c=8)
            dst_u = wu_sb[buf][:].rearrange("p (c f) -> p c f", c=8)
            it = st * NH0 + hc
            if it >= 2:
                p.wait("gpsimd", "te_gu", it - 1)
            p.dma("gpsimd", lambda e, s=src_g, d=dst_g: e.dma_start(d, s), inc=f"wgu{buf}")
            p.dma("gpsimd", lambda e, s=src_u, d=dst_u: e.dma_start(d, s), inc=f"wgu{buf}")
            p.wait("tensor", f"wgu{buf}", 32 * (it // 2 + 1))
            if st == 0 and hc == 0:
                p.wait("tensor", "ln0_dve", ln_done)
            if st * NH0 + hc >= 1:
                p.wait("tensor", "evac_gu", 2 * (st * NH0 + hc - 1) + 2)
            for dc in range(NCH):
                p.op("tensor", lambda e, buf=buf, dc=dc, tq=tq: e.matmul(
                    ps_g, wg_sb[buf][:, dc * 128:(dc + 1) * 128],
                    xn_ch[dc][:, tq], start=(dc == 0), stop=(dc == NCH - 1)))
                p.op("tensor", lambda e, buf=buf, dc=dc, tq=tq: e.matmul(
                    ps_u, wu_sb[buf][:, dc * 128:(dc + 1) * 128],
                    xn_ch[dc][:, tq], start=(dc == 0), stop=(dc == NCH - 1)),
                    inc=("te_gu" if dc == NCH - 1 else None))
            p.wait("scalar", "te_gu", p.count("te_gu"))
            if st >= 1:
                p.wait("scalar", "te_d", st * NCH)
            if st * NH0 + hc >= 1:
                p.wait("scalar", "dve_h", st * NH0 + hc)
            p.op("scalar", lambda e: e.activation(g_scr[:], ps_g[:], AF.Silu, scale=QSC),
                 inc="evac_gu")
            p.op("scalar", lambda e, hc=hc: e.activation(
                h_sb[:, hc * 512:(hc + 1) * 512], ps_u[:], AF.Copy, scale=QSC),
                inc="evac_gu")
            p.wait("vector", "evac_gu", p.count("evac_gu"))
            p.op("vector", lambda e, hc=hc: e.tensor_mul(
                h_sb[:, hc * 512:(hc + 1) * 512], g_scr[:],
                h_sb[:, hc * 512:(hc + 1) * 512]), inc="dve_h")
        for dc in range(NCH):
            buf = dc % 2
            src_d = bass.AP(wdT_d, dc * 128, [[D, 128], [128 * D, NH0], [1, 128]])
            dst_d = wd_sb[buf][:].rearrange("p (c f) -> p c f", c=NH0)
            it = st * NCH + dc
            if it >= 2:
                p.wait("gpsimd", "te_d", it - 1)
            p.dma("gpsimd", lambda e, s=src_d, d=dst_d: e.dma_start(d, s), inc=f"wdd{buf}")
            p.wait("tensor", f"wdd{buf}", 16 * (it // 2 + 1))
            p.wait("tensor", "dve_h", p.count("dve_h"))
            if st * NCH + dc >= 2:
                p.wait("tensor", "dve_acc", (st * NCH + dc - 1))
            for hc in range(NH0):
                p.op("tensor", lambda e, buf=buf, hc=hc, dc=dc: e.matmul(
                    ps_d[dc % 2], wd_sb[buf][:, hc * 128:(hc + 1) * 128],
                    h_sb[:, hc * 512:(hc + 1) * 512],
                    start=(hc == 0), stop=(hc == NH0 - 1)),
                    inc=("te_d" if hc == NH0 - 1 else None))
            p.wait("vector", "te_d", p.count("te_d"))
            p.op("vector", lambda e, dc=dc, st=st: e.scalar_tensor_tensor(
                x_ch[dc][:, st * 512:(st + 1) * 512], ps_d[dc % 2], QSC,
                x_ch[dc][:, st * 512:(st + 1) * 512], ALU.mult, ALU.add),
                inc="dve_acc")

    p.wait("sync", "ln0_dve", ln_done)
    for c in range(NCH):
        p.dma("sync", lambda e, c=c: e.dma_start(xn_d.ap()[c * 128:(c + 1) * 128, :], xn_ch[c]),
              inc="st_xn")
    p.wait("sync", "dve_acc", p.count("dve_acc"))
    for c in range(NCH):
        p.dma("sync", lambda e, c=c: e.dma_start(acc_d.ap()[c * 128:(c + 1) * 128, :], x_ch[c]),
              inc="st_acc")
    p.finish_and_emit()
    return nc


def _fm(a):
    return np.ascontiguousarray(a.T)


def kernel_unfused(**inputs):
    x = np.asarray(inputs["L0_attn_x"])
    wg = np.asarray(inputs["L0_ffn_w_gate"])
    wu = np.asarray(inputs["L0_ffn_w_up"])
    wd = np.asarray(inputs["L0_ffn_w_down"])
    log_dt = np.asarray(inputs["L0_attn_log_dt"])
    Bm = np.asarray(inputs["L0_attn_B"]); Cm = np.asarray(inputs["L0_attn_C"])
    Dv = np.asarray(inputs["L0_attn_D"])
    lnw = np.asarray(inputs["L1_attn_ln_w"]); lnb = np.asarray(inputs["L1_attn_ln_b"])
    ipw = np.asarray(inputs["L1_attn_in_proj_w"]); ipb = np.asarray(inputs["L1_attn_in_proj_b"])
    opw = np.asarray(inputs["L1_attn_out_proj_w"]); opb = np.asarray(inputs["L1_attn_out_proj_b"])
    w0 = np.asarray(inputs["L1_attn_w0"]); w1 = np.asarray(inputs["L1_attn_w1"])
    w2 = np.asarray(inputs["L1_attn_w2"])
    gw = np.asarray(inputs["L1_ffn_gate_w"]); bal = np.asarray(inputs["L1_ffn_balance_bias"])
    mw1 = np.asarray(inputs["L1_ffn_w1"]); mw2 = np.asarray(inputs["L1_ffn_w2"])

    xt = x.reshape(16384, D)
    shards = [xt[i * T:(i + 1) * T] for i in range(8)]
    cores = list(range(8))

    wgT = np.ascontiguousarray(wg.T); wuT = np.ascontiguousarray(wu.T)
    wdT = np.ascontiguousarray(wd.T)
    nc1 = build_l1()
    res = run_bass_kernel_spmd(nc1, [
        {"x_fm": _fm(s), "wgT": wgT, "wuT": wuT, "wdT": wdT} for s in shards],
        core_ids=cores)
    xn_fm = [np.asarray(r["xn_fm"]) for r in res.results]
    acc_fm = [np.asarray(r["acc_fm"]) for r in res.results]

    xn_tok = [np.concatenate([xn_fm[b * 4 + j].T for j in range(4)], axis=0)
              for b in range(2)]
    A = (-0.5 * np.arange(1, 65)).astype(np.float32)
    A_bc = np.tile(A, (128, 1))
    iA_bc = np.tile((1.0 / A).astype(np.float32), (128, 1))
    ident = np.eye(128, dtype=np.float32)
    nc2 = build_l2()
    in2 = []
    for c in range(8):
        sl = slice(c * 128, (c + 1) * 128)
        in2.append({
            "xn0": np.ascontiguousarray(xn_tok[0][:, sl]),
            "xn1": np.ascontiguousarray(xn_tok[1][:, sl]),
            "log_dt": log_dt[sl].reshape(128, 1), "Bmat": Bm[sl], "Cmat": Cm[sl],
            "Dvec": Dv[sl].reshape(128, 1), "A_bc": A_bc, "invA_bc": iA_bc,
            "ident_f": ident})
    res = run_bass_kernel_spmd(nc2, in2, core_ids=cores)
    yssm = [np.concatenate([np.asarray(res.results[c][f"y{b}"]) for c in range(8)], axis=1)
            for b in range(2)]

    ipT = np.ascontiguousarray(ipw.T)
    nc3 = build_l3a()
    in3 = []
    for c in range(8):
        b = c // 4
        sl = slice((c % 4) * T, (c % 4 + 1) * T)
        in3.append({
            "acc_fm": acc_fm[c],
            "yssm_fm": np.ascontiguousarray(yssm[b][sl].T),
            "in_projT": ipT, "in_proj_b": ipb.reshape(-1, 1),
            "ln_w": lnw.reshape(-1, 1), "ln_b": lnb.reshape(-1, 1)})
    res = run_bass_kernel_spmd(nc3, in3, core_ids=cores)
    x1_fm = [np.asarray(r["x1_fm"]) for r in res.results]
    xn1_fm = [np.asarray(r["xn1_fm"]) for r in res.results]
    xn2_fm = [np.asarray(r["xn2_fm"]) for r in res.results]
    q_fm = [np.asarray(r["q_fm"]) for r in res.results]
    k_fm = [np.asarray(r["k_fm"]) for r in res.results]
    v_tok = [np.asarray(r["v_tok"]) for r in res.results]

    K_full = [np.ascontiguousarray(np.concatenate([k_fm[b * 4 + j] for j in range(4)], axis=1))
              for b in range(2)]
    V_full = [np.ascontiguousarray(np.concatenate([v_tok[b * 4 + j] for j in range(4)], axis=0))
              for b in range(2)]
    opT = np.ascontiguousarray(opw.T)
    nc4 = build_l3b1()
    in4 = []
    for c in range(8):
        b = c // 4
        in4.append({
            "x1_fm": x1_fm[c], "q_fm": q_fm[c],
            "K_full": K_full[b], "V_full": V_full[b],
            "out_projT": opT, "out_proj_b": opb.reshape(-1, 1),
            "v_bias": ipb[2048:].reshape(-1, 1)})
    res = run_bass_kernel_spmd(nc4, in4, core_ids=cores)
    y1_fm = [np.asarray(r["y1_fm"]) for r in res.results]

    w0T = np.ascontiguousarray(np.transpose(w0, (0, 2, 1))).reshape(4 * 256, 256)
    w2T = np.ascontiguousarray(np.transpose(w2, (0, 2, 1))).reshape(4 * 256, 256)
    w1l = np.ascontiguousarray(w1).reshape(4 * 256, 256)
    w1T = np.ascontiguousarray(np.transpose(w1, (0, 2, 1))).reshape(4 * 256, 256)
    gwT = np.ascontiguousarray(gw.T)
    mw1T = np.ascontiguousarray(np.transpose(mw1, (0, 2, 1))).reshape(8 * 1024, 2048)
    mw2T = np.ascontiguousarray(np.transpose(mw2, (0, 2, 1))).reshape(8 * 2048, 1024)
    identb = np.eye(128, dtype=BF16NP)
    nc5 = build_l3b2()
    in5 = []
    for c in range(8):
        in5.append({
            "y1_fm": y1_fm[c], "xn1_fm": xn1_fm[c], "xn2_fm": xn2_fm[c],
            "w0T": w0T, "w2T": w2T, "w1l": w1l, "w1T": w1T,
            "gate_wT": gwT, "bal_b": bal.reshape(1, 8),
            "moew1T": mw1T, "moew2T": mw2T, "ident_b": identb})
    res = run_bass_kernel_spmd(nc5, in5, core_ids=cores)
    out = np.concatenate([np.asarray(r["out_fm"]).T for r in res.results], axis=0)
    return np.ascontiguousarray(out.reshape(2, 8192, D).astype(np.float32))


def build_l2(dbg=False):
    nc = bass.Bass("TRN2", target_bir_lowering=False, debug=False)
    p = Prog(nc)
    xn_d = [nc.dram_tensor(f"xn{b}", [8192, 128], BF16, kind="ExternalInput") for b in range(2)]
    ldt_d = nc.dram_tensor("log_dt", [128, 1], F32, kind="ExternalInput")
    B_d = nc.dram_tensor("Bmat", [128, 64], F32, kind="ExternalInput")
    C_d = nc.dram_tensor("Cmat", [128, 64], F32, kind="ExternalInput")
    Dv_d = nc.dram_tensor("Dvec", [128, 1], F32, kind="ExternalInput")
    A_d = nc.dram_tensor("A_bc", [128, 64], F32, kind="ExternalInput")
    iA_d = nc.dram_tensor("invA_bc", [128, 64], F32, kind="ExternalInput")
    id_d = nc.dram_tensor("ident_f", [128, 128], F32, kind="ExternalInput")
    y_d = [nc.dram_tensor(f"y{b}", [8192, 128], BF16, kind="ExternalOutput") for b in range(2)]
    if dbg:
        dbgK_d = nc.dram_tensor("dbgK", [128, 256], F32, kind="ExternalOutput")
        dbgX_d = nc.dram_tensor("dbgX", [128, 8192], BF16, kind="ExternalOutput")
        dbgP_d = nc.dram_tensor("dbgP", [64, 8192], BF16, kind="ExternalOutput")
        dbgS_d = nc.dram_tensor("dbgS", [64, 128 * 65], BF16, kind="ExternalOutput")
        dbgY_d = nc.dram_tensor("dbgY", [128, 8192], BF16, kind="ExternalOutput")
        dbgU_d = nc.dram_tensor("dbgU", [128, 8192], BF16, kind="ExternalOutput")
        dbgV_d = nc.dram_tensor("dbgV", [128, 64 * 129], BF16, kind="ExternalOutput")
        dbgQ_d = nc.dram_tensor("dbgQ", [64, 128], F32, kind="ExternalOutput")
    krep_d = nc.dram_tensor("krep", [128, 128 * 256], BF16, kind="Internal")
    urev_d = nc.dram_tensor("urev", [128, 8192], BF16, kind="Internal")
    v_d = nc.dram_tensor("vdram", [128, 64 * 129], BF16, kind="Internal")

    ldt = p.sb("ldt", [128, 1], F32)
    Bs = p.sb("Bs", [128, 64], F32)
    Cs = p.sb("Cs", [128, 64], F32)
    Ds = p.sb("Ds", [128, 1], F32)
    Abc = p.sb("Abc", [128, 64], F32)
    iAbc = p.sb("iAbc", [128, 64], F32)
    idf = p.sb("idf", [128, 128], F32)
    dt = p.sb("dt", [128, 1], F32)
    lam = p.sb("lam", [128, 64], F32)
    ones64 = p.sb("ones64", [128, 64], F32)
    CB = p.sb("CB", [128, 64], F32)
    ucur = p.sb("ucur", [128, 64], F32)
    tmp64 = p.sb("tmp64", [128, 64], F32)
    urev_sb = p.sb("urev_sb", [128, 8192], BF16)
    v_sb = p.sb("v_sb", [128, 64 * 129], BF16)
    K_sb = p.sb("K_sb", [128, 256], F32)
    K_bf = p.sb("K_bf", [128, 256], BF16)
    lamQT = p.sb("lamQT", [64, 128], F32)
    xraw = [p.sb(f"xraw{b}", [128, 8192], BF16) for b in range(2)]
    y_sb = [p.sb(f"ysb{b}", [128, 8192], BF16) for b in range(2)]
    P_all = [p.sb(f"Pall{b}", [64, 8192], BF16) for b in range(2)]
    S_shift = p.sb("S_shift", [64, 128 * 65], F32)
    S_bf = p.sb("S_bf", [64, 128 * 65], BF16)
    T8 = [p.sb(f"T8_{i}", [128, 8 * 128], BF16) for i in range(2)]
    W8 = [p.sb(f"W8_{i}", [128, 8 * 64], BF16) for i in range(2)]
    M8 = [p.sb(f"M8_{i}", [64, 8 * 128], BF16) for i in range(2)]
    pY = [p.ps(f"pY{b}", [128, 512], F32) for b in range(2)]
    pP = [p.ps(f"pP{b}", [64, 512], F32) for b in range(2)]
    pT = p.ps("pT", [64, 128], F32)

    for t, d in ((ldt, ldt_d), (Bs, B_d), (Cs, C_d), (Ds, Dv_d), (Abc, A_d),
                 (iAbc, iA_d), (idf, id_d)):
        p.dma("sync", lambda e, t=t, d=d: e.dma_start(t[:], d.ap()), inc="ld")
    p.wait("scalar", "ld", p.count("ld"))
    p.op("scalar", lambda e: e.activation(dt[:], ldt[:], AF.Exp), inc="act")
    p.wait("vector", "act", p.count("act"))
    p.op("vector", lambda e: e.tensor_scalar_mul(lam[:], Abc[:], dt[:, 0:1]))
    p.wait("scalar", "vchain", 0)
    p.op("vector", lambda e: e.memset(ones64[:], 1.0))
    p.op("vector", lambda e: e.tensor_copy(tmp64[:], lam[:]), inc="dve")
    p.wait("scalar", "dve", p.count("dve"))
    p.op("scalar", lambda e: e.activation(lam[:], tmp64[:], AF.Exp), inc="act")
    p.wait("vector", "act", p.count("act"))
    p.op("vector", lambda e: e.tensor_sub(CB[:], lam[:], ones64[:]))
    p.op("vector", lambda e: e.tensor_mul(CB[:], CB[:], iAbc[:]))
    p.op("vector", lambda e: e.tensor_mul(CB[:], CB[:], Bs[:]))
    p.op("vector", lambda e: e.tensor_mul(CB[:], CB[:], Cs[:]))
    p.op("vector", lambda e: e.memset(ucur[:], 1.0))
    p.op("vector", lambda e: e.memset(K_sb[:], 0.0))
    v_r = v_sb[:].rearrange("p (n j) -> p n j", j=129)
    for j in range(129):
        if j <= 127:
            p.op("vector", lambda e, j=j: e.tensor_copy(
                urev_sb[:, (127 - j) * 64:(128 - j) * 64], ucur[:]))
        p.op("vector", lambda e: e.tensor_mul(tmp64[:], ucur[:], CB[:]))
        p.op("vector", lambda e, j=j: e.tensor_copy(v_r[:, :, j], tmp64[:]))
        p.op("vector", lambda e, j=j: e.reduce_sum(K_sb[:, 127 + j:128 + j], tmp64[:], AX.X))
        if j <= 127:
            p.op("vector", lambda e: e.tensor_mul(ucur[:], ucur[:], lam[:]))
    p.op("vector", lambda e: e.tensor_add(K_sb[:, 127:128], K_sb[:, 127:128], Ds[:]))
    p.op("vector", lambda e: e.tensor_copy(K_bf[:], K_sb[:]), inc="chain")
    chain_done = p.count("chain")

    p.wait("gpsimd", "chain", chain_done)
    p.dma("gpsimd", lambda e: e.dma_start(urev_d.ap(), urev_sb[:]), inc="guv")
    p.dma("gpsimd", lambda e: e.dma_start(v_d.ap(), v_sb[:]), inc="guv")
    p.wait("sync", "chain", chain_done)
    for k in range(128):
        p.dma("sync", lambda e, k=k: e.dma_start(
            bass.AP(krep_d, k * 256, [[256 * 128, 128], [1, 256]]), K_bf[:]), inc="krep")
    p.wait("tensor", "chain", chain_done)
    p.op("tensor", lambda e: e.transpose(pT[:], ucur[:, 0:64], idf[:]), inc="te_t")
    p.wait("vector", "te_t", p.count("te_t"))
    p.op("vector", lambda e: e.tensor_copy(lamQT[:], pT[:]), inc="lamqt")

    for b in range(2):
        for piece in range(8):
            src = bass.AP(xn_d[b], piece * 8 * 16384,
                          [[128, 128], [16384, 8], [1, 128]])
            dst = xraw[b][:].rearrange("p (c d) -> p c d", c=64)[:, piece * 8:(piece + 1) * 8, :]
            p.dma("scalar", lambda e, s=src, d=dst: e.dma_start(d, s), inc=f"ldx{b}")

    if dbg:
        p.wait("sync", "ldx0", 8 * 16)
        p.dma("sync", lambda e: e.dma_start(dbgX_d.ap(), xraw[0][:]), inc="dbg")

    def rhs_x(b, ch):
        return bass.AP(xraw[b].handle if hasattr(xraw[b], "handle") else xraw[b],
                       ch, [[8192, 128], [128, 64]])

    nblk = 16
    for blk in range(nblk):
        buf = blk % 2
        p.wait("sync", "krep", 128 * 16)
        if blk >= 2:
            p.wait("sync", "te1", blk - 1)
        p.dma("sync", lambda e, blk=blk, buf=buf: e.dma_start(
            T8[buf][:], bass.AP(krep_d, blk * 8 * 32768 + 127,
                                [[255, 128], [32768, 8], [1, 128]])), inc=f"tw{buf}")
        p.wait("sync", "guv", 32)
        p.dma("sync", lambda e, blk=blk, buf=buf: e.dma_start(
            W8[buf][:], bass.AP(urev_d, blk * 8 * 8192,
                                [[64, 128], [8192, 8], [1, 64]])), inc=f"tw{buf}")
        p.wait("tensor", f"tw{buf}", 32 * (blk // 2 + 1))
        for b in range(2):
            p.wait("tensor", f"ldx{b}", 8 * 16)
            if blk >= 1:
                p.wait("tensor", "ev1", 4 * blk)
            for i in range(8):
                ch = blk * 8 + i
                p.op("tensor", lambda e, b=b, i=i, buf=buf, ch=ch: e.matmul(
                    pY[b][:, i * 64:(i + 1) * 64], T8[buf][:, i * 128:(i + 1) * 128],
                    rhs_x(b, ch), start=True, stop=True))
                p.op("tensor", lambda e, b=b, i=i, buf=buf, ch=ch: e.matmul(
                    pP[b][0:64, i * 64:(i + 1) * 64], W8[buf][:, i * 64:(i + 1) * 64],
                    rhs_x(b, ch), start=True, stop=True),
                    inc=("te1" if (b == 1 and i == 7) else None))
        p.wait("vector", "te1", blk + 1)
        for b in range(2):
            p.op("vector", lambda e, b=b, blk=blk: e.tensor_copy(
                y_sb[b][:, blk * 512:(blk + 1) * 512], pY[b][:]), inc="ev1")
            p.op("vector", lambda e, b=b, blk=blk: e.tensor_copy(
                P_all[b][0:64, blk * 512:(blk + 1) * 512], pP[b][0:64, :]), inc="ev1")

    for b in range(2):
        p.wait("vector", "lamqt", 1)
        p.op("vector", lambda e: e.memset(S_shift[:], 0.0))
        for d in range(128):
            p.op("vector", lambda e, b=b, d=d: e.tensor_tensor_scan(
                S_shift[0:64, d * 65 + 1:d * 65 + 65],
                lamQT[:, d:d + 1].broadcast_to((64, 64)),
                P_all[b][0:64, d * 64:(d + 1) * 64],
                0.0, ALU.mult, ALU.add))
        p.op("vector", lambda e: e.tensor_copy(S_bf[:], S_shift[:]), inc=f"scan{b}")
        for blk in range(nblk):
            buf = blk % 2
            if blk == 0:
                p.wait("scalar", "guv", 32)
            if blk >= 2:
                p.wait("scalar", f"teM{b}", blk - 1)
            p.dma("scalar", lambda e, blk=blk, buf=buf: e.dma_start(
                M8[buf][:], bass.AP(v_d, blk * 8 * 8256 + 1,
                                    [[129, 64], [8256, 8], [1, 128]])), inc=f"md{b}{buf}")
            p.wait("tensor", f"md{b}{buf}", 16 * (blk // 2 + 1))
            p.wait("tensor", f"scan{b}", 1)
            if blk >= 1:
                p.wait("tensor", f"evM{b}", blk)
            for i in range(8):
                ch = blk * 8 + i
                p.op("tensor", lambda e, b=b, i=i, buf=buf, ch=ch: e.matmul(
                    pY[b][:, i * 64:(i + 1) * 64], M8[buf][:, i * 128:(i + 1) * 128],
                    S_bf[0:64, ch * 65:ch * 65 + 64], start=True, stop=True),
                    inc=(f"teM{b}" if i == 7 else None))
            p.wait("vector", f"teM{b}", blk + 1)
            p.op("vector", lambda e, b=b, blk=blk: e.tensor_add(
                y_sb[b][:, blk * 512:(blk + 1) * 512],
                y_sb[b][:, blk * 512:(blk + 1) * 512], pY[b][:]), inc=f"evM{b}")
        out_r = xraw[b][:].rearrange("p (c d) -> p c d", c=64)
        in_r = bass.AP(y_sb[b].handle if hasattr(y_sb[b], "handle") else y_sb[b],
                       0, [[8192, 128], [1, 64], [64, 128]])
        p.op("vector", lambda e, o=out_r, i=in_r: e.tensor_copy(o, i), inc=f"rs{b}")
        p.wait("sync", f"rs{b}", 1)
        dst = bass.AP(y_d[b], 0, [[128, 128], [128 * 128, 64], [1, 128]])
        p.dma("sync", lambda e, b=b, dst=dst: e.dma_start(
            dst, xraw[b][:].rearrange("p (c d) -> p c d", c=64)), inc="sty")

    if dbg:
        p.wait("sync", "chain", chain_done)
        p.dma("sync", lambda e: e.dma_start(dbgK_d.ap(), K_sb[:]), inc="dbg")
        p.dma("sync", lambda e: e.dma_start(dbgU_d.ap(), urev_sb[:]), inc="dbg")
        p.dma("sync", lambda e: e.dma_start(dbgV_d.ap(), v_sb[:]), inc="dbg")
        p.wait("sync", "lamqt", 1)
        p.dma("sync", lambda e: e.dma_start(dbgQ_d.ap(), lamQT[:]), inc="dbg")
        p.wait("sync", "scan0", 1)
        p.dma("sync", lambda e: e.dma_start(dbgP_d.ap(), P_all[0][:]), inc="dbg")
        p.dma("sync", lambda e: e.dma_start(dbgS_d.ap(), S_bf[:]), inc="dbg")
        p.wait("sync", "evM0", 16)
        p.dma("sync", lambda e: e.dma_start(dbgY_d.ap(), y_sb[0][:]), inc="dbg")
    p.finish_and_emit()
    return nc


def build_l3a():
    nc = bass.Bass("TRN2", target_bir_lowering=False, debug=False)
    p = Prog(nc)
    acc_d = nc.dram_tensor("acc_fm", [D, T], F32, kind="ExternalInput")
    ys_d = nc.dram_tensor("yssm_fm", [D, T], BF16, kind="ExternalInput")
    wT_d = nc.dram_tensor("in_projT", [D, 3 * D], F32, kind="ExternalInput")
    ib_d = nc.dram_tensor("in_proj_b", [3 * D, 1], F32, kind="ExternalInput")
    lw_d = nc.dram_tensor("ln_w", [D, 1], F32, kind="ExternalInput")
    lb_d = nc.dram_tensor("ln_b", [D, 1], F32, kind="ExternalInput")
    x1_d = nc.dram_tensor("x1_fm", [D, T], F32, kind="ExternalOutput")
    xn1_d = nc.dram_tensor("xn1_fm", [D, T], BF16, kind="ExternalOutput")
    xn2_d = nc.dram_tensor("xn2_fm", [D, T], BF16, kind="ExternalOutput")
    q_d = nc.dram_tensor("q_fm", [D, T], BF16, kind="ExternalOutput")
    k_d = nc.dram_tensor("k_fm", [D, T], BF16, kind="ExternalOutput")
    v_d = nc.dram_tensor("v_tok", [T, D], BF16, kind="ExternalOutput")

    x1_sb = p.sb("x1_sb", [128, NCH * T], F32)
    ys_st = [p.sb(f"ys_st{i}", [128, T], BF16) for i in range(4)]
    xn1_sb = p.sb("xn1_sb", [128, NCH * T], BF16)
    xn2_sb = p.sb("xn2_sb", [128, NCH * T], BF16)
    x1_ch = [x1_sb[:, c * T:(c + 1) * T] for c in range(NCH)]
    xn1_ch = [xn1_sb[:, c * T:(c + 1) * T] for c in range(NCH)]
    xn2_ch = [xn2_sb[:, c * T:(c + 1) * T] for c in range(NCH)]
    lw_sb = p.sb("lw_sb", [128, NCH], F32)
    lb_sb = p.sb("lb_sb", [128, NCH], F32)
    qb_sb = p.sb("qb_sb", [128, 16], F32)
    wqk = [p.sb(f"wqk{i}", [128, NCH * 128], BF16) for i in range(2)]
    wv_sb = p.sb("wv_sb", [128, NCH * D], BF16)
    ev = [p.sb(f"ev{i}", [128, 1024], BF16) for i in range(2)]
    pbig1 = p.ps("pbig1", [128, T], F32)
    pbig2 = p.ps("pbig2", [128, T], F32)

    for c in range(NCH):
        p.dma("sync", lambda e, c=c: e.dma_start(x1_ch[c], acc_d.ap()[c * 128:(c + 1) * 128, :]), inc="ldx1")
    for c in range(NCH):
        if c >= 4:
            p.wait("sync", "addx", c - 3)
        p.dma("sync", lambda e, c=c: e.dma_start(ys_st[c % 4][:], ys_d.ap()[c * 128:(c + 1) * 128, :]), inc=f"ldys{c % 4}")
    for c in range(NCH):
        p.dma("sync", lambda e, c=c: e.dma_start(lw_sb[:, c:c + 1], lw_d.ap()[c * 128:(c + 1) * 128, :]), inc="ldb")
        p.dma("sync", lambda e, c=c: e.dma_start(lb_sb[:, c:c + 1], lb_d.ap()[c * 128:(c + 1) * 128, :]), inc="ldb")
    for c in range(16):
        p.dma("sync", lambda e, c=c: e.dma_start(qb_sb[:, c:c + 1], ib_d.ap()[c * 128:(c + 1) * 128, :]), inc="ldb")
    p.dma("gpsimd", lambda e: e.dma_start(
        wv_sb[:].rearrange("p (c f) -> p c f", c=NCH),
        bass.AP(wT_d, 2 * D, [[3 * D, 128], [128 * 3 * D, NCH], [1, D]])), inc="gwv")

    p.wait("vector", "ldx1", NCH * 16)
    for c in range(NCH):
        p.wait("vector", f"ldys{c % 4}", 16 * (c // 4 + 1))
        p.op("vector", lambda e, c=c: e.tensor_add(x1_ch[c], x1_ch[c], ys_st[c % 4][:]),
             inc="addx")
    lnsc = make_ln_scratch(p)
    ln1_done = emit_ln_fm(p, "ln1", x1_ch, xn1_ch, pbig1, pbig2, lnsc,
                          x_ready=("addx", NCH))
    lnw = [lw_sb[:, c:c + 1] for c in range(NCH)]
    lnb = [lb_sb[:, c:c + 1] for c in range(NCH)]
    p.wait("vector", "ldb", 32 * 16)
    ln2_done = emit_ln_fm(p, "ln2", xn1_ch, xn2_ch, pbig1, pbig2, lnsc,
                          lnw=lnw, lnb=lnb, x_ready=("ln1_dve", ln1_done))

    ps_qk = [pbig1[:, i * 512:(i + 1) * 512] for i in range(4)]
    nw = 0
    for oc in range(16):
        buf = oc % 2
        if oc >= 2:
            p.wait("gpsimd", "teqk", (oc - 1) * 4)
        p.dma("gpsimd", lambda e, oc=oc, buf=buf: e.dma_start(
            wqk[buf][:].rearrange("p (c f) -> p c f", c=NCH),
            bass.AP(wT_d, oc * 128, [[3 * D, 128], [128 * 3 * D, NCH], [1, 128]])), inc=f"gqk{buf}")
        p.wait("tensor", f"gqk{buf}", 16 * (oc // 2 + 1))
        p.wait("tensor", "ln2_dve", ln2_done)
        if oc >= 1:
            p.wait("tensor", "evqk", (oc - 1) * 4 + 4)
        for s in range(4):
            for dc in range(NCH):
                p.op("tensor", lambda e, oc=oc, s=s, dc=dc, buf=buf: e.matmul(
                    ps_qk[s], wqk[buf][:, dc * 128:(dc + 1) * 128],
                    xn2_ch[dc][:, s * 512:(s + 1) * 512],
                    start=(dc == 0), stop=(dc == NCH - 1)),
                    inc=("teqk" if dc == NCH - 1 else None))
        p.wait("vector", "teqk", p.count("teqk"))
        p.wait("vector", "ldb", 32 * 16)
        for s in range(4):
            nev = oc * 4 + s
            if nev >= 2:
                p.wait("vector", f"stq{nev % 2}", 16 * (nev // 2))
            p.op("vector", lambda e, oc=oc, s=s, buf=buf: e.tensor_scalar_add(
                ev[buf][:, (s % 2) * 512:(s % 2 + 1) * 512],
                ps_qk[s], qb_sb[:, oc:oc + 1]), inc="evqk")
            tgt = (q_d if oc < 8 else k_d)
            row = (oc % 8) * 128
            p.wait("sync", "evqk", p.count("evqk"))
            p.dma("sync", lambda e, tgt=tgt, row=row, s=s, buf=buf: e.dma_start(
                tgt.ap()[row:row + 128, s * 512:(s + 1) * 512],
                ev[buf][:, (s % 2) * 512:(s % 2 + 1) * 512]), inc=f"stq{(oc * 4 + s) % 2}")

    for tt in range(16):
        buf = tt % 2
        p.wait("tensor", "gwv", 16)
        if tt >= 1:
            p.wait("tensor", "evv", tt)
        for dc in range(NCH):
            for h in range(2):
                p.op("tensor", lambda e, tt=tt, dc=dc, h=h: e.matmul(
                    pbig2[:, h * 512:(h + 1) * 512],
                    xn2_ch[dc][:, tt * 128:(tt + 1) * 128],
                    wv_sb[:, dc * D + h * 512: dc * D + (h + 1) * 512],
                    start=(dc == 0), stop=(dc == NCH - 1)),
                    inc=("tev" if dc == NCH - 1 and h == 1 else None))
        p.wait("vector", "tev", p.count("tev"))
        if tt >= 2:
            p.wait("vector", f"sv{buf}", 16 * (tt // 2))
        p.op("vector", lambda e, buf=buf: e.tensor_copy(ev[buf][:], pbig2[:, 0:1024]), inc="evv")
        p.wait("sync", "evv", p.count("evv"))
        p.dma("sync", lambda e, tt=tt, buf=buf: e.dma_start(
            v_d.ap()[tt * 128:(tt + 1) * 128, :], ev[buf][:]), inc=f"sv{buf}")

    p.wait("sync", "ln1_dve", ln1_done)
    for c in range(NCH):
        p.dma("sync", lambda e, c=c: e.dma_start(x1_d.ap()[c * 128:(c + 1) * 128, :], x1_ch[c]), inc="stx")
        p.dma("sync", lambda e, c=c: e.dma_start(xn1_d.ap()[c * 128:(c + 1) * 128, :], xn1_ch[c]), inc="stx")
    p.wait("sync", "ln2_dve", ln2_done)
    for c in range(NCH):
        p.dma("sync", lambda e, c=c: e.dma_start(xn2_d.ap()[c * 128:(c + 1) * 128, :], xn2_ch[c]), inc="stx")
    p.finish_and_emit()
    return nc


def emit_p4(p, io, pp):
    nc = p.nc
    x1_d = io["x1"]; q_d = io["q"]; K_d = io["K"]; V_d = io["V"]
    opT_d = io["out_projT"]; opb_d = io["out_proj_b"]; vb_d = io["v_bias"]
    y1_d = io["y1"]
    psA, psB, psC, psD, psE, psF = (pp["psA"], pp["psB"], pp["psC"],
                                     pp["psD"], pp["psE"], pp["psF"])
    pS = [psA, psB]
    pV = [[psE, psF], [psC[0:65, :], psD[0:65, :]]]

    x1_sb = p.sb("x1_sb", [128, NCH * T], F32)
    x1_ch = [x1_sb[:, c * T:(c + 1) * T] for c in range(NCH)]
    q_sb = p.sb("q_sb", [128, NCH * T], BF16)
    q_ch = [q_sb[:, c * T:(c + 1) * T] for c in range(NCH)]
    ao_sb = p.sb("ao_sb", [128, NCH * T], BF16)
    ao_ch = [ao_sb[:, c * T:(c + 1) * T] for c in range(NCH)]
    kpair = p.sb("kpair", [128, 8192], BF16)
    vsb = p.sb("vsb", [128, 64 * 130], BF16)
    pexp = [[p.sb(f"pexp{s}{i}", [128, 1024], BF16) for i in range(2)] for s in range(2)]
    pvacc = [[p.sb(f"pvacc{s}{h}", [65, 512], F32) for h in range(2)] for s in range(2)]
    opw = [p.sb(f"opw{i}", [128, NCH * 128], BF16) for i in range(2)]
    opb_sb = p.sb("opb_sb", [128, NCH], F32)
    vb_sb = p.sb("vb_sb", [128, NCH], F32)
    tmp_h = [p.sb(f"tmph{h}", [64, 512], F32) for h in range(2)]
    rd_sb = p.sb("rd_sb", [1, 512], F32)
    ones_f = p.sb("ones_f", [1, 128], F32)

    for c in range(NCH):
        p.dma("sync", lambda e, c=c: e.dma_start(x1_ch[c], x1_d.ap()[c * 128:(c + 1) * 128, :]), inc="ldx")
        p.dma("sync", lambda e, c=c: e.dma_start(q_ch[c], q_d.ap()[c * 128:(c + 1) * 128, :]), inc="ldq")
        p.dma("sync", lambda e, c=c: e.dma_start(opb_sb[:, c:c + 1], opb_d.ap()[c * 128:(c + 1) * 128, :]), inc="ldb")
        p.dma("sync", lambda e, c=c: e.dma_start(vb_sb[:, c:c + 1], vb_d.ap()[c * 128:(c + 1) * 128, :]), inc="ldb")
    p.op("vector", lambda e: e.memset(ones_f[:], 1.0), inc="ones")

    NKV = 64
    for hp in range(NCH):
        if hp >= 1:
            p.wait("sync", "pvev", 32 * hp)
        p.dma("sync", lambda e, hp=hp: e.dma_start(
            kpair[:], K_d.ap()[hp * 128:(hp + 1) * 128, :]), inc="kv_ld")
        for pc in range(4):
            for h in range(2):
                srcv = bass.AP(V_d, (pc * 16 * 128) * D + hp * 128 + h * 64,
                               [[D, 128], [128 * D, 16], [1, 64]])
                dstv = bass.AP(vsb, pc * 16 * 130 + h * 65,
                               [[8320, 128], [130, 16], [1, 64]])
                p.dma("sync", lambda e, s=srcv, d=dstv: e.dma_start(d, s), inc="kv_ld")
        p.wait("vector", "kv_ld", p.count("kv_ld"))
        p.op("vector", lambda e: e.memset(
            vsb[:].rearrange("p (kv f) -> p kv f", kv=128)[:, :, 64:65], 1.0), inc="vones")
        p.wait("tensor", "vones", p.count("vones"))
        p.wait("tensor", "kv_ld", p.count("kv_ld"))
        if hp == 0:
            p.wait("tensor", "ldq", 8 * 16)
        for sp in range(2):
        
            tqs = [slice((2 * sp + s) * 512, (2 * sp + s + 1) * 512) for s in range(2)]
            for kv in range(NKV + 1):
                for s in range(2):
                    if kv < NKV:
                        p.wait("tensor", f"aex{s}", p.count(f"aex{s}"))
                        for h in range(2):
                            bse = h * 64
                            p.op("tensor", lambda e, hp=hp, kv=kv, h=h, bse=bse, s=s, tq=tqs[s]: e.matmul(
                                pS[s][:, h * 512:(h + 1) * 512],
                                kpair[bse:bse + 64, kv * 128:(kv + 1) * 128],
                                q_ch[hp][bse:bse + 64, tq], start=True, stop=True),
                                inc=(f"tes{s}" if h == 1 else None))
                for s in range(2):
                    if kv < NKV:
                        p.wait("scalar", f"tes{s}", p.count(f"tes{s}"))
                        if kv >= 2:
                            p.wait("scalar", f"tepv{s}", p.count(f"tepv{s}"))
                        p.op("scalar", lambda e, s=s, kv=kv: e.activation(
                            pexp[s][kv % 2][:], pS[s][:], AF.Exp, scale=0.125), inc=f"aex{s}")
                for s in range(2):
                    if kv >= 1:
                        k2 = kv - 1
                        p.wait("tensor", f"aex{s}", p.count(f"aex{s}") - (1 if kv < NKV else 0))
                        if k2 % 16 == 0:
                            p.wait("tensor", f"pvev{s}x" if False else "pvev",
                                   p.count("pvev"))
                        for h in range(2):
                            p.op("tensor", lambda e, s=s, k2=k2, h=h: e.matmul(
                                pV[s][h][:], vsb[:, k2 * 130 + h * 65:k2 * 130 + (h + 1) * 65],
                                pexp[s][k2 % 2][:, h * 512:(h + 1) * 512],
                                start=(k2 % 16 == 0), stop=(k2 % 16 == 15)),
                                inc=(f"tepv{s}" if h == 1 else None))
                        if k2 % 16 == 15:
                            p.wait("vector", f"tepv{s}", p.count(f"tepv{s}"))
                            for h in range(2):
                                if k2 == 15:
                                    p.op("vector", lambda e, s=s, h=h: e.tensor_copy(
                                        pvacc[s][h][:], pV[s][h][:]), inc="pvev")
                                else:
                                    p.op("vector", lambda e, s=s, h=h: e.tensor_add(
                                        pvacc[s][h][:], pvacc[s][h][:], pV[s][h][:]), inc="pvev")
            p.wait("vector", "ones", 1)
            p.wait("vector", "ldb", 16 * 16)
            for s in range(2):
                for h in range(2):
                    p.op("vector", lambda e, s=s, h=h: e.reciprocal(
                        rd_sb[:], pvacc[s][h][64:65, :]), inc="fin")
                    p.wait("tensor", "fin", p.count("fin"))
                    p.op("tensor", lambda e, s=s: e.matmul(
                        pV[s][0][0:64, :], ones_f[:, 0:64], rd_sb[:],
                        start=True, stop=True), inc="finb")
                    p.wait("vector", "finb", p.count("finb"))
                    p.op("vector", lambda e, s=s, h=h: e.tensor_mul(
                        tmp_h[h][:], pvacc[s][h][0:64, :], pV[s][0][0:64, :]), inc="fin")
                    p.op("vector", lambda e, s=s, h=h, hp=hp, tq=tqs[s]: e.tensor_scalar_add(
                        ao_ch[hp][h * 64:h * 64 + 64, tq],
                        tmp_h[h][:], vb_sb[h * 64:h * 64 + 64, hp:hp + 1]), inc="fin")

    ao_done = p.count("fin")
    for oc in range(NCH):
        buf = oc % 2
        if oc >= 2:
            p.wait("gpsimd", "teop", oc - 1)
        p.dma("gpsimd", lambda e, oc=oc, buf=buf: e.dma_start(
            opw[buf][:].rearrange("p (c f) -> p c f", c=NCH),
            bass.AP(opT_d, oc * 128, [[D, 128], [128 * D, NCH], [1, 128]])), inc=f"gop{buf}")
        p.wait("tensor", f"gop{buf}", 16 * (oc // 2 + 1))
        p.wait("tensor", "fin", ao_done)
        if oc >= 1:
            p.wait("tensor", "evop", 4 * oc)
        for s in range(4):
            ps_s = (psA if s < 2 else psB)[:, (s % 2) * 512:(s % 2 + 1) * 512]
            for dc in range(NCH):
                p.op("tensor", lambda e, oc=oc, s=s, dc=dc, buf=buf, ps_s=ps_s: e.matmul(
                    ps_s, opw[buf][:, dc * 128:(dc + 1) * 128],
                    ao_ch[dc][:, s * 512:(s + 1) * 512],
                    start=(dc == 0), stop=(dc == NCH - 1)),
                    inc=("teop" if (dc == NCH - 1 and s == 3) else None))
        p.wait("vector", "teop", oc + 1)
        p.wait("vector", "ldx", NCH * 16)
        for s in range(4):
            ps_s = (psA if s < 2 else psB)[:, (s % 2) * 512:(s % 2 + 1) * 512]
            p.op("vector", lambda e, oc=oc, s=s, ps_s=ps_s: e.scalar_tensor_tensor(
                x1_ch[oc][:, s * 512:(s + 1) * 512], ps_s, 1.0,
                x1_ch[oc][:, s * 512:(s + 1) * 512], ALU.mult, ALU.add), inc="evop")
    p.wait("vector", "ldb", 16 * 16)
    for c in range(NCH):
        p.op("vector", lambda e, c=c: e.tensor_scalar_add(
            x1_ch[c], x1_ch[c], opb_sb[:, c:c + 1]), inc="bias")
    p.wait("sync", "bias", p.count("bias"))
    for c in range(NCH):
        p.dma("sync", lambda e, c=c: e.dma_start(y1_d.ap()[c * 128:(c + 1) * 128, :], x1_ch[c]), inc="sty")


def build_l3b1():
    nc = bass.Bass("TRN2", target_bir_lowering=False, debug=False)
    p = Prog(nc)
    io = dict(
        x1=nc.dram_tensor("x1_fm", [D, T], F32, kind="ExternalInput"),
        q=nc.dram_tensor("q_fm", [D, T], BF16, kind="ExternalInput"),
        K=nc.dram_tensor("K_full", [D, 8192], BF16, kind="ExternalInput"),
        V=nc.dram_tensor("V_full", [8192, D], BF16, kind="ExternalInput"),
        out_projT=nc.dram_tensor("out_projT", [D, D], F32, kind="ExternalInput"),
        out_proj_b=nc.dram_tensor("out_proj_b", [D, 1], F32, kind="ExternalInput"),
        v_bias=nc.dram_tensor("v_bias", [D, 1], F32, kind="ExternalInput"),
        y1=nc.dram_tensor("y1_fm", [D, T], F32, kind="ExternalOutput"),
    )
    PB1 = p.ps("PB1", [128, 2048], F32)
    PB2 = p.ps("PB2", [128, 2048], F32)
    pp = dict(psA=PB1[:, 0:1024], psB=PB1[:, 1024:2048],
              psE=PB2[0:65, 0:512], psF=PB2[0:65, 512:1024],
              psC=PB2[:, 1024:1536], psD=PB2[:, 1536:2048])
    emit_p4(p, io, pp)
    p.finish_and_emit()
    return nc


def emit_p5(p, io, pp):
    nc = p.nc
    y1_d = io["y1"]; xn1_d = io["xn1"]; xn2_d = io["xn2"]
    w0T_d = io["w0T"]; w2T_d = io["w2T"]; w1_d = io["w1l"]; w1T_d = io["w1T"]
    gwT_d = io["gate_wT"]; bal_d = io["bal_b"]
    mw1_d = io["moew1T"]; mw2_d = io["moew2T"]
    out_d = io["out"]
    mw1b_d = io["mw1b"]; mw2b_d = io["mw2b"]; wc_d = io["wc"]

    y1_sb = p.sb("y1_sb", [128, NCH * T], F32)
    y1_ch = [y1_sb[:, c * T:(c + 1) * T] for c in range(NCH)]
    xn1_sb = p.sb("xn1_sb", [128, NCH * T], BF16)
    xn1_ch = [xn1_sb[:, c * T:(c + 1) * T] for c in range(NCH)]
    xn2_sb = p.sb("xn2_sb", [128, NCH * T], BF16)
    xn2_ch = [xn2_sb[:, c * T:(c + 1) * T] for c in range(NCH)]
    lw = p.sb("lw", [128, 4 * 4 * 512], BF16)
    def lw_sl(h, m, dc2):
        return lw[:, (h * 4 + m) * 512 + dc2 * 256:(h * 4 + m) * 512 + (dc2 + 1) * 256]
    kvt = p.sb("kvt", [128, 8 * 256], BF16)
    hid_sb = p.sb("hid_sb", [128, 8 * 256], BF16)
    dpre_sb = p.sb("dpre_sb", [128, 8 * 256], BF16)
    dg_sb = None
    gt_t = g2_t = u_t = w_t = wn0 = wn2 = wn1 = None
    hq_sb = p.sb("hq_sb", [128, 2 * 1024], BF16)
    he_sb = p.sb("he_sb", [128, 2 * 1024], BF16)
    dg_sb = he_sb
    gw_sb = p.sb("gw_sb", [128, NCH * 8], BF16)
    glt = p.sb("glt", [128, 16 * 8], F32)
    wcoef = p.sb("wcoef", [128, 16 * 8], F32)
    wrowT = None
    m1_t = p.sb("m1_t", [128, 1], F32)
    m2_t = p.sb("m2_t", [128, 1], F32)
    den_t = p.sb("den_t", [128, 1], F32)
    eps6 = p.sb("eps6", [128, 1], F32)
    bb_sb = p.sb("bb_sb", [128, 8], F32)
    ones_f = p.sb("ones_f", [1, 128], F32)
    identb = p.sb("identb", [128, 128], BF16)
    wrowT = None
    mw2_sb = [p.sb(f"mw2_sb{i}", [128, 2048], BF16) for i in range(2)]
    hst = p.sb("hst", [128, 16 * 1024], BF16)
    wrowT = hst[:].bitcast(F32)[0:8, 0:2048]
    sg_p = [hst[:, 8192:8448], hst[:, 11264:11520]]
    dh0_p = [hst[:, 8960:9216], hst[:, 11520:11776]]
    gt_t = hst[:, 8448:8704]; g2_t = hst[:, 8704:8960]
    u_t = hst[:, 9216:9472]; w_t = hst[:, 9472:9728]
    wn0 = hst[:, 9728:10240]; wn2 = hst[:, 10240:10752]; wn1 = hst[:, 10752:11264]
    wbc = None

    wbc = kvt[:].bitcast(F32)[:, 0:512]
    wbc2 = kvt[:].bitcast(F32)[:, 512:1024]
    psA, psB, psC, psD, psT = pp["psA"], pp["psB"], pp["psC"], pp["psD"], pp["psT"]

    for c in range(NCH):
        p.dma("sync", lambda e, c=c: e.dma_start(y1_ch[c], y1_d.ap()[c * 128:(c + 1) * 128, :]), inc="ldy")
        p.dma("sync", lambda e, c=c: e.dma_start(xn1_ch[c], xn1_d.ap()[c * 128:(c + 1) * 128, :]), inc="ldn1")
        p.dma("sync", lambda e, c=c: e.dma_start(xn2_ch[c], xn2_d.ap()[c * 128:(c + 1) * 128, :]), inc="ldn2")
    for h in range(4):
        for m, wd in ((0, w0T_d), (1, w2T_d), (2, w1_d), (3, w1T_d)):
            src = bass.AP(wd, h * 256 * 256, [[256, 128], [128 * 256, 2], [1, 256]])
            dst = bass.AP(lw, (h * 4 + m) * 512, [[8192, 128], [256, 2], [1, 256]])
            p.dma("gpsimd", lambda e, s=src, d=dst: e.dma_start(d, s), inc="glw")
    p.dma("gpsimd", lambda e: e.dma_start(
        gw_sb[:].rearrange("p (c f) -> p c f", c=NCH),
        bass.AP(gwT_d, 0, [[8, 128], [128 * 8, NCH], [1, 8]])), inc="glw")
    p.dma("sync", lambda e: e.dma_start(identb[:], io["ident_b"].ap()), inc="ldi")
    p.dma("sync", lambda e: e.dma_start(bb_sb[0:1, :], bal_d.ap()), inc="ldi")
    for ee in range(8):
        p.dma("gpsimd", lambda e, ee=ee: e.dma_start(
            mw1b_d.ap()[ee * D:(ee + 1) * D, :], mw1_d.ap()[ee * D:(ee + 1) * D, :]), inc="gcw1")
        p.dma("gpsimd", lambda e, ee=ee: e.dma_start(
            mw2b_d.ap()[ee * 2048:(ee + 1) * 2048, :], mw2_d.ap()[ee * 2048:(ee + 1) * 2048, :]), inc="gcw2")

    p.op("vector", lambda e: e.memset(ones_f[:], 1.0), inc="ones")
    p.op("vector", lambda e: e.memset(eps6[:], 1e-6), inc="ones")
    p.wait("tensor", "glw", 17 * 16)
    p.wait("tensor", "ldn1", 8 * 16)
    for s in range(4):
        pr = (psA if s < 2 else psB)[0:8, (s % 2) * 512:(s % 2 + 1) * 512]
        for dc in range(NCH):
            p.op("tensor", lambda e, s=s, dc=dc, pr=pr: e.matmul(
                pr, gw_sb[:, dc * 8:(dc + 1) * 8],
                xn1_ch[dc][:, s * 512:(s + 1) * 512],
                start=(dc == 0), stop=(dc == NCH - 1)),
                inc=("ter" if dc == NCH - 1 else None))
    p.wait("tensor", "ldi", 32)
    p.op("tensor", lambda e: e.matmul(psC[:, 0:8], ones_f[:], bb_sb[0:1, :],
                                      start=True, stop=True), inc="ter")
    gl8 = hq_sb[0:8, 0:T]
    p.wait("vector", "ter", p.count("ter"))
    for s in range(4):
        pr = (psA if s < 2 else psB)[0:8, (s % 2) * 512:(s % 2 + 1) * 512]
        p.op("vector", lambda e, s=s, pr=pr: e.tensor_copy(gl8[:, s * 512:(s + 1) * 512], pr), inc="evr")
    p.op("vector", lambda e: e.tensor_copy(bb_sb[:], psC[:, 0:8]), inc="evr")
    p.wait("tensor", "evr", p.count("evr"))
    for tt in range(16):
        p.op("tensor", lambda e, tt=tt: e.transpose(
            psT[:, 0:8], gl8[0:8, tt * 128:(tt + 1) * 128], identb[0:8, 0:8]), inc="tet")
        p.wait("vector", "tet", p.count("tet"))
        p.op("vector", lambda e, tt=tt: e.tensor_copy(glt[:, tt * 8:(tt + 1) * 8], psT[0:128, 0:8]), inc="evt")
        p.wait("tensor", "evt", p.count("evt"))
    p.wait("scalar", "evt", p.count("evt"))
    for tt in range(16):
        sl = slice(tt * 8, (tt + 1) * 8)
        p.op("vector", lambda e, sl=sl: e.tensor_add(glt[:, sl], glt[:, sl], bb_sb[:]), inc="rt")
        p.wait("scalar", "rt", p.count("rt"))
        p.op("scalar", lambda e, sl=sl: e.activation(glt[:, sl], glt[:, sl], AF.Sigmoid), inc="rta")
        p.wait("vector", "rta", p.count("rta"))
        p.op("vector", lambda e, sl=sl: e.reduce_max(m1_t[:], glt[:, sl], AX.X), inc="rt")
        p.op("vector", lambda e, sl=sl: e.tensor_scalar(wcoef[:, sl], glt[:, sl], m1_t[:, 0:1], -1e30,
                                                        ALU.is_ge, ALU.mult), inc="rt")
        p.op("vector", lambda e, sl=sl: e.tensor_add(wcoef[:, sl], wcoef[:, sl], glt[:, sl]), inc="rt")
        p.op("vector", lambda e, sl=sl: e.reduce_max(m2_t[:], wcoef[:, sl], AX.X), inc="rt")
        p.op("vector", lambda e: e.tensor_add(den_t[:], m1_t[:], m2_t[:]), inc="rt")
        p.op("vector", lambda e: e.tensor_add(den_t[:], den_t[:], eps6[:]), inc="rt")
        p.op("vector", lambda e: e.reciprocal(den_t[:], den_t[:]), inc="rt")
        p.op("vector", lambda e, sl=sl: e.tensor_scalar(wcoef[:, sl], glt[:, sl], m2_t[:, 0:1], None,
                                                        ALU.is_ge), inc="rt")
        p.op("vector", lambda e, sl=sl: e.tensor_mul(wcoef[:, sl], wcoef[:, sl], glt[:, sl]), inc="rt")
        p.op("vector", lambda e, sl=sl: e.tensor_scalar_mul(wcoef[:, sl], wcoef[:, sl], den_t[:, 0:1]), inc="rt")
    p.wait("vector", "rt", p.count("rt"))
    for tt in range(16):
        p.op("vector", lambda e, tt=tt: e.tensor_copy(glt[:, tt * 8:(tt + 1) * 8], wcoef[:, tt * 8:(tt + 1) * 8]), inc="wcb")
    wct = p.sb("wct", [128, 16 * 8], BF16)
    p.op("vector", lambda e: e.tensor_copy(wct[:], wcoef[:]), inc="wcb")
    p.wait("tensor", "wcb", p.count("wcb"))
    for tt in range(16):
        p.op("tensor", lambda e, tt=tt: e.transpose(
            psT[0:8, :], wct[:, tt * 8:(tt + 1) * 8], identb[:]), inc="tet2")
        p.wait("vector", "tet2", p.count("tet2"))
        p.op("vector", lambda e, tt=tt: e.tensor_copy(wrowT[0:8, tt * 128:(tt + 1) * 128], psT[0:8, 0:128]), inc="evt2")
        p.wait("tensor", "evt2", p.count("evt2"))
    p.wait("sync", "evt2", p.count("evt2"))
    p.dma("sync", lambda e: e.dma_start(wc_d.ap(), wrowT[0:8, :]), inc="stwc")

    p.wait("tensor", "ldn2", 8 * 16)
    for unit in range(8):
        n = unit // 4
        h = unit % 4
        cch = [2 * h, 2 * h + 1]
        ncol = n * 1024
        for ct in range(8):
            for d2 in range(2):
                p.op("tensor", lambda e, ct=ct, d2=d2, cch=cch, ncol=ncol: e.transpose(
                    psT[:], xn2_ch[cch[d2]][:, ncol + ct * 128:ncol + (ct + 1) * 128], identb[:]),
                    inc="lt1")
                p.wait("vector", "lt1", p.count("lt1"))
                p.op("vector", lambda e, ct=ct, d2=d2: e.tensor_copy(
                    kvt[:, ct * 256 + d2 * 128:ct * 256 + (d2 + 1) * 128], psT[:]), inc="lev1")
                p.wait("tensor", "lev1", p.count("lev1"))
        for ct in range(8):
            par = ct % 2
            if par == 0:
                pG = psA[:, 0:256]; pG2 = psA[:, 512:768]; pH = psB[:, 0:256]
            else:
                pG = psB[:, 512:768]; pG2 = psC[:, 0:256]; pH = psD[:, 0:256]
            sg_t = sg_p[par]; dh0_t = dh0_p[par]
            if ct == 1:
                p.wait("tensor", "le5n", p.count("le5n"))
            if ct >= 2:
                p.wait("tensor", "lely", p.count("lely") - 1)
            for d2 in range(2):
                for m, pp in ((0, pG), (1, pG2), (2, pH)):
                    p.op("tensor", lambda e, m=m, pp=pp, d2=d2, ct=ct, cch=cch, ncol=ncol, h=h: e.matmul(
                        pp, xn2_ch[cch[d2]][:, ncol + ct * 128:ncol + (ct + 1) * 128],
                        lw_sl(h, m, d2), start=(d2 == 0), stop=(d2 == 1)),
                        inc=("lt2" if (d2 == 1 and m == 2) else None))
            p.wait("scalar", "lt2", p.count("lt2"))
            p.op("scalar", lambda e, pG=pG, sg_t=sg_t: e.activation(sg_t[:], pG, AF.Sigmoid), inc="la")
            p.op("scalar", lambda e, pH=pH, dh0_t=dh0_t: e.activation(dh0_t[:], pH, AF.Copy, scale=-1.0), inc="la")
            p.wait("vector", "la", p.count("la"))
            p.op("vector", lambda e, pG=pG: e.tensor_copy(gt_t[:], pG), inc="le")
            p.op("vector", lambda e, pG2=pG2: e.tensor_copy(g2_t[:], pG2), inc="le")
            sl2 = slice(ct * 256, (ct + 1) * 256)
            p.op("vector", lambda e, sg_t=sg_t: e.tensor_mul(u_t[:], gt_t[:], sg_t[:]), inc="le")
            p.op("vector", lambda e, sl2=sl2: e.tensor_mul(hid_sb[:, sl2], u_t[:], g2_t[:]), inc="le")
            p.op("vector", lambda e, sl2=sl2, dh0_t=dh0_t: e.tensor_mul(dg_sb[:, sl2], dh0_t[:], u_t[:]), inc="le")
            p.op("vector", lambda e, sg_t=sg_t: e.tensor_add(w_t[:], sg_t[:], u_t[:]), inc="le")
            p.op("vector", lambda e, sg_t=sg_t: e.tensor_mul(gt_t[:], u_t[:], sg_t[:]), inc="le")
            p.op("vector", lambda e: e.tensor_sub(w_t[:], w_t[:], gt_t[:]), inc="le")
            p.op("vector", lambda e, dh0_t=dh0_t: e.tensor_mul(gt_t[:], dh0_t[:], g2_t[:]), inc="le")
            p.op("vector", lambda e, sl2=sl2: e.tensor_mul(dpre_sb[:, sl2], gt_t[:], w_t[:]), inc="le",
                 n=1)
            p.op("vector", lambda e: e.engine_nop(), inc="lely")
        p.wait("tensor", "le", p.count("le"))
        p.wait("tensor", "le5n", p.count("le5n"))
        for d2 in range(2):
            for ct in range(8):
                p.op("tensor", lambda e, ct=ct, d2=d2: e.matmul(
                    psC[:, 0:256] if False else psA[:, d2 * 256:(d2 + 1) * 256],
                    kvt[:, ct * 256 + d2 * 128:ct * 256 + (d2 + 1) * 128],
                    dpre_sb[:, ct * 256:(ct + 1) * 256],
                    start=(ct == 0), stop=(ct == 7)), inc=("lt3" if ct == 7 else None))
                p.op("tensor", lambda e, ct=ct, d2=d2: e.matmul(
                    psB[:, d2 * 256:(d2 + 1) * 256],
                    kvt[:, ct * 256 + d2 * 128:ct * 256 + (d2 + 1) * 128],
                    dg_sb[:, ct * 256:(ct + 1) * 256],
                    start=(ct == 0), stop=(ct == 7)), inc=("lt3" if ct == 7 else None))
                p.op("tensor", lambda e, ct=ct, d2=d2: e.matmul(
                    psC[:, d2 * 256:(d2 + 1) * 256] if True else None,
                    hid_sb[:].rearrange("p (ct f) -> p ct f", ct=8)[:, ct, d2 * 128:(d2 + 1) * 128],
                    kvt[:, ct * 256:(ct + 1) * 256],
                    start=(ct == 0), stop=(ct == 7)), inc=("lt3" if ct == 7 else None))
        p.wait("vector", "lt3", p.count("lt3"))
        p.op("vector", lambda e, h=h: e.tensor_sub(wn0[:], lw[:, (h * 4 + 0) * 512:(h * 4 + 1) * 512], psA[:, 0:512]), inc="lwn")
        p.op("vector", lambda e, h=h: e.tensor_sub(wn2[:], lw[:, (h * 4 + 1) * 512:(h * 4 + 2) * 512], psB[:, 0:512]), inc="lwn")
        p.op("vector", lambda e, h=h: e.tensor_add(wn1[:], lw[:, (h * 4 + 3) * 512:(h * 4 + 4) * 512], psC[:, 0:512]), inc="lwn")
        p.wait("tensor", "lwn", p.count("lwn"))
        for es in range(2):
            for cs in range(2):
                for d2 in range(2):
                    p.op("tensor", lambda e, es=es, cs=cs, d2=d2, cch=cch, ncol=ncol: e.matmul(
                        psA[:, cs * 512:(cs + 1) * 512] if es == 0 else psA[:, cs * 512:(cs + 1) * 512],
                        wn0[:, es * 256 + 0:0] if False else wn0[:].rearrange("p (d2 f) -> p d2 f", d2=2)[:, d2, es * 128:(es + 1) * 128],
                        xn2_ch[cch[d2]][:, ncol + cs * 512:ncol + (cs + 1) * 512],
                        start=(d2 == 0), stop=(d2 == 1)),
                        inc=("lt4" if d2 == 1 else None))
                    p.op("tensor", lambda e, es=es, cs=cs, d2=d2, cch=cch, ncol=ncol: e.matmul(
                        psB[:, cs * 512:(cs + 1) * 512],
                        wn2[:].rearrange("p (d2 f) -> p d2 f", d2=2)[:, d2, es * 128:(es + 1) * 128],
                        xn2_ch[cch[d2]][:, ncol + cs * 512:ncol + (cs + 1) * 512],
                        start=(d2 == 0), stop=(d2 == 1)),
                        inc=("lt4" if d2 == 1 else None))
                p.wait("scalar", "lt4", p.count("lt4"))
                p.op("scalar", lambda e, es=es, cs=cs: e.activation(
                    hq_sb[:, es * 1024 + cs * 512:es * 1024 + (cs + 1) * 512],
                    psA[:, cs * 512:(cs + 1) * 512], AF.Silu), inc="la4")
                p.wait("vector", "la4", p.count("la4"))
                p.op("vector", lambda e, es=es, cs=cs: e.tensor_copy(
                    he_sb[:, es * 1024 + cs * 512:es * 1024 + (cs + 1) * 512],
                    psB[:, cs * 512:(cs + 1) * 512]), inc="le4")
                p.op("vector", lambda e, es=es, cs=cs: e.tensor_mul(
                    he_sb[:, es * 1024 + cs * 512:es * 1024 + (cs + 1) * 512],
                    he_sb[:, es * 1024 + cs * 512:es * 1024 + (cs + 1) * 512],
                    hq_sb[:, es * 1024 + cs * 512:es * 1024 + (cs + 1) * 512]), inc="le4")
                p.op("vector", lambda e: e.engine_nop(), inc="le4n")
                if es * 2 + cs >= 1:
                    p.wait("tensor", "le4n", p.count("le4n") - 1)
        p.wait("tensor", "le4n", p.count("le4n"))
        for ds in range(2):
            for cs in range(2):
                par5 = (ds * 2 + cs) % 2
                pOQ = psC[:, 0:512] if par5 == 0 else psD[:, 0:512]
                if ds * 2 + cs >= 2:
                    p.wait("tensor", "le5n", p.count("le5n") - 1)
                for es in range(2):
                    p.op("tensor", lambda e, ds=ds, cs=cs, es=es, pOQ=pOQ: e.matmul(
                        pOQ,
                        wn1[:].rearrange("p (es f) -> p es f", es=2)[:, es, ds * 128:(ds + 1) * 128],
                        he_sb[:, es * 1024 + cs * 512:es * 1024 + (cs + 1) * 512],
                        start=(es == 0), stop=(es == 1)),
                        inc=("lt5" if es == 1 else None))
                p.wait("vector", "lt5", p.count("lt5"))
                p.wait("vector", "ldy", 8 * 16)
                p.op("vector", lambda e, ds=ds, cs=cs, cch=cch, ncol=ncol, pOQ=pOQ: e.scalar_tensor_tensor(
                    y1_ch[cch[ds]][:, ncol + cs * 512:ncol + (cs + 1) * 512], pOQ, 1.0,
                    y1_ch[cch[ds]][:, ncol + cs * 512:ncol + (cs + 1) * 512], ALU.mult, ALU.add), inc="le5")
                p.op("vector", lambda e: e.engine_nop(), inc="le5n")

    mh = [he_sb[:, 0:1024], he_sb[:, 1024:2048]]
    p.wait("sync", "stwc", 16)
    p.wait("sync", "le5n", 32)
    p.wait("vector", "le5n", 32)
    for sp in range(2):
        for ee in range(8):
            ie = sp * 8 + ee
            for hc in range(16):
                it = ie * 16 + hc
                buf = it % 2
                dq = "sync"
                if sp == 0 and hc <= 1:
                    p.wait(dq, "gcw1", 16 * (ee + 1))
                if it >= 2:
                    p.wait(dq, "temh", it - 1)
                p.dma(dq, lambda e, ee=ee, hc=hc, buf=buf: e.dma_start(
                    mh[buf][:].rearrange("p (c f) -> p c f", c=NCH),
                    bass.AP(mw1b_d, ee * D * 2048 + hc * 128,
                            [[2048, 128], [128 * 2048, NCH], [1, 128]])), inc=f"mhd{buf}")
                p.wait("tensor", f"mhd{buf}", 16 * (it // 2 + 1))
                ph2 = [psC[:], psD[:]] if it % 2 == 0 else [psA[:, 0:512], psA[:, 512:1024]]
                if it >= 2:
                    p.wait("tensor", "evh", 2 * (it - 1))
                if it % 2 == 1:
                    p.wait("tensor", "evd", 2 * 8 * ie)
                for dc in range(NCH):
                    for t2 in range(2):
                        p.op("tensor", lambda e, sp=sp, dc=dc, t2=t2, buf=buf, ph2=ph2: e.matmul(
                            ph2[t2],
                            mh[buf][:, dc * 128:(dc + 1) * 128],
                            xn1_ch[dc][:, (sp * 2 + t2) * 512:(sp * 2 + t2 + 1) * 512],
                            start=(dc == 0), stop=(dc == NCH - 1)),
                            inc=("temh" if (dc == NCH - 1 and t2 == 1) else None))
                p.wait("scalar", "temh", p.count("temh"))
                p.wait("scalar", "temd", 8 * ie)
                if it == 0:
                    p.wait("scalar", "stwc", 16)
                for t2 in range(2):
                    p.op("scalar", lambda e, hc=hc, t2=t2, ph2=ph2: e.activation(
                        hst[:, hc * 1024 + t2 * 512:hc * 1024 + (t2 + 1) * 512],
                        ph2[t2], AF.Silu), inc="evh")
            p.dma("scalar", lambda e, ee=ee, sp=sp: e.dma_start(
                wbc, bass.AP(wc_d, ee * T + sp * 1024, [[0, 128], [1, 512]])), inc="wbcd")
            p.dma("scalar", lambda e, ee=ee, sp=sp: e.dma_start(
                wbc2, bass.AP(wc_d, ee * T + sp * 1024 + 512, [[0, 128], [1, 512]])), inc="wbcd")
            for dc in range(NCH):
                it2 = ie * 8 + dc
                buf = it2 % 2
                dq2 = "scalar"
                if sp == 0 and dc <= 1:
                    p.wait(dq2, "gcw2", 16 * (ee + 1))
                if it2 >= 2:
                    p.wait(dq2, "temd", it2 - 1)
                p.dma(dq2, lambda e, ee=ee, dc=dc, buf=buf: e.dma_start(
                    mw2_sb[buf][:].rearrange("p (c f) -> p c f", c=16),
                    bass.AP(mw2b_d, ee * 2048 * D + dc * 128,
                            [[D, 128], [128 * D, 16], [1, 128]])), inc=f"md{buf}")
                p.wait("tensor", f"md{buf}", 16 * (it2 // 2 + 1))
                p.wait("tensor", "evh", p.count("evh"))
                if it2 >= 2:
                    p.wait("tensor", "evd", 2 * (it2 - 1))
                pd2 = psA if dc % 2 == 0 else psB
                for hc in range(16):
                    for t2 in range(2):
                        p.op("tensor", lambda e, dc=dc, hc=hc, t2=t2, buf=buf, pd2=pd2: e.matmul(
                            pd2[:, t2 * 512:(t2 + 1) * 512],
                            mw2_sb[buf][:, hc * 128:(hc + 1) * 128],
                            hst[:, hc * 1024 + t2 * 512:hc * 1024 + (t2 + 1) * 512],
                            start=(hc == 0), stop=(hc == 15)),
                            inc=("temd" if (hc == 15 and t2 == 1) else None))
                p.wait("vector", "temd", p.count("temd"))
                p.wait("vector", "wbcd", p.count("wbcd"))
                for t2 in range(2):
                    p.op("vector", lambda e, t2=t2, pd2=pd2: e.tensor_mul(
                        hq_sb[:, 0:512] if t2 == 0 else hq_sb[:, 512:1024],
                        pd2[:, t2 * 512:(t2 + 1) * 512],
                        wbc if t2 == 0 else wbc2), inc="evd0")
                p.wait("gpsimd", "evd0", p.count("evd0"))
                for t2 in range(2):
                    p.op("gpsimd", lambda e, sp=sp, dc=dc, t2=t2: e.tensor_add(
                        y1_ch[dc][:, (sp * 2 + t2) * 512:(sp * 2 + t2 + 1) * 512],
                        y1_ch[dc][:, (sp * 2 + t2) * 512:(sp * 2 + t2 + 1) * 512],
                        hq_sb[:, 0:512] if t2 == 0 else hq_sb[:, 512:1024]), inc="evd")
    p.wait("sync", "evd", p.count("evd"))
    for c in range(NCH):
        p.dma("sync", lambda e, c=c: e.dma_start(out_d.ap()[c * 128:(c + 1) * 128, :], y1_ch[c]), inc="sto")




def build_l3b2():
    nc = bass.Bass("TRN2", target_bir_lowering=False, debug=False)
    p = Prog(nc)
    io = dict(
        y1=nc.dram_tensor("y1_fm", [D, T], F32, kind="ExternalInput"),
        xn1=nc.dram_tensor("xn1_fm", [D, T], BF16, kind="ExternalInput"),
        xn2=nc.dram_tensor("xn2_fm", [D, T], BF16, kind="ExternalInput"),
        w0T=nc.dram_tensor("w0T", [4 * 256, 256], F32, kind="ExternalInput"),
        w2T=nc.dram_tensor("w2T", [4 * 256, 256], F32, kind="ExternalInput"),
        w1l=nc.dram_tensor("w1l", [4 * 256, 256], F32, kind="ExternalInput"),
        w1T=nc.dram_tensor("w1T", [4 * 256, 256], F32, kind="ExternalInput"),
        gate_wT=nc.dram_tensor("gate_wT", [D, 8], F32, kind="ExternalInput"),
        bal_b=nc.dram_tensor("bal_b", [1, 8], F32, kind="ExternalInput"),
        moew1T=nc.dram_tensor("moew1T", [8 * D, 2048], F32, kind="ExternalInput"),
        moew2T=nc.dram_tensor("moew2T", [8 * 2048, D], F32, kind="ExternalInput"),
        ident_b=nc.dram_tensor("ident_b", [128, 128], BF16, kind="ExternalInput"),
        out=nc.dram_tensor("out_fm", [D, T], F32, kind="ExternalOutput"),
        mw1b=nc.dram_tensor("mw1b", [8 * D, 2048], BF16, kind="Internal"),
        mw2b=nc.dram_tensor("mw2b", [8 * 2048, D], BF16, kind="Internal"),
        wc=nc.dram_tensor("wc_dram", [8, T], F32, kind="Internal"),
    )
    PB1 = p.ps("PB1", [128, 2048], F32)
    PB2 = p.ps("PB2", [128, 2048], F32)
    pp = dict(psA=PB1[:, 0:1024], psB=PB1[:, 1024:2048],
              psC=PB2[:, 0:512], psD=PB2[:, 512:1024],
              psT=PB2[:, 1024:1088].bitcast(BF16))
    emit_p5(p, io, pp)
    p.finish_and_emit()
    return nc


T8 = 8192


def emit_p1(p, io, pp):
    nc = p.nc
    x_d, wgT_d, wuT_d, wdT_d = io["x"], io["wgT"], io["wuT"], io["wdT"]
    xn_d, acc_d = io["xn"], io["acc"]
    pbig1, pbig2 = pp["PB1"], pp["PB2"]

    x_sb = p.sb("x_sb", [128, NCH * T], F32)
    xn_sb = p.sb("xn_sb", [128, NCH * T], BF16)
    x_ch = [x_sb[:, c * T:(c + 1) * T] for c in range(NCH)]
    xn_ch = [xn_sb[:, c * T:(c + 1) * T] for c in range(NCH)]
    lnsc = make_ln_scratch(p)
    wg_sb = [p.sb(f"wg{i}", [128, NCH * 128], BF16) for i in range(2)]
    wu_sb = [p.sb(f"wu{i}", [128, NCH * 128], BF16) for i in range(2)]
    wd_sb = [p.sb(f"wd{i}", [128, NH0 * 128], BF16) for i in range(2)]
    g_sb = p.sb("g_sb", [128, 1024], BF16)
    hTall = p.sb("hTall", [128, NH0 * 1024], BF16)
    ps4a = [pbig1[:, i * 512:(i + 1) * 512] for i in range(4)]
    ps4b = [pbig2[:, i * 512:(i + 1) * 512] for i in range(4)]

    for s4 in range(4):
        co = s4 * T
        ldx_base = p.count("ld_x")
        for c in range(NCH):
            if s4 >= 1:
                p.wait("sync", "st_acc", 16 * (8 * (s4 - 1) + c + 1))
            p.dma("sync", lambda e, c=c, co=co: e.dma_start(
                x_ch[c], x_d.ap()[c * 128:(c + 1) * 128, co:co + T]), inc="ld_x")
        if s4 >= 1:
            p.wait("tensor", "dve_acc", 32 * s4)
        ln_done = emit_ln_fm(p, f"l0{s4}", x_ch, xn_ch, pbig1, pbig2, lnsc,
                             x_chunks=("ld_x", ldx_base),
                             bc_sb=hTall[:].bitcast(F32)[:, 0:2 * T])
        p.wait("sync", f"l0{s4}_dve", ln_done)
        for c in range(NCH):
            p.dma("sync", lambda e, c=c, co=co: e.dma_start(
                xn_d.ap()[c * 128:(c + 1) * 128, co:co + T], xn_ch[c]), inc="st_xn")
        for hv in range(2):
          ho = hv * 1024
          for hc in range(NH0):
            buf = hc % 2
            it = (s4 * 2 + hv) * NH0 + hc
            src_g = bass.AP(wgT_d, hc * 128, [[HID0, 128], [128 * HID0, 8], [1, 128]])
            src_u = bass.AP(wuT_d, hc * 128, [[HID0, 128], [128 * HID0, 8], [1, 128]])
            if it >= 2:
                p.wait("gpsimd", "te_gu", it - 1)
            p.dma("gpsimd", lambda e, s=src_g, buf=buf: e.dma_start(
                wg_sb[buf][:].rearrange("p (c f) -> p c f", c=8), s), inc=f"wgu{buf}")
            p.dma("gpsimd", lambda e, s=src_u, buf=buf: e.dma_start(
                wu_sb[buf][:].rearrange("p (c f) -> p c f", c=8), s), inc=f"wgu{buf}")
            p.wait("tensor", f"wgu{buf}", 32 * (it // 2 + 1))
            pg = pbig1[:, (it % 2) * 1024:(it % 2 + 1) * 1024]
            pu = pbig2[:, (it % 2) * 1024:(it % 2 + 1) * 1024]
            if it >= 2:
                p.wait("tensor", "evac_gu", 3 * (it - 1))
            if hc <= 1 and it >= 2:
                p.wait("tensor", "dve_acc", 2 * NCH * (s4 * 2 + hv))
            for dc in range(NCH):
                if hc == 0 and hv == 0:
                    p.wait("tensor", f"l0{s4}_dve", ln_done - 2 * (NCH - 1 - dc))
                for tq in range(2):
                    p.op("tensor", lambda e, buf=buf, dc=dc, tq=tq, ho=ho, pg=pg: e.matmul(
                        pg[:, tq * 512:(tq + 1) * 512], wg_sb[buf][:, dc * 128:(dc + 1) * 128],
                        xn_ch[dc][:, ho + tq * 512:ho + (tq + 1) * 512],
                        start=(dc == 0), stop=(dc == NCH - 1)))
            for dc in range(NCH):
                for tq in range(2):
                    p.op("tensor", lambda e, buf=buf, dc=dc, tq=tq, ho=ho, pu=pu: e.matmul(
                        pu[:, tq * 512:(tq + 1) * 512], wu_sb[buf][:, dc * 128:(dc + 1) * 128],
                        xn_ch[dc][:, ho + tq * 512:ho + (tq + 1) * 512],
                        start=(dc == 0), stop=(dc == NCH - 1)),
                        inc=("te_gu" if (dc == NCH - 1 and tq == 1) else None))
            p.wait("scalar", "te_gu", p.count("te_gu"))
            if hc == 0:
                p.wait("scalar", "te_d", (s4 * 2 + hv) * NCH)
            p.op("scalar", lambda e, pg=pg: e.activation(g_sb[:], pg, AF.Silu, scale=QSC),
                 inc="evac_gu")
            p.op("scalar", lambda e, hc=hc, pu=pu: e.activation(
                hTall[:, hc * 1024:(hc + 1) * 1024], pu, AF.Copy, scale=QSC),
                inc="evac_gu")
            p.wait("vector", "evac_gu", p.count("evac_gu"))
            p.op("vector", lambda e, hc=hc: e.tensor_mul(
                hTall[:, hc * 1024:(hc + 1) * 1024], g_sb[:],
                hTall[:, hc * 1024:(hc + 1) * 1024]), inc="evac_gu")
          for dc in range(NCH):
            buf = dc % 2
            it = (s4 * 2 + hv) * NCH + dc
            src_d = bass.AP(wdT_d, dc * 128, [[D, 128], [128 * D, NH0], [1, 128]])
            if it >= 2:
                p.wait("gpsimd", "te_d", it - 1)
            p.dma("gpsimd", lambda e, s=src_d, buf=buf: e.dma_start(
                wd_sb[buf][:].rearrange("p (c f) -> p c f", c=NH0), s), inc=f"wdd{buf}")
            p.wait("tensor", f"wdd{buf}", 16 * (it // 2 + 1))
            pd2 = pbig1[:, (it % 2) * 1024:(it % 2 + 1) * 1024]
            if dc == 0:
                p.wait("tensor", "evac_gu", p.count("evac_gu"))
            if it >= 2:
                p.wait("tensor", "dve_acc", 2 * (it - 1))
            for hc in range(NH0):
                for tq in range(2):
                    p.op("tensor", lambda e, buf=buf, hc=hc, tq=tq, pd2=pd2: e.matmul(
                        pd2[:, tq * 512:(tq + 1) * 512], wd_sb[buf][:, hc * 128:(hc + 1) * 128],
                        hTall[:, hc * 1024 + tq * 512:hc * 1024 + (tq + 1) * 512],
                        start=(hc == 0), stop=(hc == NH0 - 1)),
                        inc=("te_d" if (hc == NH0 - 1 and tq == 1) else None))
            p.wait("vector", "te_d", p.count("te_d"))
            for tq in range(2):
                p.op("vector", lambda e, dc=dc, tq=tq, ho=ho, pd2=pd2: e.scalar_tensor_tensor(
                    x_ch[dc][:, ho + tq * 512:ho + (tq + 1) * 512],
                    pd2[:, tq * 512:(tq + 1) * 512], QSC,
                    x_ch[dc][:, ho + tq * 512:ho + (tq + 1) * 512], ALU.mult, ALU.add),
                    inc="dve_acc")
        acc_base = 32 * s4
        for c in range(NCH):
            p.wait("sync", "dve_acc", acc_base + 16 + 2 * (c + 1))
            p.dma("sync", lambda e, c=c, co=co: e.dma_start(
                acc_d.ap()[c * 128:(c + 1) * 128, co:co + T], x_ch[c]), inc="st_acc")


def emit_p2_tile(p, io, pp, dt8):
    nc = p.nc
    xn_d, y_d = io["xn"], io["y"]
    ldt_d, B_d, C_d, Dv_d = io["log_dt"], io["Bmat"], io["Cmat"], io["Dvec"]
    A_d, iA_d = io["A_bc"], io["invA_bc"]
    idb_d, jrev_d, jv129_d, gm_d = io["ident_b"], io["jrev"], io["jv129"], io["gmask"]
    krep_d, urev_d, v_d = io["krep"], io["urev"], io["vdram"]
    PB1, PB2 = pp["PB1"], pp["PB2"]
    pY = PB1[:, 0:512]
    pP = PB1[0:64, 512:1024]
    pT = PB2[0:64, 0:128]
    pTRb = [PB2[:, 512:576].bitcast(BF16), PB2[:, 1024:1088].bitcast(BF16),
            PB2[:, 1536:1600].bitcast(BF16), PB2[:, 0:64].bitcast(BF16)]
    ro = dt8 * 128

    ldt = p.sb("ldt", [128, 1], F32)
    Bs = p.sb("Bs", [128, 64], F32)
    Cs = p.sb("Cs", [128, 64], F32)
    Ds = p.sb("Ds", [128, 1], F32)
    Abc = p.sb("Abc", [128, 64], F32)
    iAbc = p.sb("iAbc", [128, 64], F32)
    idb = p.sb("idb", [128, 128], BF16)
    jrev = p.sb("jrev", [128, 128], F32)
    jv129 = p.sb("jv129", [128, 129], F32)
    gm = p.sb("gm", [64, 4], F32)
    dt_t = p.sb("dt_t", [128, 1], F32)
    dtA = p.sb("dtA", [128, 64], F32)
    ones64 = p.sb("ones64", [128, 64], F32)
    CBt = p.sb("CBt", [128, 64], F32)
    grid = p.sb("grid", [128, 129 * 64], F32)
    urev_sb = p.sb("urev_sb", [128, 8192], BF16)
    v_sb = p.sb("v_sb", [128, 64 * 129], BF16)
    K_sb = p.sb("K_sb", [128, 256], F32)
    K_bf = p.sb("K_bf", [128, 256], BF16)
    lamQ = p.sb("lamQ", [128, 64], F32)
    lamQT = p.sb("lamQT", [64, 128], F32)
    lamQ16 = p.sb("lamQ16", [128, 64], F32)
    lamQ16T = p.sb("lamQ16T", [64, 128], F32)
    lqp = p.sb("lqp", [64, 128 * 16], F32)
    pwt = p.sb("pwt", [64, 128], F32)
    xfm = p.sb("xfm", [128, 8192], BF16)
    xraw = p.sb("xraw", [128, 8192], BF16)
    y_sb = p.sb("ysb", [128, 8192], BF16)
    P_all = p.sb("Pall", [64, 8192], BF16)
    S_shift = p.sb("S_shift", [64, 128 * 65], F32)
    S_bf = grid[:].bitcast(BF16)[0:64, 0:8320]
    EndA = grid[0:64, 4500:5012]
    Dd = grid[0:64, 5012:5524]
    Outq = grid[0:64, 5524:5652]
    ctmp = grid[0:64, 5652:7700]
    T8b = [p.sb(f"T8_{i}", [128, 8 * 128], BF16) for i in range(2)]
    W8 = [p.sb(f"W8_{i}", [128, 8 * 64], BF16) for i in range(2)]
    M8 = [p.sb(f"M8_{i}", [64, 8 * 128], BF16) for i in range(2)]

    for t, d in ((idb, idb_d), (jrev, jrev_d), (jv129, jv129_d), (gm, gm_d),
                 (Abc, A_d), (iAbc, iA_d)):
        p.dma("sync", lambda e, t=t, d=d: e.dma_start(t[:], d.ap()), inc="lp")
    for t, d in ((ldt, ldt_d), (Bs, B_d), (Cs, C_d), (Ds, Dv_d)):
        p.dma("sync", lambda e, t=t, d=d: e.dma_start(t[:], d.ap()[ro:ro + 128, :]), inc="lp")
    for pc in range(8):
        p.dma("scalar", lambda e, pc=pc: e.dma_start(
            xfm[:, pc * 1024:(pc + 1) * 1024],
            xn_d.ap()[ro:ro + 128, pc * 1024:(pc + 1) * 1024]), inc="lx")

    p.wait("scalar", "lp", 160)
    p.op("scalar", lambda e: e.activation(dt_t[:], ldt[:], AF.Exp), inc="a")
    p.wait("vector", "a", p.count("a"))
    p.op("vector", lambda e: e.memset(ones64[:], 1.0))
    p.op("vector", lambda e: e.tensor_scalar_mul(dtA[:], Abc[:], dt_t[:, 0:1]))
    p.op("vector", lambda e: e.tensor_copy(CBt[:], dtA[:]), inc="v")
    p.wait("scalar", "v", p.count("v"))
    p.op("scalar", lambda e: e.activation(CBt[:], CBt[:], AF.Exp), inc="a")
    p.wait("vector", "a", p.count("a"))
    p.op("vector", lambda e: e.tensor_sub(CBt[:], CBt[:], ones64[:]))
    p.op("vector", lambda e: e.tensor_mul(CBt[:], CBt[:], iAbc[:]))
    p.op("vector", lambda e: e.tensor_mul(CBt[:], CBt[:], Bs[:]))
    p.op("vector", lambda e: e.tensor_mul(CBt[:], CBt[:], Cs[:]))
    p.op("vector", lambda e: e.memset(K_sb[:], 0.0))
    for hf in range(2):
        p.op("vector", lambda e, hf=hf: e.tensor_tensor(
            bass.AP(grid, hf * 4096, [[8256, 128], [64, 64], [1, 64]]),
            bass.AP(dtA, 0, [[64, 128], [0, 64], [1, 64]]),
            bass.AP(jv129, hf * 64, [[129, 128], [1, 64], [0, 64]]), ALU.mult), inc="v")
        p.wait("scalar", "v", p.count("v"))
        p.op("scalar", lambda e, hf=hf: e.activation(
            grid[:, hf * 4096:(hf + 1) * 4096], grid[:, hf * 4096:(hf + 1) * 4096],
            AF.Exp), inc="a")
    for hf in range(2):
        p.wait("vector", "a", hf + 1)
        p.op("vector", lambda e, hf=hf: e.tensor_tensor(
            grid[:, hf * 4096:(hf + 1) * 4096], grid[:, hf * 4096:(hf + 1) * 4096],
            bass.AP(CBt, 0, [[64, 128], [0, 64], [1, 64]]), ALU.mult))
        p.op("vector", lambda e, hf=hf: e.tensor_reduce(
            K_sb[:, 127 + hf * 64:127 + (hf + 1) * 64],
            grid[:].rearrange("p (j n) -> p j n", n=64)[:, hf * 64:(hf + 1) * 64, :],
            AX.X, ALU.add))
    p.op("vector", lambda e: e.tensor_add(K_sb[:, 127:128], K_sb[:, 127:128], Ds[:]))
    p.op("vector", lambda e: e.tensor_copy(K_bf[:], K_sb[:]), inc="kv")
    p.wait("sync", "kv", 1)
    p.dma("sync", lambda e: e.dma_start(
        bass.AP(krep_d, ro * 32768, [[32768, 128], [1, 256]]), K_bf[:]), inc="kr")
    for s in range(7):
        n = 256 << s
        p.wait("sync", "kr", 16 * (s + 1))
        p.dma("sync", lambda e, n=n: e.dma_start(
            bass.AP(krep_d, ro * 32768 + n, [[32768, 128], [1, n]]),
            bass.AP(krep_d, ro * 32768, [[32768, 128], [1, n]])), inc="kr")
    for hf in range(2):
        p.op("vector", lambda e, hf=hf: e.tensor_tensor(
            bass.AP(grid, hf * 4096, [[8256, 128], [64, 64], [1, 64]]),
            bass.AP(dtA, 0, [[64, 128], [0, 64], [1, 64]]),
            bass.AP(jrev, hf * 64, [[128, 128], [1, 64], [0, 64]]), ALU.mult), inc="v")
        p.wait("scalar", "v", p.count("v"))
        p.op("scalar", lambda e, hf=hf: e.activation(
            urev_sb[:, hf * 4096:(hf + 1) * 4096], grid[:, hf * 4096:(hf + 1) * 4096],
            AF.Exp), inc="au")
        p.wait("gpsimd", "au", hf + 1)
        p.dma("gpsimd", lambda e, hf=hf: e.dma_start(
            bass.AP(urev_d, ro * 8192 + hf * 4096, [[8192, 128], [1, 4096]]),
            urev_sb[:, hf * 4096:(hf + 1) * 4096]), inc="g")
    for hf in range(2):
        p.wait("vector", "au", 2)
        p.op("vector", lambda e, hf=hf: e.tensor_tensor(
            bass.AP(grid, hf * 4128, [[8256, 128], [129, 32], [1, 129]]),
            bass.AP(dtA, hf * 32, [[64, 128], [1, 32], [0, 129]]),
            bass.AP(jv129, 0, [[129, 128], [0, 32], [1, 129]]), ALU.mult), inc="v")
        p.wait("scalar", "v", p.count("v"))
        p.op("scalar", lambda e, hf=hf: e.activation(
            grid[:, hf * 4128:(hf + 1) * 4128], grid[:, hf * 4128:(hf + 1) * 4128],
            AF.Exp), inc="a")
        p.wait("vector", "a", p.count("a"))
        p.op("vector", lambda e, hf=hf: e.tensor_tensor(
            v_sb[:, hf * 4128:(hf + 1) * 4128], grid[:, hf * 4128:(hf + 1) * 4128],
            bass.AP(CBt, hf * 32, [[64, 128], [1, 32], [0, 129]]), ALU.mult), inc="vv")
    p.wait("scalar", "vv", 2)
    p.op("scalar", lambda e: e.activation(lamQ[:], dtA[:], AF.Exp, scale=128.0), inc="a")
    p.op("scalar", lambda e: e.activation(lamQ16[:], dtA[:], AF.Exp, scale=2048.0), inc="a")
    p.wait("gpsimd", "vv", 2)
    p.dma("gpsimd", lambda e: e.dma_start(
        bass.AP(v_d, ro * 8256, [[8256, 128], [1, 8256]]), v_sb[:]), inc="g")
    p.wait("tensor", "lx", 128)
    for tt in range(64):
        pb = pTRb[tt % 4]
        if tt >= 4:
            p.wait("tensor", "ex", tt - 3)
        p.op("tensor", lambda e, tt=tt, pb=pb: e.transpose(
            pb, xfm[:, tt * 128:(tt + 1) * 128], idb[:]), inc="tx")
        p.wait("scalar", "tx", tt + 1)
        p.op("scalar", lambda e, tt=tt, pb=pb: e.activation(
            xraw[:, tt * 128:(tt + 1) * 128], pb, AF.Copy), inc="ex")

    idf = p.sb("idf", [128, 128], F32)
    p.wait("vector", "lp", 160)
    p.op("vector", lambda e: e.tensor_copy(idf[:], idb[:]))
    p.op("vector", lambda e: e.engine_nop(), inc="idr")
    p.wait("tensor", "idr", 1)
    p.wait("tensor", "a", p.count("a"))
    p.wait("tensor", "ex", 64)
    p.op("tensor", lambda e: e.transpose(pT[:], lamQ[:, 0:64], idf[:]), inc="t1")
    p.wait("vector", "t1", 1)
    p.op("vector", lambda e: e.tensor_copy(lamQT[:], pT[:]), inc="vq")
    p.wait("tensor", "vq", 1)
    p.op("tensor", lambda e: e.transpose(pT[:], lamQ16[:, 0:64], idf[:]), inc="t1")
    p.wait("vector", "t1", 2)
    p.op("vector", lambda e: e.tensor_copy(lamQ16T[:], pT[:]), inc="vq")
    p.op("vector", lambda e: e.memset(pwt[:], 1.0))
    for t in range(16):
        p.op("vector", lambda e, t=t: e.tensor_copy(
            bass.AP(lqp, t, [[2048, 64], [16, 128]]), pwt[:]))
        if t < 15:
            p.op("vector", lambda e: e.tensor_mul(pwt[:], pwt[:], lamQT[:]))
    p.op("vector", lambda e: e.engine_nop(), inc="tabv")

    def rhs_x(ch):
        return bass.AP(xraw, ch, [[8192, 128], [128, 64]])

    p.op("vector", lambda e: e.memset(S_shift[:], 0.0))
    p.wait("tensor", "ex", 64)
    p.wait("sync", "kr", 8 * 16)
    nblk = 16
    for blk in range(nblk):
        buf = blk % 2
        tq_ = "sync" if blk % 2 == 0 else "scalar"
        if blk >= 2:
            p.wait(tq_, "te1", blk - 1)
        p.dma(tq_, lambda e, blk=blk, buf=buf: e.dma_start(
            T8b[buf][:], bass.AP(krep_d, (ro + blk * 8) * 32768 + 127,
                                 [[255, 128], [32768, 8], [1, 128]])), inc=f"tw{buf}")
        if blk >= 2:
            p.wait("gpsimd", "te1", blk - 1)
        p.wait("gpsimd", "g", 32)
        p.dma("gpsimd", lambda e, blk=blk, buf=buf: e.dma_start(
            W8[buf][:], bass.AP(urev_d, (ro + blk * 8) * 8192,
                                [[64, 128], [8192, 8], [1, 64]])), inc=f"w8{buf}")
        p.wait("tensor", f"tw{buf}", 16 * (blk // 2 + 1))
        p.wait("tensor", f"w8{buf}", 16 * (blk // 2 + 1))
        if blk >= 1:
            p.wait("tensor", "ev1", 2 * blk)
        for i in range(8):
            ch = blk * 8 + i
            p.op("tensor", lambda e, i=i, buf=buf, ch=ch: e.matmul(
                pY[:, i * 64:(i + 1) * 64], T8b[buf][:, i * 128:(i + 1) * 128],
                rhs_x(ch), start=True, stop=True))
            p.op("tensor", lambda e, i=i, buf=buf, ch=ch: e.matmul(
                pP[:, i * 64:(i + 1) * 64], W8[buf][:, i * 64:(i + 1) * 64],
                rhs_x(ch), start=True, stop=True),
                inc=("te1" if i == 7 else None))
        p.wait("vector", "te1", blk + 1)
        p.op("vector", lambda e, blk=blk: e.tensor_copy(
            y_sb[:, blk * 512:(blk + 1) * 512], pY[:]), inc="ev1")
        p.op("vector", lambda e, blk=blk: e.tensor_copy(
            P_all[0:64, blk * 512:(blk + 1) * 512], pP[:]), inc="ev1")
        for d in range(blk * 8, blk * 8 + 8):
            p.op("vector", lambda e, d=d: e.tensor_tensor_scan(
                S_shift[0:64, d * 65 + 1:d * 65 + 65],
                lamQT[:, d:d + 1].broadcast_to((64, 64)),
                P_all[0:64, d * 64:(d + 1) * 64],
                0.0, ALU.mult, ALU.add))

    p.wait("vector", "tabv", 1)
    for j in range(4):
        p.op("vector", lambda e, j=j: e.tensor_copy(
            EndA[:, j * 128:(j + 1) * 128],
            bass.AP(S_shift, 16 * j + 16, [[8320, 64], [65, 128]])))
    p.op("vector", lambda e: e.memset(Dd[:], 0.0))
    for pas in range(2):
        if pas == 1:
            p.op("vector", lambda e: e.tensor_scalar_mul(
                Dd[:, 0:128], Outq[:], gm[:, 0:1]))
        for j in range(4):
            p.op("vector", lambda e, j=j: e.tensor_mul(
                Outq[:], lamQ16T[:], Dd[:, j * 128:(j + 1) * 128]))
            p.op("vector", lambda e, j=j: e.tensor_add(
                Outq[:], Outq[:], EndA[:, j * 128:(j + 1) * 128]))
            if j < 3:
                p.op("vector", lambda e, j=j: e.tensor_scalar_mul(
                    Dd[:, (j + 1) * 128:(j + 2) * 128], Outq[:], gm[:, j + 1:j + 2]))
                p.op("vector", lambda e, j=j: e.tensor_sub(
                    Dd[:, (j + 1) * 128:(j + 2) * 128],
                    Dd[:, (j + 1) * 128:(j + 2) * 128],
                    EndA[:, j * 128:(j + 1) * 128]))
    for j in range(4):
        p.op("vector", lambda e, j=j: e.tensor_tensor(
            ctmp[:, 0:2048], lqp[:],
            bass.AP(grid, 5012 + j * 128, [[8256, 64], [1, 128], [0, 16]]), ALU.mult))
        p.op("vector", lambda e, j=j: e.tensor_tensor(
            bass.AP(S_shift, 16 * j, [[8320, 64], [65, 128], [1, 16]]),
            bass.AP(S_shift, 16 * j, [[8320, 64], [65, 128], [1, 16]]),
            ctmp[:, 0:2048], ALU.add))
    p.op("vector", lambda e: e.tensor_copy(S_bf, S_shift[:]), inc="scan")

    for blk in range(nblk):
        buf = blk % 2
        if blk == 0:
            p.wait("scalar", "g", 48)
        if blk >= 2:
            p.wait("scalar", "teM", blk - 1)
        p.dma("scalar", lambda e, blk=blk, buf=buf: e.dma_start(
            M8[buf][:], bass.AP(v_d, (ro + blk * 8) * 8256 + 1,
                                [[129, 64], [8256, 8], [1, 128]])), inc=f"md{buf}")
        p.wait("tensor", f"md{buf}", 16 * (blk // 2 + 1))
        p.wait("tensor", "scan", 1)
        if blk >= 1:
            p.wait("tensor", "evM", blk)
        for i in range(8):
            ch = blk * 8 + i
            p.op("tensor", lambda e, i=i, buf=buf, ch=ch: e.matmul(
                pY[:, i * 64:(i + 1) * 64], M8[buf][:, i * 128:(i + 1) * 128],
                S_bf[0:64, ch * 65:ch * 65 + 64], start=True, stop=True),
                inc=("teM" if i == 7 else None))
        p.wait("vector", "teM", blk + 1)
        p.op("vector", lambda e, blk=blk: e.tensor_add(
            y_sb[:, blk * 512:(blk + 1) * 512],
            y_sb[:, blk * 512:(blk + 1) * 512], pY[:]), inc="evM")

    out_r = xraw[:].rearrange("p (c d) -> p c d", c=64)
    in_r = bass.AP(y_sb, 0, [[8192, 128], [1, 64], [64, 128]])
    p.wait("vector", "evM", nblk)
    p.op("vector", lambda e, o=out_r, i=in_r: e.tensor_copy(o, i), inc="rs")
    p.wait("sync", "rs", 1)
    dst = bass.AP(y_d, ro, [[1024, 128], [128 * 1024, 64], [1, 128]])
    p.dma("sync", lambda e, dst=dst: e.dma_start(
        dst, xraw[:].rearrange("p (c d) -> p c d", c=64)), inc="sty")


def emit_p3_stripe(p, io, pp, s4):
    nc = p.nc
    acc_d, y_d = io["acc"], io["y"]
    wT_d, ib_d, lw_d, lb_d = io["in_projT"], io["in_proj_b"], io["ln_w"], io["ln_b"]
    x1_d, xn1_d, xn2_d = io["x1"], io["xn1"], io["xn2"]
    q_d, K_d, V_d, idb_d = io["q"], io["K"], io["V"], io["ident_b"]
    pbig1, pbig2 = pp["PB1"], pp["PB2"]
    co = s4 * T

    x1_sb = p.sb("x1_sb", [128, NCH * T], F32)
    xn1_sb = p.sb("xn1_sb", [128, NCH * T], BF16)
    xn2_sb = p.sb("xn2_sb", [128, NCH * T], BF16)
    x1_ch = [x1_sb[:, c * T:(c + 1) * T] for c in range(NCH)]
    xn1_ch = [xn1_sb[:, c * T:(c + 1) * T] for c in range(NCH)]
    xn2_ch = [xn2_sb[:, c * T:(c + 1) * T] for c in range(NCH)]
    lw_sb = p.sb("lw_sb", [128, NCH], F32)
    lb_sb = p.sb("lb_sb", [128, NCH], F32)
    qb_sb = p.sb("qb_sb", [128, 16], F32)
    idb = p.sb("idb", [128, 128], BF16)
    yst = [p.sb(f"yst{i}", [128, 16 * 128], BF16) for i in range(2)]
    wqk = [p.sb(f"wqk{i}", [128, NCH * 128], BF16) for i in range(2)]
    wv_sb = p.sb("wv_sb", [128, NCH * D], BF16)
    ev = [p.sb(f"ev{i}", [128, 1024], BF16) for i in range(2)]
    pTRb = [pbig2[:, 0:64].bitcast(BF16), pbig2[:, 512:576].bitcast(BF16)]

    for c in range(NCH):
        p.dma("sync", lambda e, c=c: e.dma_start(
            x1_ch[c], acc_d.ap()[c * 128:(c + 1) * 128, co:co + T]), inc="ldx1")
    p.dma("sync", lambda e: e.dma_start(idb[:], idb_d.ap()), inc="ldb")
    for c in range(NCH):
        p.dma("sync", lambda e, c=c: e.dma_start(lw_sb[:, c:c + 1], lw_d.ap()[c * 128:(c + 1) * 128, :]), inc="ldb")
        p.dma("sync", lambda e, c=c: e.dma_start(lb_sb[:, c:c + 1], lb_d.ap()[c * 128:(c + 1) * 128, :]), inc="ldb")
    for c in range(16):
        p.dma("sync", lambda e, c=c: e.dma_start(qb_sb[:, c:c + 1], ib_d.ap()[c * 128:(c + 1) * 128, :]), inc="ldb")
    p.dma("gpsimd", lambda e: e.dma_start(
        wv_sb[:].rearrange("p (c f) -> p c f", c=NCH),
        bass.AP(wT_d, 2 * D, [[3 * D, 128], [128 * 3 * D, NCH], [1, D]])), inc="gwv")

    p.wait("vector", "ldx1", NCH * 16)
    p.wait("tensor", "ldb", 33 * 16)
    for c in range(NCH):
        buf = c % 2
        if c >= 2:
            p.wait("scalar", "jt", (c - 1) * 16)
        src = bass.AP(y_d, co * D + c * 128, [[D, 128], [128 * D, 16], [1, 128]])
        p.dma("scalar", lambda e, src=src, buf=buf: e.dma_start(
            yst[buf][:].rearrange("p (t f) -> p t f", t=16), src), inc=f"jy{buf}")
        p.wait("tensor", f"jy{buf}", 16 * (c // 2 + 1))
        for tt in range(16):
            nj = c * 16 + tt
            if nj >= 2:
                p.wait("tensor", "ja", nj - 1)
            p.op("tensor", lambda e, buf=buf, tt=tt, nj=nj: e.transpose(
                pTRb[nj % 2], yst[buf][:, tt * 128:(tt + 1) * 128], idb[:]), inc="jt")
            p.wait("vector", "jt", nj + 1)
            p.op("vector", lambda e, c=c, tt=tt, nj=nj: e.tensor_add(
                x1_ch[c][:, tt * 128:(tt + 1) * 128],
                x1_ch[c][:, tt * 128:(tt + 1) * 128],
                pTRb[nj % 2]), inc="ja")
    lnsc = make_ln_scratch(p)
    bc1 = xn2_sb[:].bitcast(F32)[:, 0:2 * T]
    ln1_done = emit_ln_fm(p, "n1", x1_ch, xn1_ch, pbig1, pbig2, lnsc,
                          x_ready=("ja", nj), bc_sb=bc1)
    lnw = [lw_sb[:, c:c + 1] for c in range(NCH)]
    lnb = [lb_sb[:, c:c + 1] for c in range(NCH)]
    p.wait("vector", "ldb", 33 * 16)
    ln2_done = emit_ln_fm(p, "n2", xn1_ch, xn2_ch, pbig1, pbig2, lnsc,
                          lnw=lnw, lnb=lnb, x_ready=("n1_dve", ln1_done),
                          x_prod=("n1_dve", ln1_done - 16, 2))

    if s4 == 0:
        p.wait("sync", "n1_dve", ln1_done)
        for c in range(NCH):
            p.dma("sync", lambda e, c=c: e.dma_start(
                xn1_d.ap()[c * 128:(c + 1) * 128, :], xn1_ch[c]), inc="st1")
            p.dma("sync", lambda e, c=c: e.dma_start(
                x1_d.ap()[c * 128:(c + 1) * 128, :], x1_ch[c]), inc="st1")
        p.wait("sync", "n2_dve", ln2_done)
        for c in range(NCH):
            p.dma("sync", lambda e, c=c: e.dma_start(
                xn2_d.ap()[c * 128:(c + 1) * 128, :], xn2_ch[c]), inc="st1")

    ps_qk = [pbig1[:, i * 512:(i + 1) * 512] for i in range(4)]
    oc_list = list(range(16)) if s4 == 0 else list(range(8, 16))
    for idx, oc in enumerate(oc_list):
        buf = idx % 2
        if idx >= 2:
            p.wait("gpsimd", "teqk", idx - 1)
        p.dma("gpsimd", lambda e, oc=oc, buf=buf: e.dma_start(
            wqk[buf][:].rearrange("p (c f) -> p c f", c=NCH),
            bass.AP(wT_d, oc * 128, [[3 * D, 128], [128 * 3 * D, NCH], [1, 128]])), inc=f"gqk{buf}")
        p.wait("tensor", f"gqk{buf}", 16 * (idx // 2 + 1))
        p.wait("tensor", "n2_dve", ln2_done)
        if idx >= 1:
            p.wait("tensor", "evqk", (idx - 1) * 4 + 4)
        for dc in range(NCH):
            for s in range(4):
                p.op("tensor", lambda e, s=s, dc=dc, buf=buf: e.matmul(
                    ps_qk[s], wqk[buf][:, dc * 128:(dc + 1) * 128],
                    xn2_ch[dc][:, s * 512:(s + 1) * 512],
                    start=(dc == 0), stop=(dc == NCH - 1)),
                    inc=("teqk" if (dc == NCH - 1 and s == 3) else None))
        p.wait("vector", "teqk", p.count("teqk"))
        for s in range(4):
            nev = idx * 4 + s
            if nev >= 2:
                p.wait("vector", f"stq{nev % 2}", 16 * (nev // 2))
            p.op("vector", lambda e, oc=oc, s=s, buf=buf: e.tensor_scalar_add(
                ev[buf][:, (s % 2) * 512:(s % 2 + 1) * 512],
                ps_qk[s], qb_sb[:, oc:oc + 1]), inc="evqk")
            if oc < 8:
                tgt, row, coo = q_d, oc * 128, 0
            else:
                tgt, row, coo = K_d, (oc - 8) * 128, co
            p.wait("sync", "evqk", p.count("evqk"))
            p.dma("sync", lambda e, tgt=tgt, row=row, s=s, buf=buf, coo=coo: e.dma_start(
                tgt.ap()[row:row + 128, coo + s * 512:coo + (s + 1) * 512],
                ev[buf][:, (s % 2) * 512:(s % 2 + 1) * 512]), inc=f"stq{(idx * 4 + s) % 2}")

    for tt in range(16):
        buf = tt % 2
        p.wait("tensor", "gwv", 16)
        if tt >= 1:
            p.wait("tensor", "evv", tt)
        for dc in range(NCH):
            for h in range(2):
                p.op("tensor", lambda e, tt=tt, dc=dc, h=h: e.matmul(
                    pbig2[:, h * 512:(h + 1) * 512],
                    xn2_ch[dc][:, tt * 128:(tt + 1) * 128],
                    wv_sb[:, dc * D + h * 512: dc * D + (h + 1) * 512],
                    start=(dc == 0), stop=(dc == NCH - 1)),
                    inc=("tev" if dc == NCH - 1 and h == 1 else None))
        p.wait("vector", "tev", p.count("tev"))
        if tt >= 2:
            p.wait("vector", f"sv{buf}", 16 * (tt // 2))
        p.op("vector", lambda e, buf=buf: e.tensor_copy(ev[buf][:], pbig2[:, 0:1024]), inc="evv")
        p.wait("sync", "evv", p.count("evv"))
        p.dma("sync", lambda e, tt=tt, buf=buf: e.dma_start(
            V_d.ap()[co + tt * 128:co + (tt + 1) * 128, :], ev[buf][:]), inc=f"sv{buf}")


def build_fused(dbg=False):
    nc = bass.Bass("TRN2", target_bir_lowering=False, debug=False)
    p = Prog(nc)
    EI, INT = "ExternalInput", "Internal"
    io = dict(
        x=nc.dram_tensor("x_rot_fm", [D, T8], F32, kind=EI),
        wgT=nc.dram_tensor("wgT", [D, HID0], I32, kind=EI),
        wuT=nc.dram_tensor("wuT", [D, HID0], I32, kind=EI),
        wdT=nc.dram_tensor("wdT", [HID0, D], I32, kind=EI),
        log_dt=nc.dram_tensor("log_dt", [D, 1], F32, kind=EI),
        Bmat=nc.dram_tensor("Bmat", [D, 64], F32, kind=EI),
        Cmat=nc.dram_tensor("Cmat", [D, 64], F32, kind=EI),
        Dvec=nc.dram_tensor("Dvec", [D, 1], F32, kind=EI),
        A_bc=nc.dram_tensor("A_bc", [128, 64], F32, kind=EI),
        invA_bc=nc.dram_tensor("invA_bc", [128, 64], F32, kind=EI),
        ident_b=nc.dram_tensor("ident_b", [128, 128], BF16, kind=EI),
        jrev=nc.dram_tensor("jrev", [128, 128], F32, kind=EI),
        jv129=nc.dram_tensor("jv129", [128, 129], F32, kind=EI),
        gmask=nc.dram_tensor("gmask", [64, 4], F32, kind=EI),
        in_projT=nc.dram_tensor("in_projT", [D, 3 * D], F32, kind=EI),
        in_proj_b=nc.dram_tensor("in_proj_b", [3 * D, 1], F32, kind=EI),
        ln_w=nc.dram_tensor("ln_w", [D, 1], F32, kind=EI),
        ln_b=nc.dram_tensor("ln_b", [D, 1], F32, kind=EI),
        out_projT=nc.dram_tensor("out_projT", [D, D], F32, kind=EI),
        out_proj_b=nc.dram_tensor("out_proj_b", [D, 1], F32, kind=EI),
        v_bias=nc.dram_tensor("v_bias", [D, 1], F32, kind=EI),
        w0T=nc.dram_tensor("w0T", [4 * 256, 256], F32, kind=EI),
        w2T=nc.dram_tensor("w2T", [4 * 256, 256], F32, kind=EI),
        w1l=nc.dram_tensor("w1l", [4 * 256, 256], F32, kind=EI),
        w1T=nc.dram_tensor("w1T", [4 * 256, 256], F32, kind=EI),
        gate_wT=nc.dram_tensor("gate_wT", [D, 8], F32, kind=EI),
        bal_b=nc.dram_tensor("bal_b", [1, 8], F32, kind=EI),
        moew1T=nc.dram_tensor("moew1T", [8 * D, 2048], F32, kind=EI),
        moew2T=nc.dram_tensor("moew2T", [8 * 2048, D], F32, kind=EI),
        xn=nc.dram_tensor("xn_i", [D, T8], BF16, kind=INT),
        acc=nc.dram_tensor("acc_i", [D, T8], F32, kind=INT),
        krep=nc.dram_tensor("krep_i", [D, 32768], BF16, kind=INT),
        urev=nc.dram_tensor("urev_i", [D, 8192], BF16, kind=INT),
        vdram=nc.dram_tensor("vdram_i", [D, 8256], BF16, kind=INT),
        y=nc.dram_tensor("y_i", [T8, D], BF16, kind=INT),
        x1=nc.dram_tensor("x1_i", [D, T], F32, kind=INT),
        xn1=nc.dram_tensor("xn1_i", [D, T], BF16, kind=INT),
        xn2=nc.dram_tensor("xn2_i", [D, T], BF16, kind=INT),
        q=nc.dram_tensor("q_i", [D, T], BF16, kind=INT),
        K=nc.dram_tensor("K_i", [D, T8], BF16, kind=INT),
        V=nc.dram_tensor("V_i", [T8, D], BF16, kind=INT),
        y1=nc.dram_tensor("y1_i", [D, T], F32, kind=INT),
        mw1b=nc.dram_tensor("mw1b", [8 * D, 2048], BF16, kind=INT),
        mw2b=nc.dram_tensor("mw2b", [8 * 2048, D], BF16, kind=INT),
        wc=nc.dram_tensor("wc_dram", [8, T], F32, kind=INT),
        out=nc.dram_tensor("out_fm", [D, T], F32, kind="ExternalOutput"),
    )
    PB1 = p.ps("PB1", [128, 2048], F32)
    PB2 = p.ps("PB2", [128, 2048], F32)
    ppA = dict(PB1=PB1, PB2=PB2)
    pp4 = dict(psA=PB1[:, 0:1024], psB=PB1[:, 1024:2048],
               psE=PB2[0:65, 0:512], psF=PB2[0:65, 512:1024],
               psC=PB2[:, 1024:1536], psD=PB2[:, 1536:2048])
    pp5 = dict(psA=PB1[:, 0:1024], psB=PB1[:, 1024:2048],
               psC=PB2[:, 0:512], psD=PB2[:, 512:1024],
               psT=PB2[:, 1024:1088].bitcast(BF16))

    if dbg:
        dbg_keys = ["xn", "acc", "y", "x1", "xn1", "xn2", "q", "K", "V", "y1"]
        for kk in dbg_keys:
            t = io[kk]
            io["dbg_" + kk] = nc.dram_tensor("dbg_" + kk, list(t.shape), t.dtype,
                                             kind="ExternalOutput")
    emit_p1(p, io, ppA)
    p.barrier()
    for dt8 in range(NCH):
        emit_p2_tile(p, io, ppA, dt8)
        p.barrier()
    for s4 in range(4):
        emit_p3_stripe(p, io, ppA, s4)
        p.barrier()
    io4 = dict(io)
    emit_p4(p, io4, pp4)
    p.barrier()
    emit_p5(p, io, pp5)
    if dbg:
        p.barrier()
        for kk in dbg_keys:
            p.dma("sync", lambda e, kk=kk: e.dma_start(
                io["dbg_" + kk].ap(), io[kk].ap()), inc="dbgo")
    p.finish_and_emit()
    nins = sum(len(q) for q in p.q.values())
    return nc, nins


def kernel(**inputs):
    x = np.asarray(inputs["L0_attn_x"])
    wg = np.asarray(inputs["L0_ffn_w_gate"]); wu = np.asarray(inputs["L0_ffn_w_up"])
    wd = np.asarray(inputs["L0_ffn_w_down"])
    log_dt = np.asarray(inputs["L0_attn_log_dt"])
    Bm = np.asarray(inputs["L0_attn_B"]); Cm = np.asarray(inputs["L0_attn_C"])
    Dv = np.asarray(inputs["L0_attn_D"])
    lnw = np.asarray(inputs["L1_attn_ln_w"]); lnb = np.asarray(inputs["L1_attn_ln_b"])
    ipw = np.asarray(inputs["L1_attn_in_proj_w"]); ipb = np.asarray(inputs["L1_attn_in_proj_b"])
    opw = np.asarray(inputs["L1_attn_out_proj_w"]); opb = np.asarray(inputs["L1_attn_out_proj_b"])
    w0 = np.asarray(inputs["L1_attn_w0"]); w1 = np.asarray(inputs["L1_attn_w1"])
    w2 = np.asarray(inputs["L1_attn_w2"])
    gw = np.asarray(inputs["L1_ffn_gate_w"]); bal = np.asarray(inputs["L1_ffn_balance_bias"])
    mw1 = np.asarray(inputs["L1_ffn_w1"]); mw2 = np.asarray(inputs["L1_ffn_w2"])

    common = dict(
        wgT=np.ascontiguousarray(wg.T), wuT=np.ascontiguousarray(wu.T),
        wdT=np.ascontiguousarray(wd.T),
        log_dt=log_dt.reshape(-1, 1), Bmat=Bm, Cmat=Cm, Dvec=Dv.reshape(-1, 1),
        A_bc=np.tile((-0.5 * np.arange(1, 65)).astype(np.float32), (128, 1)),
        invA_bc=np.tile((1.0 / (-0.5 * np.arange(1, 65))).astype(np.float32), (128, 1)),
        ident_b=np.eye(128, dtype=BF16NP),
        jrev=np.tile((127.0 - np.arange(128)).astype(np.float32), (128, 1)),
        jv129=np.tile(np.arange(129, dtype=np.float32), (128, 1)),
        in_projT=np.ascontiguousarray(ipw.T), in_proj_b=ipb.reshape(-1, 1),
        ln_w=lnw.reshape(-1, 1), ln_b=lnb.reshape(-1, 1),
        out_projT=np.ascontiguousarray(opw.T), out_proj_b=opb.reshape(-1, 1),
        v_bias=ipb[2048:].reshape(-1, 1),
        w0T=np.ascontiguousarray(np.transpose(w0, (0, 2, 1))).reshape(4 * 256, 256),
        w2T=np.ascontiguousarray(np.transpose(w2, (0, 2, 1))).reshape(4 * 256, 256),
        w1l=np.ascontiguousarray(w1).reshape(4 * 256, 256),
        w1T=np.ascontiguousarray(np.transpose(w1, (0, 2, 1))).reshape(4 * 256, 256),
        gate_wT=np.ascontiguousarray(gw.T), bal_b=bal.reshape(1, 8),
        moew1T=np.ascontiguousarray(np.transpose(mw1, (0, 2, 1))).reshape(8 * 1024, 2048),
        moew2T=np.ascontiguousarray(np.transpose(mw2, (0, 2, 1))).reshape(8 * 2048, 1024),
    )
    in_maps = []
    for c in range(8):
        b, r = c // 4, c % 4
        xb = x[b]
        xrot = np.concatenate([xb[r * T:], xb[:r * T]], axis=0)
        g = np.ones(4, np.float32)
        g[(4 - r) % 4] = 0.0
        m = dict(common)
        m["x_rot_fm"] = np.ascontiguousarray(xrot.T)
        m["gmask"] = np.tile(g, (64, 1))
        in_maps.append(m)
    nc, nins = build_fused(dbg=bool(inputs.get("_dbg")))
    res = run_bass_kernel_spmd(nc, in_maps, core_ids=list(range(8)))
    if inputs.get("_dbg"):
        kernel._dbg_res = res
    out = np.zeros((2, T8, D), np.float32)
    for c in range(8):
        b, r = c // 4, c % 4
        out[b, r * T:(r + 1) * T] = np.asarray(res.results[c]["out_fm"]).T
    return np.ascontiguousarray(out)


# version 24
# speedup vs baseline: 1.1385x; 1.1385x over previous
import numpy as np
import ml_dtypes
import concourse.bass as bass
import concourse.mybir as mybir
from concourse.bass_utils import run_bass_kernel_spmd

F32 = mybir.dt.float32
BF16 = mybir.dt.bfloat16
I32 = mybir.dt.int32
AF = mybir.ActivationFunctionType
ALU = mybir.AluOpType
AX = mybir.AxisListType
BF16NP = ml_dtypes.bfloat16

D = 1024
T = 2048
NCH = 8
HID0 = 3072
NH0 = HID0 // 128
QSC = 1.0 / 127.0
EPS = 1e-5


class Prog:
    ENGINES = ("sync", "vector", "scalar", "tensor", "gpsimd")

    def __init__(self, nc):
        self.nc = nc
        self.q = {e: [] for e in self.ENGINES}
        self.pool = []
        self.lmap = {}
        self._tensors = {}
        self._uid = 0
        self.nbar = 0
        self.bar = None
        self.arena0 = None
        self.arena_off = 0
        self.arena_hwm = 0

    def sb(self, name, shape, dtype=F32):
        if self.arena0 is None:
            self.arena0 = (self.nc.sbuf_base + 63) & ~63
        esz = {F32: 4, BF16: 2, I32: 4}[dtype]
        nbytes = int(shape[1]) * esz
        off = (self.arena0 + self.arena_off + 63) & ~63
        self._uid += 1
        t = self.nc.alloc_sbuf_tensor_at(f"{name}_u{self._uid}", list(shape), dtype,
                                         offset=off)
        self.arena_off = off + nbytes - self.arena0
        self.arena_hwm = max(self.arena_hwm, self.arena_off)
        assert self.arena0 + self.arena_off <= self.nc.sbuf_top - 64, \
            f"arena overflow at {name}: {self.arena0 + self.arena_off}"
        self._tensors[name] = t
        return t

    def ps(self, name, shape, dtype=F32):
        t = self.nc.alloc_psum_tensor(name, list(shape), dtype)
        self._tensors[name] = t
        return t

    def _sem(self, name):
        if name not in self.lmap:
            idx = len(self.lmap)
            if idx >= len(self.pool):
                self.pool.append([self.nc.alloc_semaphore(f"ps{len(self.pool)}"), 0])
            self.lmap[name] = (idx, self.pool[idx][1])
        idx, base = self.lmap[name]
        return self.pool[idx], base

    def count(self, name):
        s, base = self._sem(name)
        return s[1] - base

    def wait(self, eng, semname, val=None):
        s, base = self._sem(semname)
        v = (s[1] - base) if val is None else val
        if v > 0:
            self.q[eng].append(("w", s[0], v + base))

    def op(self, eng, fn, inc=None, n=1):
        if inc is not None:
            s, _ = self._sem(inc)
            s[1] += n
            self.q[eng].append(("o", fn, s[0], n))
        else:
            self.q[eng].append(("o", fn, None, 0))

    def dma(self, eng, fn, inc):
        self.op(eng, fn, inc=inc, n=16)

    def barrier(self):
        if self.bar is None:
            self.bar = self.nc.alloc_semaphore("bar")
        for h, c in self.pool:
            if c > 0:
                self.q["sync"].append(("w", h, c))
        self.nbar += 1
        for eng in self.ENGINES:
            self.q[eng].append(("o", (lambda e: e.nop()), self.bar, 1))
        for eng in self.ENGINES:
            self.q[eng].append(("w", self.bar, 5 * self.nbar))
        self.lmap = {}
        self.arena_off = 0

    def finish_and_emit(self):
        for h, c in self.pool:
            if c > 0:
                self.q["sync"].append(("w", h, c))
        nc = self.nc
        with nc.Block() as block:
            for eng_name, dec in (("sync", block.sync), ("vector", block.vector),
                                  ("scalar", block.scalar), ("tensor", block.tensor),
                                  ("gpsimd", block.gpsimd)):
                items = self.q[eng_name]
                if not items:
                    continue

                def body(e, items=items):
                    for it in items:
                        if it[0] == "w":
                            e.wait_ge(it[1], it[2])
                        else:
                            ins = it[1](e)
                            if it[2] is not None:
                                ins.then_inc(it[2], it[3])
                dec(body)


def make_ln_scratch(p, pre="lns"):
    rows1 = p.sb(pre + "_rows1", [33, T], F32)
    rows2 = p.sb(pre + "_rows2", [33, T], F32)
    return dict(
        ones_b=p.sb(pre + "_onesb", [128, 1], BF16),
        ones_f=p.sb(pre + "_onesf", [1, 128], F32),
        sq=[p.sb(pre + f"_sq{i}", [128, T], BF16) for i in range(2)],
        rowm=rows1[0:1, :],
        rowq=rows1[32:33, :],
        rowt=rows2[32:33, :],
        rowr=rows2[0:1, :],
        epst=p.sb(pre + "_eps", [1, 1], F32),
        init=[False],
    )


def emit_ln_fm(p, pre, x_ch, xn_ch, ps1, ps2, sc, lnw=None, lnb=None, x_ready=None,
               x_chunks=None, bc_sb=None, x_prod=None):
    nc = p.nc
    ones_b = sc["ones_b"]; ones_f = sc["ones_f"]; sq = sc["sq"]
    rowm = sc["rowm"]; rowq = sc["rowq"]; rowr = sc["rowr"]
    rowt = sc["rowt"]; epst = sc["epst"]
    s_te, s_act, s_dve = pre + "_te", pre + "_act", pre + "_dve"
    NS = T // 512
    in_bf = (x_ch[0].dtype == BF16)

    xb = x_ch if in_bf else xn_ch
    if not sc["init"][0]:
        sc["init"][0] = True
        p.op("vector", lambda e: e.memset(ones_b[:], 1.0), inc=s_dve)
        p.op("vector", lambda e: e.memset(ones_f[:], 1.0), inc=s_dve)
        p.op("vector", lambda e: e.memset(epst[:], EPS), inc=s_dve)
    cast_start = p.count(s_dve)
    if x_chunks is not None:
        xname, xbase = x_chunks
        if not in_bf:
            for c in range(NCH):
                p.wait("vector", xname, xbase + 16 * (c + 1))
                p.op("vector", lambda e, c=c: e.tensor_copy(xb[c], x_ch[c]), inc=s_dve)
    else:
        if x_ready is not None:
            p.wait("vector", *x_ready)
        if not in_bf:
            for c in range(NCH):
                p.op("vector", lambda e, c=c: e.tensor_copy(xb[c], x_ch[c]), inc=s_dve)
    casts_done = p.count(s_dve)

    if in_bf and x_ready is not None and x_prod is None:
        p.wait("tensor", *x_ready)
        p.wait("scalar", *x_ready)
    for s in range(NS):
        for c in range(NCH):
            if not in_bf:
                p.wait("tensor", s_dve, cast_start + c + 1)
            elif x_prod is not None and s == 0:
                p.wait("tensor", x_prod[0], x_prod[1] + x_prod[2] * (c + 1))
            p.op("tensor",
                 lambda e, s=s, c=c: e.matmul(ps1[0:1, s * 512:(s + 1) * 512],
                                              ones_b[:], xb[c][:, s * 512:(s + 1) * 512],
                                              start=(c == 0), stop=(c == NCH - 1)),
                 inc=(s_te if (s == NS - 1 and c == NCH - 1) else None))
    for c in range(NCH):
        if not in_bf:
            p.wait("scalar", s_dve, cast_start + c + 1)
        elif x_prod is not None:
            p.wait("scalar", x_prod[0], x_prod[1] + x_prod[2] * (c + 1))
        else:
            p.wait("scalar", s_dve, casts_done)
        if c >= 2:
            p.wait("scalar", s_te, 1 + (c - 1))
        p.op("scalar", lambda e, c=c: e.activation(sq[c % 2][:], xb[c], AF.Square),
             inc=s_act)
        p.wait("tensor", s_act, p.count(s_act))
        for s in range(NS):
            p.op("tensor",
                 lambda e, s=s, c=c: e.matmul(ps1[32:33, s * 512:(s + 1) * 512],
                                              ones_b[:], sq[c % 2][:, s * 512:(s + 1) * 512],
                                              start=(c == 0), stop=(c == NCH - 1)),
                 inc=(s_te if s == NS - 1 else None))
    p.wait("vector", s_te, p.count(s_te))
    p.op("vector", lambda e: e.tensor_scalar_mul(rowm, ps1[0:1, :], 1.0 / D), inc=s_dve)
    p.op("vector", lambda e: e.tensor_scalar_mul(rowq, ps1[32:33, :], 1.0 / D), inc=s_dve)
    p.wait("scalar", s_dve, p.count(s_dve))
    p.op("scalar", lambda e: e.activation(rowt, rowm, AF.Square), inc=s_act)
    p.wait("vector", s_act, p.count(s_act))
    p.op("vector", lambda e: e.tensor_sub(rowq, rowq, rowt), inc=s_dve)
    p.wait("scalar", s_dve, p.count(s_dve))
    p.op("scalar", lambda e: e.activation(rowt, rowq, AF.Sqrt, bias=epst[:]), inc=s_act)
    p.wait("vector", s_act, p.count(s_act))
    p.op("vector", lambda e: e.reciprocal(rowr, rowt), inc=s_dve)
    fin_done = p.count(s_dve)
    p.wait("tensor", s_dve, fin_done)
    for s in range(NS):
        p.op("tensor", lambda e, s=s: e.matmul(ps2[:, s * 512:(s + 1) * 512], ones_f[:],
                                               rowr[:, s * 512:(s + 1) * 512],
                                               start=True, stop=True),
             inc=(s_te if s == NS - 1 else None))
    for s in range(NS):
        p.op("tensor", lambda e, s=s: e.matmul(ps1[:, s * 512:(s + 1) * 512], ones_f[:],
                                               rowm[:, s * 512:(s + 1) * 512],
                                               start=True, stop=True),
             inc=(s_te if s == NS - 1 else None))
    bc_done = p.count(s_te)
    p.wait("vector", s_te, bc_done)
    if bc_sb is not None:
        bcm, bcr = bc_sb[:, 0:T], bc_sb[:, T:2 * T]
        p.op("vector", lambda e, bcm=bcm: e.tensor_copy(bcm, ps1[:]), inc=s_dve)
        p.op("vector", lambda e, bcr=bcr: e.tensor_copy(bcr, ps2[:]), inc=s_dve)
    else:
        bcm, bcr = ps1[:], ps2[:]
    for c in range(NCH):
        p.op("vector", lambda e, c=c, bcm=bcm: e.tensor_sub(xn_ch[c], x_ch[c], bcm), inc=s_dve)
        p.op("vector", lambda e, c=c, bcr=bcr: e.tensor_mul(xn_ch[c], xn_ch[c], bcr), inc=s_dve)
        if lnw is not None:
            p.op("vector", lambda e, c=c: e.tensor_scalar(xn_ch[c], xn_ch[c],
                                                          lnw[c], lnb[c],
                                                          ALU.mult, ALU.add), inc=s_dve)
    return p.count(s_dve)


def build_l1():
    nc = bass.Bass("TRN2", target_bir_lowering=False, debug=False)
    p = Prog(nc)
    x_d = nc.dram_tensor("x_fm", [D, T], F32, kind="ExternalInput")
    wgT_d = nc.dram_tensor("wgT", [D, HID0], I32, kind="ExternalInput")
    wuT_d = nc.dram_tensor("wuT", [D, HID0], I32, kind="ExternalInput")
    wdT_d = nc.dram_tensor("wdT", [HID0, D], I32, kind="ExternalInput")
    xn_d = nc.dram_tensor("xn_fm", [D, T], BF16, kind="ExternalOutput")
    acc_d = nc.dram_tensor("acc_fm", [D, T], F32, kind="ExternalOutput")

    pbig1 = p.ps("pbig1", [128, T], F32)
    pbig2 = p.ps("pbig2", [128, T], F32)
    x_sb = p.sb("x_sb", [128, NCH * T], F32)
    xn_sb = p.sb("xn_sb", [128, NCH * T], BF16)
    x_ch = [x_sb[:, c * T:(c + 1) * T] for c in range(NCH)]
    xn_ch = [xn_sb[:, c * T:(c + 1) * T] for c in range(NCH)]

    for c in range(NCH):
        p.dma("sync", lambda e, c=c: e.dma_start(x_ch[c], x_d.ap()[c * 128:(c + 1) * 128, :]),
              inc="ld_x")
    lnsc = make_ln_scratch(p)
    ln_done = emit_ln_fm(p, "ln0", x_ch, xn_ch, pbig1, pbig2, lnsc, x_ready=("ld_x", 8 * 16))

    wg_sb = [p.sb(f"wg{i}", [128, NCH * 128], BF16) for i in range(2)]
    wu_sb = [p.sb(f"wu{i}", [128, NCH * 128], BF16) for i in range(2)]
    wd_sb = [p.sb(f"wd{i}", [128, NH0 * 128], BF16) for i in range(2)]
    h_sb = p.sb("h_sb", [128, NH0 * 512], BF16)
    g_scr = p.sb("g_scr", [128, 512], BF16)
    ps_g = pbig1[:, 0:512]
    ps_u = pbig1[:, 512:1024]
    ps_d = [pbig1[:, 1024:1536], pbig1[:, 1536:2048]]

    n_wdma = 0

    for st in range(4):
        tq = slice(st * 512, (st + 1) * 512)
        for hc in range(NH0):
            buf = hc % 2
            src_g = bass.AP(wgT_d, hc * 128, [[HID0, 128], [128 * HID0, 8], [1, 128]])
            src_u = bass.AP(wuT_d, hc * 128, [[HID0, 128], [128 * HID0, 8], [1, 128]])
            dst_g = wg_sb[buf][:].rearrange("p (c f) -> p c f", c=8)
            dst_u = wu_sb[buf][:].rearrange("p (c f) -> p c f", c=8)
            it = st * NH0 + hc
            if it >= 2:
                p.wait("gpsimd", "te_gu", it - 1)
            p.dma("gpsimd", lambda e, s=src_g, d=dst_g: e.dma_start(d, s), inc=f"wgu{buf}")
            p.dma("gpsimd", lambda e, s=src_u, d=dst_u: e.dma_start(d, s), inc=f"wgu{buf}")
            p.wait("tensor", f"wgu{buf}", 32 * (it // 2 + 1))
            if st == 0 and hc == 0:
                p.wait("tensor", "ln0_dve", ln_done)
            if st * NH0 + hc >= 1:
                p.wait("tensor", "evac_gu", 2 * (st * NH0 + hc - 1) + 2)
            for dc in range(NCH):
                p.op("tensor", lambda e, buf=buf, dc=dc, tq=tq: e.matmul(
                    ps_g, wg_sb[buf][:, dc * 128:(dc + 1) * 128],
                    xn_ch[dc][:, tq], start=(dc == 0), stop=(dc == NCH - 1)))
                p.op("tensor", lambda e, buf=buf, dc=dc, tq=tq: e.matmul(
                    ps_u, wu_sb[buf][:, dc * 128:(dc + 1) * 128],
                    xn_ch[dc][:, tq], start=(dc == 0), stop=(dc == NCH - 1)),
                    inc=("te_gu" if dc == NCH - 1 else None))
            p.wait("scalar", "te_gu", p.count("te_gu"))
            if st >= 1:
                p.wait("scalar", "te_d", st * NCH)
            if st * NH0 + hc >= 1:
                p.wait("scalar", "dve_h", st * NH0 + hc)
            p.op("scalar", lambda e: e.activation(g_scr[:], ps_g[:], AF.Silu, scale=QSC),
                 inc="evac_gu")
            p.op("scalar", lambda e, hc=hc: e.activation(
                h_sb[:, hc * 512:(hc + 1) * 512], ps_u[:], AF.Copy, scale=QSC),
                inc="evac_gu")
            p.wait("vector", "evac_gu", p.count("evac_gu"))
            p.op("vector", lambda e, hc=hc: e.tensor_mul(
                h_sb[:, hc * 512:(hc + 1) * 512], g_scr[:],
                h_sb[:, hc * 512:(hc + 1) * 512]), inc="dve_h")
        for dc in range(NCH):
            buf = dc % 2
            src_d = bass.AP(wdT_d, dc * 128, [[D, 128], [128 * D, NH0], [1, 128]])
            dst_d = wd_sb[buf][:].rearrange("p (c f) -> p c f", c=NH0)
            it = st * NCH + dc
            if it >= 2:
                p.wait("gpsimd", "te_d", it - 1)
            p.dma("gpsimd", lambda e, s=src_d, d=dst_d: e.dma_start(d, s), inc=f"wdd{buf}")
            p.wait("tensor", f"wdd{buf}", 16 * (it // 2 + 1))
            p.wait("tensor", "dve_h", p.count("dve_h"))
            if st * NCH + dc >= 2:
                p.wait("tensor", "dve_acc", (st * NCH + dc - 1))
            for hc in range(NH0):
                p.op("tensor", lambda e, buf=buf, hc=hc, dc=dc: e.matmul(
                    ps_d[dc % 2], wd_sb[buf][:, hc * 128:(hc + 1) * 128],
                    h_sb[:, hc * 512:(hc + 1) * 512],
                    start=(hc == 0), stop=(hc == NH0 - 1)),
                    inc=("te_d" if hc == NH0 - 1 else None))
            p.wait("vector", "te_d", p.count("te_d"))
            p.op("vector", lambda e, dc=dc, st=st: e.scalar_tensor_tensor(
                x_ch[dc][:, st * 512:(st + 1) * 512], ps_d[dc % 2], QSC,
                x_ch[dc][:, st * 512:(st + 1) * 512], ALU.mult, ALU.add),
                inc="dve_acc")

    p.wait("sync", "ln0_dve", ln_done)
    for c in range(NCH):
        p.dma("sync", lambda e, c=c: e.dma_start(xn_d.ap()[c * 128:(c + 1) * 128, :], xn_ch[c]),
              inc="st_xn")
    p.wait("sync", "dve_acc", p.count("dve_acc"))
    for c in range(NCH):
        p.dma("sync", lambda e, c=c: e.dma_start(acc_d.ap()[c * 128:(c + 1) * 128, :], x_ch[c]),
              inc="st_acc")
    p.finish_and_emit()
    return nc


def _fm(a):
    return np.ascontiguousarray(a.T)


def kernel_unfused(**inputs):
    x = np.asarray(inputs["L0_attn_x"])
    wg = np.asarray(inputs["L0_ffn_w_gate"])
    wu = np.asarray(inputs["L0_ffn_w_up"])
    wd = np.asarray(inputs["L0_ffn_w_down"])
    log_dt = np.asarray(inputs["L0_attn_log_dt"])
    Bm = np.asarray(inputs["L0_attn_B"]); Cm = np.asarray(inputs["L0_attn_C"])
    Dv = np.asarray(inputs["L0_attn_D"])
    lnw = np.asarray(inputs["L1_attn_ln_w"]); lnb = np.asarray(inputs["L1_attn_ln_b"])
    ipw = np.asarray(inputs["L1_attn_in_proj_w"]); ipb = np.asarray(inputs["L1_attn_in_proj_b"])
    opw = np.asarray(inputs["L1_attn_out_proj_w"]); opb = np.asarray(inputs["L1_attn_out_proj_b"])
    w0 = np.asarray(inputs["L1_attn_w0"]); w1 = np.asarray(inputs["L1_attn_w1"])
    w2 = np.asarray(inputs["L1_attn_w2"])
    gw = np.asarray(inputs["L1_ffn_gate_w"]); bal = np.asarray(inputs["L1_ffn_balance_bias"])
    mw1 = np.asarray(inputs["L1_ffn_w1"]); mw2 = np.asarray(inputs["L1_ffn_w2"])

    xt = x.reshape(16384, D)
    shards = [xt[i * T:(i + 1) * T] for i in range(8)]
    cores = list(range(8))

    wgT = np.ascontiguousarray(wg.T); wuT = np.ascontiguousarray(wu.T)
    wdT = np.ascontiguousarray(wd.T)
    nc1 = build_l1()
    res = run_bass_kernel_spmd(nc1, [
        {"x_fm": _fm(s), "wgT": wgT, "wuT": wuT, "wdT": wdT} for s in shards],
        core_ids=cores)
    xn_fm = [np.asarray(r["xn_fm"]) for r in res.results]
    acc_fm = [np.asarray(r["acc_fm"]) for r in res.results]

    xn_tok = [np.concatenate([xn_fm[b * 4 + j].T for j in range(4)], axis=0)
              for b in range(2)]
    A = (-0.5 * np.arange(1, 65)).astype(np.float32)
    A_bc = np.tile(A, (128, 1))
    iA_bc = np.tile((1.0 / A).astype(np.float32), (128, 1))
    ident = np.eye(128, dtype=np.float32)
    nc2 = build_l2()
    in2 = []
    for c in range(8):
        sl = slice(c * 128, (c + 1) * 128)
        in2.append({
            "xn0": np.ascontiguousarray(xn_tok[0][:, sl]),
            "xn1": np.ascontiguousarray(xn_tok[1][:, sl]),
            "log_dt": log_dt[sl].reshape(128, 1), "Bmat": Bm[sl], "Cmat": Cm[sl],
            "Dvec": Dv[sl].reshape(128, 1), "A_bc": A_bc, "invA_bc": iA_bc,
            "ident_f": ident})
    res = run_bass_kernel_spmd(nc2, in2, core_ids=cores)
    yssm = [np.concatenate([np.asarray(res.results[c][f"y{b}"]) for c in range(8)], axis=1)
            for b in range(2)]

    ipT = np.ascontiguousarray(ipw.T)
    nc3 = build_l3a()
    in3 = []
    for c in range(8):
        b = c // 4
        sl = slice((c % 4) * T, (c % 4 + 1) * T)
        in3.append({
            "acc_fm": acc_fm[c],
            "yssm_fm": np.ascontiguousarray(yssm[b][sl].T),
            "in_projT": ipT, "in_proj_b": ipb.reshape(-1, 1),
            "ln_w": lnw.reshape(-1, 1), "ln_b": lnb.reshape(-1, 1)})
    res = run_bass_kernel_spmd(nc3, in3, core_ids=cores)
    x1_fm = [np.asarray(r["x1_fm"]) for r in res.results]
    xn1_fm = [np.asarray(r["xn1_fm"]) for r in res.results]
    xn2_fm = [np.asarray(r["xn2_fm"]) for r in res.results]
    q_fm = [np.asarray(r["q_fm"]) for r in res.results]
    k_fm = [np.asarray(r["k_fm"]) for r in res.results]
    v_tok = [np.asarray(r["v_tok"]) for r in res.results]

    K_full = [np.ascontiguousarray(np.concatenate([k_fm[b * 4 + j] for j in range(4)], axis=1))
              for b in range(2)]
    V_full = [np.ascontiguousarray(np.concatenate([v_tok[b * 4 + j] for j in range(4)], axis=0))
              for b in range(2)]
    opT = np.ascontiguousarray(opw.T)
    nc4 = build_l3b1()
    in4 = []
    for c in range(8):
        b = c // 4
        in4.append({
            "x1_fm": x1_fm[c], "q_fm": q_fm[c],
            "K_full": K_full[b], "V_full": V_full[b],
            "out_projT": opT, "out_proj_b": opb.reshape(-1, 1),
            "v_bias": ipb[2048:].reshape(-1, 1)})
    res = run_bass_kernel_spmd(nc4, in4, core_ids=cores)
    y1_fm = [np.asarray(r["y1_fm"]) for r in res.results]

    w0T = np.ascontiguousarray(np.transpose(w0, (0, 2, 1))).reshape(4 * 256, 256)
    w2T = np.ascontiguousarray(np.transpose(w2, (0, 2, 1))).reshape(4 * 256, 256)
    w1l = np.ascontiguousarray(w1).reshape(4 * 256, 256)
    w1T = np.ascontiguousarray(np.transpose(w1, (0, 2, 1))).reshape(4 * 256, 256)
    gwT = np.ascontiguousarray(gw.T)
    mw1T = np.ascontiguousarray(np.transpose(mw1, (0, 2, 1))).reshape(8 * 1024, 2048)
    mw2T = np.ascontiguousarray(np.transpose(mw2, (0, 2, 1))).reshape(8 * 2048, 1024)
    identb = np.eye(128, dtype=BF16NP)
    nc5 = build_l3b2()
    in5 = []
    for c in range(8):
        in5.append({
            "y1_fm": y1_fm[c], "xn1_fm": xn1_fm[c], "xn2_fm": xn2_fm[c],
            "w0T": w0T, "w2T": w2T, "w1l": w1l, "w1T": w1T,
            "gate_wT": gwT, "bal_b": bal.reshape(1, 8),
            "moew1T": mw1T, "moew2T": mw2T, "ident_b": identb})
    res = run_bass_kernel_spmd(nc5, in5, core_ids=cores)
    out = np.concatenate([np.asarray(r["out_fm"]).T for r in res.results], axis=0)
    return np.ascontiguousarray(out.reshape(2, 8192, D).astype(np.float32))


def build_l2(dbg=False):
    nc = bass.Bass("TRN2", target_bir_lowering=False, debug=False)
    p = Prog(nc)
    xn_d = [nc.dram_tensor(f"xn{b}", [8192, 128], BF16, kind="ExternalInput") for b in range(2)]
    ldt_d = nc.dram_tensor("log_dt", [128, 1], F32, kind="ExternalInput")
    B_d = nc.dram_tensor("Bmat", [128, 64], F32, kind="ExternalInput")
    C_d = nc.dram_tensor("Cmat", [128, 64], F32, kind="ExternalInput")
    Dv_d = nc.dram_tensor("Dvec", [128, 1], F32, kind="ExternalInput")
    A_d = nc.dram_tensor("A_bc", [128, 64], F32, kind="ExternalInput")
    iA_d = nc.dram_tensor("invA_bc", [128, 64], F32, kind="ExternalInput")
    id_d = nc.dram_tensor("ident_f", [128, 128], F32, kind="ExternalInput")
    y_d = [nc.dram_tensor(f"y{b}", [8192, 128], BF16, kind="ExternalOutput") for b in range(2)]
    if dbg:
        dbgK_d = nc.dram_tensor("dbgK", [128, 256], F32, kind="ExternalOutput")
        dbgX_d = nc.dram_tensor("dbgX", [128, 8192], BF16, kind="ExternalOutput")
        dbgP_d = nc.dram_tensor("dbgP", [64, 8192], BF16, kind="ExternalOutput")
        dbgS_d = nc.dram_tensor("dbgS", [64, 128 * 65], BF16, kind="ExternalOutput")
        dbgY_d = nc.dram_tensor("dbgY", [128, 8192], BF16, kind="ExternalOutput")
        dbgU_d = nc.dram_tensor("dbgU", [128, 8192], BF16, kind="ExternalOutput")
        dbgV_d = nc.dram_tensor("dbgV", [128, 64 * 129], BF16, kind="ExternalOutput")
        dbgQ_d = nc.dram_tensor("dbgQ", [64, 128], F32, kind="ExternalOutput")
    krep_d = nc.dram_tensor("krep", [128, 128 * 256], BF16, kind="Internal")
    urev_d = nc.dram_tensor("urev", [128, 8192], BF16, kind="Internal")
    v_d = nc.dram_tensor("vdram", [128, 64 * 129], BF16, kind="Internal")

    ldt = p.sb("ldt", [128, 1], F32)
    Bs = p.sb("Bs", [128, 64], F32)
    Cs = p.sb("Cs", [128, 64], F32)
    Ds = p.sb("Ds", [128, 1], F32)
    Abc = p.sb("Abc", [128, 64], F32)
    iAbc = p.sb("iAbc", [128, 64], F32)
    idf = p.sb("idf", [128, 128], F32)
    dt = p.sb("dt", [128, 1], F32)
    lam = p.sb("lam", [128, 64], F32)
    ones64 = p.sb("ones64", [128, 64], F32)
    CB = p.sb("CB", [128, 64], F32)
    ucur = p.sb("ucur", [128, 64], F32)
    tmp64 = p.sb("tmp64", [128, 64], F32)
    urev_sb = p.sb("urev_sb", [128, 8192], BF16)
    v_sb = p.sb("v_sb", [128, 64 * 129], BF16)
    K_sb = p.sb("K_sb", [128, 256], F32)
    K_bf = p.sb("K_bf", [128, 256], BF16)
    lamQT = p.sb("lamQT", [64, 128], F32)
    xraw = [p.sb(f"xraw{b}", [128, 8192], BF16) for b in range(2)]
    y_sb = [p.sb(f"ysb{b}", [128, 8192], BF16) for b in range(2)]
    P_all = [p.sb(f"Pall{b}", [64, 8192], BF16) for b in range(2)]
    S_shift = p.sb("S_shift", [64, 128 * 65], F32)
    S_bf = p.sb("S_bf", [64, 128 * 65], BF16)
    T8 = [p.sb(f"T8_{i}", [128, 8 * 128], BF16) for i in range(2)]
    W8 = [p.sb(f"W8_{i}", [128, 8 * 64], BF16) for i in range(2)]
    M8 = [p.sb(f"M8_{i}", [64, 8 * 128], BF16) for i in range(2)]
    pY = [p.ps(f"pY{b}", [128, 512], F32) for b in range(2)]
    pP = [p.ps(f"pP{b}", [64, 512], F32) for b in range(2)]
    pT = p.ps("pT", [64, 128], F32)

    for t, d in ((ldt, ldt_d), (Bs, B_d), (Cs, C_d), (Ds, Dv_d), (Abc, A_d),
                 (iAbc, iA_d), (idf, id_d)):
        p.dma("sync", lambda e, t=t, d=d: e.dma_start(t[:], d.ap()), inc="ld")
    p.wait("scalar", "ld", p.count("ld"))
    p.op("scalar", lambda e: e.activation(dt[:], ldt[:], AF.Exp), inc="act")
    p.wait("vector", "act", p.count("act"))
    p.op("vector", lambda e: e.tensor_scalar_mul(lam[:], Abc[:], dt[:, 0:1]))
    p.wait("scalar", "vchain", 0)
    p.op("vector", lambda e: e.memset(ones64[:], 1.0))
    p.op("vector", lambda e: e.tensor_copy(tmp64[:], lam[:]), inc="dve")
    p.wait("scalar", "dve", p.count("dve"))
    p.op("scalar", lambda e: e.activation(lam[:], tmp64[:], AF.Exp), inc="act")
    p.wait("vector", "act", p.count("act"))
    p.op("vector", lambda e: e.tensor_sub(CB[:], lam[:], ones64[:]))
    p.op("vector", lambda e: e.tensor_mul(CB[:], CB[:], iAbc[:]))
    p.op("vector", lambda e: e.tensor_mul(CB[:], CB[:], Bs[:]))
    p.op("vector", lambda e: e.tensor_mul(CB[:], CB[:], Cs[:]))
    p.op("vector", lambda e: e.memset(ucur[:], 1.0))
    p.op("vector", lambda e: e.memset(K_sb[:], 0.0))
    v_r = v_sb[:].rearrange("p (n j) -> p n j", j=129)
    for j in range(129):
        if j <= 127:
            p.op("vector", lambda e, j=j: e.tensor_copy(
                urev_sb[:, (127 - j) * 64:(128 - j) * 64], ucur[:]))
        p.op("vector", lambda e: e.tensor_mul(tmp64[:], ucur[:], CB[:]))
        p.op("vector", lambda e, j=j: e.tensor_copy(v_r[:, :, j], tmp64[:]))
        p.op("vector", lambda e, j=j: e.reduce_sum(K_sb[:, 127 + j:128 + j], tmp64[:], AX.X))
        if j <= 127:
            p.op("vector", lambda e: e.tensor_mul(ucur[:], ucur[:], lam[:]))
    p.op("vector", lambda e: e.tensor_add(K_sb[:, 127:128], K_sb[:, 127:128], Ds[:]))
    p.op("vector", lambda e: e.tensor_copy(K_bf[:], K_sb[:]), inc="chain")
    chain_done = p.count("chain")

    p.wait("gpsimd", "chain", chain_done)
    p.dma("gpsimd", lambda e: e.dma_start(urev_d.ap(), urev_sb[:]), inc="guv")
    p.dma("gpsimd", lambda e: e.dma_start(v_d.ap(), v_sb[:]), inc="guv")
    p.wait("sync", "chain", chain_done)
    for k in range(128):
        p.dma("sync", lambda e, k=k: e.dma_start(
            bass.AP(krep_d, k * 256, [[256 * 128, 128], [1, 256]]), K_bf[:]), inc="krep")
    p.wait("tensor", "chain", chain_done)
    p.op("tensor", lambda e: e.transpose(pT[:], ucur[:, 0:64], idf[:]), inc="te_t")
    p.wait("vector", "te_t", p.count("te_t"))
    p.op("vector", lambda e: e.tensor_copy(lamQT[:], pT[:]), inc="lamqt")

    for b in range(2):
        for piece in range(8):
            src = bass.AP(xn_d[b], piece * 8 * 16384,
                          [[128, 128], [16384, 8], [1, 128]])
            dst = xraw[b][:].rearrange("p (c d) -> p c d", c=64)[:, piece * 8:(piece + 1) * 8, :]
            p.dma("scalar", lambda e, s=src, d=dst: e.dma_start(d, s), inc=f"ldx{b}")

    if dbg:
        p.wait("sync", "ldx0", 8 * 16)
        p.dma("sync", lambda e: e.dma_start(dbgX_d.ap(), xraw[0][:]), inc="dbg")

    def rhs_x(b, ch):
        return bass.AP(xraw[b].handle if hasattr(xraw[b], "handle") else xraw[b],
                       ch, [[8192, 128], [128, 64]])

    nblk = 16
    for blk in range(nblk):
        buf = blk % 2
        p.wait("sync", "krep", 128 * 16)
        if blk >= 2:
            p.wait("sync", "te1", blk - 1)
        p.dma("sync", lambda e, blk=blk, buf=buf: e.dma_start(
            T8[buf][:], bass.AP(krep_d, blk * 8 * 32768 + 127,
                                [[255, 128], [32768, 8], [1, 128]])), inc=f"tw{buf}")
        p.wait("sync", "guv", 32)
        p.dma("sync", lambda e, blk=blk, buf=buf: e.dma_start(
            W8[buf][:], bass.AP(urev_d, blk * 8 * 8192,
                                [[64, 128], [8192, 8], [1, 64]])), inc=f"tw{buf}")
        p.wait("tensor", f"tw{buf}", 32 * (blk // 2 + 1))
        for b in range(2):
            p.wait("tensor", f"ldx{b}", 8 * 16)
            if blk >= 1:
                p.wait("tensor", "ev1", 4 * blk)
            for i in range(8):
                ch = blk * 8 + i
                p.op("tensor", lambda e, b=b, i=i, buf=buf, ch=ch: e.matmul(
                    pY[b][:, i * 64:(i + 1) * 64], T8[buf][:, i * 128:(i + 1) * 128],
                    rhs_x(b, ch), start=True, stop=True))
                p.op("tensor", lambda e, b=b, i=i, buf=buf, ch=ch: e.matmul(
                    pP[b][0:64, i * 64:(i + 1) * 64], W8[buf][:, i * 64:(i + 1) * 64],
                    rhs_x(b, ch), start=True, stop=True),
                    inc=("te1" if (b == 1 and i == 7) else None))
        p.wait("vector", "te1", blk + 1)
        for b in range(2):
            p.op("vector", lambda e, b=b, blk=blk: e.tensor_copy(
                y_sb[b][:, blk * 512:(blk + 1) * 512], pY[b][:]), inc="ev1")
            p.op("vector", lambda e, b=b, blk=blk: e.tensor_copy(
                P_all[b][0:64, blk * 512:(blk + 1) * 512], pP[b][0:64, :]), inc="ev1")

    for b in range(2):
        p.wait("vector", "lamqt", 1)
        p.op("vector", lambda e: e.memset(S_shift[:], 0.0))
        for d in range(128):
            p.op("vector", lambda e, b=b, d=d: e.tensor_tensor_scan(
                S_shift[0:64, d * 65 + 1:d * 65 + 65],
                lamQT[:, d:d + 1].broadcast_to((64, 64)),
                P_all[b][0:64, d * 64:(d + 1) * 64],
                0.0, ALU.mult, ALU.add))
        p.op("vector", lambda e: e.tensor_copy(S_bf[:], S_shift[:]), inc=f"scan{b}")
        for blk in range(nblk):
            buf = blk % 2
            if blk == 0:
                p.wait("scalar", "guv", 32)
            if blk >= 2:
                p.wait("scalar", f"teM{b}", blk - 1)
            p.dma("scalar", lambda e, blk=blk, buf=buf: e.dma_start(
                M8[buf][:], bass.AP(v_d, blk * 8 * 8256 + 1,
                                    [[129, 64], [8256, 8], [1, 128]])), inc=f"md{b}{buf}")
            p.wait("tensor", f"md{b}{buf}", 16 * (blk // 2 + 1))
            p.wait("tensor", f"scan{b}", 1)
            if blk >= 1:
                p.wait("tensor", f"evM{b}", blk)
            for i in range(8):
                ch = blk * 8 + i
                p.op("tensor", lambda e, b=b, i=i, buf=buf, ch=ch: e.matmul(
                    pY[b][:, i * 64:(i + 1) * 64], M8[buf][:, i * 128:(i + 1) * 128],
                    S_bf[0:64, ch * 65:ch * 65 + 64], start=True, stop=True),
                    inc=(f"teM{b}" if i == 7 else None))
            p.wait("vector", f"teM{b}", blk + 1)
            p.op("vector", lambda e, b=b, blk=blk: e.tensor_add(
                y_sb[b][:, blk * 512:(blk + 1) * 512],
                y_sb[b][:, blk * 512:(blk + 1) * 512], pY[b][:]), inc=f"evM{b}")
        out_r = xraw[b][:].rearrange("p (c d) -> p c d", c=64)
        in_r = bass.AP(y_sb[b].handle if hasattr(y_sb[b], "handle") else y_sb[b],
                       0, [[8192, 128], [1, 64], [64, 128]])
        p.op("vector", lambda e, o=out_r, i=in_r: e.tensor_copy(o, i), inc=f"rs{b}")
        p.wait("sync", f"rs{b}", 1)
        dst = bass.AP(y_d[b], 0, [[128, 128], [128 * 128, 64], [1, 128]])
        p.dma("sync", lambda e, b=b, dst=dst: e.dma_start(
            dst, xraw[b][:].rearrange("p (c d) -> p c d", c=64)), inc="sty")

    if dbg:
        p.wait("sync", "chain", chain_done)
        p.dma("sync", lambda e: e.dma_start(dbgK_d.ap(), K_sb[:]), inc="dbg")
        p.dma("sync", lambda e: e.dma_start(dbgU_d.ap(), urev_sb[:]), inc="dbg")
        p.dma("sync", lambda e: e.dma_start(dbgV_d.ap(), v_sb[:]), inc="dbg")
        p.wait("sync", "lamqt", 1)
        p.dma("sync", lambda e: e.dma_start(dbgQ_d.ap(), lamQT[:]), inc="dbg")
        p.wait("sync", "scan0", 1)
        p.dma("sync", lambda e: e.dma_start(dbgP_d.ap(), P_all[0][:]), inc="dbg")
        p.dma("sync", lambda e: e.dma_start(dbgS_d.ap(), S_bf[:]), inc="dbg")
        p.wait("sync", "evM0", 16)
        p.dma("sync", lambda e: e.dma_start(dbgY_d.ap(), y_sb[0][:]), inc="dbg")
    p.finish_and_emit()
    return nc


def build_l3a():
    nc = bass.Bass("TRN2", target_bir_lowering=False, debug=False)
    p = Prog(nc)
    acc_d = nc.dram_tensor("acc_fm", [D, T], F32, kind="ExternalInput")
    ys_d = nc.dram_tensor("yssm_fm", [D, T], BF16, kind="ExternalInput")
    wT_d = nc.dram_tensor("in_projT", [D, 3 * D], F32, kind="ExternalInput")
    ib_d = nc.dram_tensor("in_proj_b", [3 * D, 1], F32, kind="ExternalInput")
    lw_d = nc.dram_tensor("ln_w", [D, 1], F32, kind="ExternalInput")
    lb_d = nc.dram_tensor("ln_b", [D, 1], F32, kind="ExternalInput")
    x1_d = nc.dram_tensor("x1_fm", [D, T], F32, kind="ExternalOutput")
    xn1_d = nc.dram_tensor("xn1_fm", [D, T], BF16, kind="ExternalOutput")
    xn2_d = nc.dram_tensor("xn2_fm", [D, T], BF16, kind="ExternalOutput")
    q_d = nc.dram_tensor("q_fm", [D, T], BF16, kind="ExternalOutput")
    k_d = nc.dram_tensor("k_fm", [D, T], BF16, kind="ExternalOutput")
    v_d = nc.dram_tensor("v_tok", [T, D], BF16, kind="ExternalOutput")

    x1_sb = p.sb("x1_sb", [128, NCH * T], F32)
    ys_st = [p.sb(f"ys_st{i}", [128, T], BF16) for i in range(4)]
    xn1_sb = p.sb("xn1_sb", [128, NCH * T], BF16)
    xn2_sb = p.sb("xn2_sb", [128, NCH * T], BF16)
    x1_ch = [x1_sb[:, c * T:(c + 1) * T] for c in range(NCH)]
    xn1_ch = [xn1_sb[:, c * T:(c + 1) * T] for c in range(NCH)]
    xn2_ch = [xn2_sb[:, c * T:(c + 1) * T] for c in range(NCH)]
    lw_sb = p.sb("lw_sb", [128, NCH], F32)
    lb_sb = p.sb("lb_sb", [128, NCH], F32)
    qb_sb = p.sb("qb_sb", [128, 16], F32)
    wqk = [p.sb(f"wqk{i}", [128, NCH * 128], BF16) for i in range(2)]
    wv_sb = p.sb("wv_sb", [128, NCH * D], BF16)
    ev = [p.sb(f"ev{i}", [128, 1024], BF16) for i in range(2)]
    pbig1 = p.ps("pbig1", [128, T], F32)
    pbig2 = p.ps("pbig2", [128, T], F32)

    for c in range(NCH):
        p.dma("sync", lambda e, c=c: e.dma_start(x1_ch[c], acc_d.ap()[c * 128:(c + 1) * 128, :]), inc="ldx1")
    for c in range(NCH):
        if c >= 4:
            p.wait("sync", "addx", c - 3)
        p.dma("sync", lambda e, c=c: e.dma_start(ys_st[c % 4][:], ys_d.ap()[c * 128:(c + 1) * 128, :]), inc=f"ldys{c % 4}")
    for c in range(NCH):
        p.dma("sync", lambda e, c=c: e.dma_start(lw_sb[:, c:c + 1], lw_d.ap()[c * 128:(c + 1) * 128, :]), inc="ldb")
        p.dma("sync", lambda e, c=c: e.dma_start(lb_sb[:, c:c + 1], lb_d.ap()[c * 128:(c + 1) * 128, :]), inc="ldb")
    for c in range(16):
        p.dma("sync", lambda e, c=c: e.dma_start(qb_sb[:, c:c + 1], ib_d.ap()[c * 128:(c + 1) * 128, :]), inc="ldb")
    p.dma("gpsimd", lambda e: e.dma_start(
        wv_sb[:].rearrange("p (c f) -> p c f", c=NCH),
        bass.AP(wT_d, 2 * D, [[3 * D, 128], [128 * 3 * D, NCH], [1, D]])), inc="gwv")

    p.wait("vector", "ldx1", NCH * 16)
    for c in range(NCH):
        p.wait("vector", f"ldys{c % 4}", 16 * (c // 4 + 1))
        p.op("vector", lambda e, c=c: e.tensor_add(x1_ch[c], x1_ch[c], ys_st[c % 4][:]),
             inc="addx")
    lnsc = make_ln_scratch(p)
    ln1_done = emit_ln_fm(p, "ln1", x1_ch, xn1_ch, pbig1, pbig2, lnsc,
                          x_ready=("addx", NCH))
    lnw = [lw_sb[:, c:c + 1] for c in range(NCH)]
    lnb = [lb_sb[:, c:c + 1] for c in range(NCH)]
    p.wait("vector", "ldb", 32 * 16)
    ln2_done = emit_ln_fm(p, "ln2", xn1_ch, xn2_ch, pbig1, pbig2, lnsc,
                          lnw=lnw, lnb=lnb, x_ready=("ln1_dve", ln1_done))

    ps_qk = [pbig1[:, i * 512:(i + 1) * 512] for i in range(4)]
    nw = 0
    for oc in range(16):
        buf = oc % 2
        if oc >= 2:
            p.wait("gpsimd", "teqk", (oc - 1) * 4)
        p.dma("gpsimd", lambda e, oc=oc, buf=buf: e.dma_start(
            wqk[buf][:].rearrange("p (c f) -> p c f", c=NCH),
            bass.AP(wT_d, oc * 128, [[3 * D, 128], [128 * 3 * D, NCH], [1, 128]])), inc=f"gqk{buf}")
        p.wait("tensor", f"gqk{buf}", 16 * (oc // 2 + 1))
        p.wait("tensor", "ln2_dve", ln2_done)
        if oc >= 1:
            p.wait("tensor", "evqk", (oc - 1) * 4 + 4)
        for s in range(4):
            for dc in range(NCH):
                p.op("tensor", lambda e, oc=oc, s=s, dc=dc, buf=buf: e.matmul(
                    ps_qk[s], wqk[buf][:, dc * 128:(dc + 1) * 128],
                    xn2_ch[dc][:, s * 512:(s + 1) * 512],
                    start=(dc == 0), stop=(dc == NCH - 1)),
                    inc=("teqk" if dc == NCH - 1 else None))
        p.wait("vector", "teqk", p.count("teqk"))
        p.wait("vector", "ldb", 32 * 16)
        for s in range(4):
            nev = oc * 4 + s
            if nev >= 2:
                p.wait("vector", f"stq{nev % 2}", 16 * (nev // 2))
            p.op("vector", lambda e, oc=oc, s=s, buf=buf: e.tensor_scalar_add(
                ev[buf][:, (s % 2) * 512:(s % 2 + 1) * 512],
                ps_qk[s], qb_sb[:, oc:oc + 1]), inc="evqk")
            tgt = (q_d if oc < 8 else k_d)
            row = (oc % 8) * 128
            p.wait("sync", "evqk", p.count("evqk"))
            p.dma("sync", lambda e, tgt=tgt, row=row, s=s, buf=buf: e.dma_start(
                tgt.ap()[row:row + 128, s * 512:(s + 1) * 512],
                ev[buf][:, (s % 2) * 512:(s % 2 + 1) * 512]), inc=f"stq{(oc * 4 + s) % 2}")

    for tt in range(16):
        buf = tt % 2
        p.wait("tensor", "gwv", 16)
        if tt >= 1:
            p.wait("tensor", "evv", tt)
        for dc in range(NCH):
            for h in range(2):
                p.op("tensor", lambda e, tt=tt, dc=dc, h=h: e.matmul(
                    pbig2[:, h * 512:(h + 1) * 512],
                    xn2_ch[dc][:, tt * 128:(tt + 1) * 128],
                    wv_sb[:, dc * D + h * 512: dc * D + (h + 1) * 512],
                    start=(dc == 0), stop=(dc == NCH - 1)),
                    inc=("tev" if dc == NCH - 1 and h == 1 else None))
        p.wait("vector", "tev", p.count("tev"))
        if tt >= 2:
            p.wait("vector", f"sv{buf}", 16 * (tt // 2))
        p.op("vector", lambda e, buf=buf: e.tensor_copy(ev[buf][:], pbig2[:, 0:1024]), inc="evv")
        p.wait("sync", "evv", p.count("evv"))
        p.dma("sync", lambda e, tt=tt, buf=buf: e.dma_start(
            v_d.ap()[tt * 128:(tt + 1) * 128, :], ev[buf][:]), inc=f"sv{buf}")

    p.wait("sync", "ln1_dve", ln1_done)
    for c in range(NCH):
        p.dma("sync", lambda e, c=c: e.dma_start(x1_d.ap()[c * 128:(c + 1) * 128, :], x1_ch[c]), inc="stx")
        p.dma("sync", lambda e, c=c: e.dma_start(xn1_d.ap()[c * 128:(c + 1) * 128, :], xn1_ch[c]), inc="stx")
    p.wait("sync", "ln2_dve", ln2_done)
    for c in range(NCH):
        p.dma("sync", lambda e, c=c: e.dma_start(xn2_d.ap()[c * 128:(c + 1) * 128, :], xn2_ch[c]), inc="stx")
    p.finish_and_emit()
    return nc


def emit_p4(p, io, pp):
    nc = p.nc
    x1_d = io["x1"]; q_d = io["q"]; K_d = io["K"]; V_d = io["V"]
    opT_d = io["out_projT"]; opb_d = io["out_proj_b"]; vb_d = io["v_bias"]
    y1_d = io["y1"]
    psA, psB, psC, psD, psE, psF = (pp["psA"], pp["psB"], pp["psC"],
                                     pp["psD"], pp["psE"], pp["psF"])
    pS = [psA, psB]
    pV = [[psE, psF], [psC[0:65, :], psD[0:65, :]]]

    x1_sb = p.sb("x1_sb", [128, NCH * T], F32)
    x1_ch = [x1_sb[:, c * T:(c + 1) * T] for c in range(NCH)]
    q_sb = p.sb("q_sb", [128, NCH * T], BF16)
    q_ch = [q_sb[:, c * T:(c + 1) * T] for c in range(NCH)]
    ao_sb = p.sb("ao_sb", [128, NCH * T], BF16)
    ao_ch = [ao_sb[:, c * T:(c + 1) * T] for c in range(NCH)]
    kpair = p.sb("kpair", [128, 8192], BF16)
    vsb = p.sb("vsb", [128, 64 * 130], BF16)
    pexp = [[p.sb(f"pexp{s}{i}", [128, 1024], BF16) for i in range(2)] for s in range(2)]
    pvacc = [[p.sb(f"pvacc{s}{h}", [65, 512], F32) for h in range(2)] for s in range(2)]
    opw = [p.sb(f"opw{i}", [128, NCH * 128], BF16) for i in range(2)]
    opb_sb = p.sb("opb_sb", [128, NCH], F32)
    vb_sb = p.sb("vb_sb", [128, NCH], F32)
    tmp_h = [p.sb(f"tmph{h}", [64, 512], F32) for h in range(2)]
    rd_sb = p.sb("rd_sb", [1, 512], F32)
    ones_f = p.sb("ones_f", [1, 128], F32)

    for c in range(NCH):
        p.dma("sync", lambda e, c=c: e.dma_start(x1_ch[c], x1_d.ap()[c * 128:(c + 1) * 128, :]), inc="ldx")
        p.dma("sync", lambda e, c=c: e.dma_start(q_ch[c], q_d.ap()[c * 128:(c + 1) * 128, :]), inc="ldq")
        p.dma("sync", lambda e, c=c: e.dma_start(opb_sb[:, c:c + 1], opb_d.ap()[c * 128:(c + 1) * 128, :]), inc="ldb")
        p.dma("sync", lambda e, c=c: e.dma_start(vb_sb[:, c:c + 1], vb_d.ap()[c * 128:(c + 1) * 128, :]), inc="ldb")
    p.op("vector", lambda e: e.memset(ones_f[:], 1.0), inc="ones")

    NKV = 64
    for hp in range(NCH):
        if hp >= 1:
            p.wait("sync", "pvev", 32 * hp)
        p.dma("sync", lambda e, hp=hp: e.dma_start(
            kpair[:], K_d.ap()[hp * 128:(hp + 1) * 128, :]), inc="kv_ld")
        for pc in range(4):
            for h in range(2):
                srcv = bass.AP(V_d, (pc * 16 * 128) * D + hp * 128 + h * 64,
                               [[D, 128], [128 * D, 16], [1, 64]])
                dstv = bass.AP(vsb, pc * 16 * 130 + h * 65,
                               [[8320, 128], [130, 16], [1, 64]])
                p.dma("sync", lambda e, s=srcv, d=dstv: e.dma_start(d, s), inc="kv_ld")
        p.wait("vector", "kv_ld", p.count("kv_ld"))
        p.op("vector", lambda e: e.memset(
            vsb[:].rearrange("p (kv f) -> p kv f", kv=128)[:, :, 64:65], 1.0), inc="vones")
        p.wait("tensor", "vones", p.count("vones"))
        p.wait("tensor", "kv_ld", p.count("kv_ld"))
        if hp == 0:
            p.wait("tensor", "ldq", 8 * 16)
        for sp in range(2):
        
            tqs = [slice((2 * sp + s) * 512, (2 * sp + s + 1) * 512) for s in range(2)]
            for kv in range(NKV + 1):
                for s in range(2):
                    if kv < NKV:
                        p.wait("tensor", f"aex{s}", p.count(f"aex{s}"))
                        for h in range(2):
                            bse = h * 64
                            p.op("tensor", lambda e, hp=hp, kv=kv, h=h, bse=bse, s=s, tq=tqs[s]: e.matmul(
                                pS[s][:, h * 512:(h + 1) * 512],
                                kpair[bse:bse + 64, kv * 128:(kv + 1) * 128],
                                q_ch[hp][bse:bse + 64, tq], start=True, stop=True),
                                inc=(f"tes{s}" if h == 1 else None))
                for s in range(2):
                    if kv < NKV:
                        p.wait("scalar", f"tes{s}", p.count(f"tes{s}"))
                        if kv >= 2:
                            p.wait("scalar", f"tepv{s}", p.count(f"tepv{s}"))
                        p.op("scalar", lambda e, s=s, kv=kv: e.activation(
                            pexp[s][kv % 2][:], pS[s][:], AF.Exp, scale=0.125), inc=f"aex{s}")
                for s in range(2):
                    if kv >= 1:
                        k2 = kv - 1
                        p.wait("tensor", f"aex{s}", p.count(f"aex{s}") - (1 if kv < NKV else 0))
                        if k2 % 16 == 0:
                            p.wait("tensor", f"pvev{s}x" if False else "pvev",
                                   p.count("pvev"))
                        for h in range(2):
                            p.op("tensor", lambda e, s=s, k2=k2, h=h: e.matmul(
                                pV[s][h][:], vsb[:, k2 * 130 + h * 65:k2 * 130 + (h + 1) * 65],
                                pexp[s][k2 % 2][:, h * 512:(h + 1) * 512],
                                start=(k2 % 16 == 0), stop=(k2 % 16 == 15)),
                                inc=(f"tepv{s}" if h == 1 else None))
                        if k2 % 16 == 15:
                            p.wait("vector", f"tepv{s}", p.count(f"tepv{s}"))
                            for h in range(2):
                                if k2 == 15:
                                    p.op("vector", lambda e, s=s, h=h: e.tensor_copy(
                                        pvacc[s][h][:], pV[s][h][:]), inc="pvev")
                                else:
                                    p.op("vector", lambda e, s=s, h=h: e.tensor_add(
                                        pvacc[s][h][:], pvacc[s][h][:], pV[s][h][:]), inc="pvev")
            p.wait("vector", "ones", 1)
            p.wait("vector", "ldb", 16 * 16)
            for s in range(2):
                for h in range(2):
                    p.op("vector", lambda e, s=s, h=h: e.reciprocal(
                        rd_sb[:], pvacc[s][h][64:65, :]), inc="fin")
                    p.wait("tensor", "fin", p.count("fin"))
                    p.op("tensor", lambda e, s=s: e.matmul(
                        pV[s][0][0:64, :], ones_f[:, 0:64], rd_sb[:],
                        start=True, stop=True), inc="finb")
                    p.wait("vector", "finb", p.count("finb"))
                    p.op("vector", lambda e, s=s, h=h: e.tensor_mul(
                        tmp_h[h][:], pvacc[s][h][0:64, :], pV[s][0][0:64, :]), inc="fin")
                    p.op("vector", lambda e, s=s, h=h, hp=hp, tq=tqs[s]: e.tensor_scalar_add(
                        ao_ch[hp][h * 64:h * 64 + 64, tq],
                        tmp_h[h][:], vb_sb[h * 64:h * 64 + 64, hp:hp + 1]), inc="fin")

    ao_done = p.count("fin")
    for oc in range(NCH):
        buf = oc % 2
        if oc >= 2:
            p.wait("gpsimd", "teop", oc - 1)
        p.dma("gpsimd", lambda e, oc=oc, buf=buf: e.dma_start(
            opw[buf][:].rearrange("p (c f) -> p c f", c=NCH),
            bass.AP(opT_d, oc * 128, [[D, 128], [128 * D, NCH], [1, 128]])), inc=f"gop{buf}")
        p.wait("tensor", f"gop{buf}", 16 * (oc // 2 + 1))
        p.wait("tensor", "fin", ao_done)
        if oc >= 1:
            p.wait("tensor", "evop", 4 * oc)
        for s in range(4):
            ps_s = (psA if s < 2 else psB)[:, (s % 2) * 512:(s % 2 + 1) * 512]
            for dc in range(NCH):
                p.op("tensor", lambda e, oc=oc, s=s, dc=dc, buf=buf, ps_s=ps_s: e.matmul(
                    ps_s, opw[buf][:, dc * 128:(dc + 1) * 128],
                    ao_ch[dc][:, s * 512:(s + 1) * 512],
                    start=(dc == 0), stop=(dc == NCH - 1)),
                    inc=("teop" if (dc == NCH - 1 and s == 3) else None))
        p.wait("vector", "teop", oc + 1)
        p.wait("vector", "ldx", NCH * 16)
        for s in range(4):
            ps_s = (psA if s < 2 else psB)[:, (s % 2) * 512:(s % 2 + 1) * 512]
            p.op("vector", lambda e, oc=oc, s=s, ps_s=ps_s: e.scalar_tensor_tensor(
                x1_ch[oc][:, s * 512:(s + 1) * 512], ps_s, 1.0,
                x1_ch[oc][:, s * 512:(s + 1) * 512], ALU.mult, ALU.add), inc="evop")
    p.wait("vector", "ldb", 16 * 16)
    for c in range(NCH):
        p.op("vector", lambda e, c=c: e.tensor_scalar_add(
            x1_ch[c], x1_ch[c], opb_sb[:, c:c + 1]), inc="bias")
    p.wait("sync", "bias", p.count("bias"))
    for c in range(NCH):
        p.dma("sync", lambda e, c=c: e.dma_start(y1_d.ap()[c * 128:(c + 1) * 128, :], x1_ch[c]), inc="sty")


def build_l3b1():
    nc = bass.Bass("TRN2", target_bir_lowering=False, debug=False)
    p = Prog(nc)
    io = dict(
        x1=nc.dram_tensor("x1_fm", [D, T], F32, kind="ExternalInput"),
        q=nc.dram_tensor("q_fm", [D, T], BF16, kind="ExternalInput"),
        K=nc.dram_tensor("K_full", [D, 8192], BF16, kind="ExternalInput"),
        V=nc.dram_tensor("V_full", [8192, D], BF16, kind="ExternalInput"),
        out_projT=nc.dram_tensor("out_projT", [D, D], F32, kind="ExternalInput"),
        out_proj_b=nc.dram_tensor("out_proj_b", [D, 1], F32, kind="ExternalInput"),
        v_bias=nc.dram_tensor("v_bias", [D, 1], F32, kind="ExternalInput"),
        y1=nc.dram_tensor("y1_fm", [D, T], F32, kind="ExternalOutput"),
    )
    PB1 = p.ps("PB1", [128, 2048], F32)
    PB2 = p.ps("PB2", [128, 2048], F32)
    pp = dict(psA=PB1[:, 0:1024], psB=PB1[:, 1024:2048],
              psE=PB2[0:65, 0:512], psF=PB2[0:65, 512:1024],
              psC=PB2[:, 1024:1536], psD=PB2[:, 1536:2048])
    emit_p4(p, io, pp)
    p.finish_and_emit()
    return nc


def emit_p5(p, io, pp):
    nc = p.nc
    y1_d = io["y1"]; xn1_d = io["xn1"]; xn2_d = io["xn2"]
    w0T_d = io["w0T"]; w2T_d = io["w2T"]; w1_d = io["w1l"]; w1T_d = io["w1T"]
    gwT_d = io["gate_wT"]; bal_d = io["bal_b"]
    mw1_d = io["moew1T"]; mw2_d = io["moew2T"]
    out_d = io["out"]
    mw1b_d = io["mw1b"]; mw2b_d = io["mw2b"]; wc_d = io["wc"]

    y1_sb = p.sb("y1_sb", [128, NCH * T], F32)
    y1_ch = [y1_sb[:, c * T:(c + 1) * T] for c in range(NCH)]
    xn1_sb = p.sb("xn1_sb", [128, NCH * T], BF16)
    xn1_ch = [xn1_sb[:, c * T:(c + 1) * T] for c in range(NCH)]
    xn2_sb = p.sb("xn2_sb", [128, NCH * T], BF16)
    xn2_ch = [xn2_sb[:, c * T:(c + 1) * T] for c in range(NCH)]
    lw = p.sb("lw", [128, 4 * 4 * 512], BF16)
    def lw_sl(h, m, dc2):
        return lw[:, (h * 4 + m) * 512 + dc2 * 256:(h * 4 + m) * 512 + (dc2 + 1) * 256]
    kvt = p.sb("kvt", [128, 8 * 256], BF16)
    hid_sb = p.sb("hid_sb", [128, 8 * 256], BF16)
    dpre_sb = p.sb("dpre_sb", [128, 8 * 256], BF16)
    dg_sb = None
    gt_t = g2_t = u_t = w_t = wn0 = wn2 = wn1 = None
    hq_sb = p.sb("hq_sb", [128, 2 * 1024], BF16)
    he_sb = p.sb("he_sb", [128, 2 * 1024], BF16)
    dg_sb = he_sb
    gw_sb = p.sb("gw_sb", [128, NCH * 8], BF16)
    glt = p.sb("glt", [128, 16 * 8], F32)
    wcoef = p.sb("wcoef", [128, 16 * 8], F32)
    wrowT = None
    m1_t = p.sb("m1_t", [128, 1], F32)
    m2_t = p.sb("m2_t", [128, 1], F32)
    den_t = p.sb("den_t", [128, 1], F32)
    eps6 = p.sb("eps6", [128, 1], F32)
    bb_sb = p.sb("bb_sb", [128, 8], F32)
    ones_f = p.sb("ones_f", [1, 128], F32)
    identb = p.sb("identb", [128, 128], BF16)
    wrowT = None
    mw2_sb = [p.sb(f"mw2_sb{i}", [128, 2048], BF16) for i in range(2)]
    hst = p.sb("hst", [128, 16 * 1024], BF16)
    wrowT = hst[:].bitcast(F32)[0:8, 0:2048]
    sg_p = [hst[:, 8192:8448], hst[:, 11264:11520]]
    dh0_p = [hst[:, 8960:9216], hst[:, 11520:11776]]
    gt_t = hst[:, 8448:8704]; g2_t = hst[:, 8704:8960]
    u_t = hst[:, 9216:9472]; w_t = hst[:, 9472:9728]
    wn0 = hst[:, 9728:10240]; wn2 = hst[:, 10240:10752]; wn1 = hst[:, 10752:11264]
    wbc = None

    wbc = kvt[:].bitcast(F32)[:, 0:512]
    wbc2 = kvt[:].bitcast(F32)[:, 512:1024]
    psA, psB, psC, psD, psT = pp["psA"], pp["psB"], pp["psC"], pp["psD"], pp["psT"]

    for c in range(NCH):
        p.dma("sync", lambda e, c=c: e.dma_start(y1_ch[c], y1_d.ap()[c * 128:(c + 1) * 128, :]), inc="ldy")
        p.dma("sync", lambda e, c=c: e.dma_start(xn1_ch[c], xn1_d.ap()[c * 128:(c + 1) * 128, :]), inc="ldn1")
        p.dma("sync", lambda e, c=c: e.dma_start(xn2_ch[c], xn2_d.ap()[c * 128:(c + 1) * 128, :]), inc="ldn2")
    for h in range(4):
        for m, wd in ((0, w0T_d), (1, w2T_d), (2, w1_d), (3, w1T_d)):
            src = bass.AP(wd, h * 256 * 256, [[256, 128], [128 * 256, 2], [1, 256]])
            dst = bass.AP(lw, (h * 4 + m) * 512, [[8192, 128], [256, 2], [1, 256]])
            p.dma("gpsimd", lambda e, s=src, d=dst: e.dma_start(d, s), inc="glw")
    p.dma("gpsimd", lambda e: e.dma_start(
        gw_sb[:].rearrange("p (c f) -> p c f", c=NCH),
        bass.AP(gwT_d, 0, [[8, 128], [128 * 8, NCH], [1, 8]])), inc="glw")
    p.dma("sync", lambda e: e.dma_start(identb[:], io["ident_b"].ap()), inc="ldi")
    p.dma("sync", lambda e: e.dma_start(bb_sb[0:1, :], bal_d.ap()), inc="ldi")
    for ee in range(8):
        p.dma("gpsimd", lambda e, ee=ee: e.dma_start(
            mw1b_d.ap()[ee * D:(ee + 1) * D, :], mw1_d.ap()[ee * D:(ee + 1) * D, :]), inc="gcw1")
        p.dma("gpsimd", lambda e, ee=ee: e.dma_start(
            mw2b_d.ap()[ee * 2048:(ee + 1) * 2048, :], mw2_d.ap()[ee * 2048:(ee + 1) * 2048, :]), inc="gcw2")

    p.op("vector", lambda e: e.memset(ones_f[:], 1.0), inc="ones")
    p.op("vector", lambda e: e.memset(eps6[:], 1e-6), inc="ones")
    p.wait("tensor", "glw", 17 * 16)
    p.wait("tensor", "ldn1", 8 * 16)
    for s in range(4):
        pr = (psA if s < 2 else psB)[0:8, (s % 2) * 512:(s % 2 + 1) * 512]
        for dc in range(NCH):
            p.op("tensor", lambda e, s=s, dc=dc, pr=pr: e.matmul(
                pr, gw_sb[:, dc * 8:(dc + 1) * 8],
                xn1_ch[dc][:, s * 512:(s + 1) * 512],
                start=(dc == 0), stop=(dc == NCH - 1)),
                inc=("ter" if dc == NCH - 1 else None))
    p.wait("tensor", "ldi", 32)
    p.op("tensor", lambda e: e.matmul(psC[:, 0:8], ones_f[:], bb_sb[0:1, :],
                                      start=True, stop=True), inc="ter")
    gl8 = hq_sb[0:8, 0:T]
    p.wait("vector", "ter", p.count("ter"))
    for s in range(4):
        pr = (psA if s < 2 else psB)[0:8, (s % 2) * 512:(s % 2 + 1) * 512]
        p.op("vector", lambda e, s=s, pr=pr: e.tensor_copy(gl8[:, s * 512:(s + 1) * 512], pr), inc="evr")
    p.op("vector", lambda e: e.tensor_copy(bb_sb[:], psC[:, 0:8]), inc="evr")
    p.wait("tensor", "evr", p.count("evr"))
    psTs = [psT, pp["psT2"]]
    for tt in range(16):
        slot = psTs[tt % 2]
        if tt >= 2:
            p.wait("tensor", "evt", tt - 1)
        p.op("tensor", lambda e, tt=tt, slot=slot: e.transpose(
            slot[:, 0:8], gl8[0:8, tt * 128:(tt + 1) * 128], identb[0:8, 0:8]), inc="tet")
        p.wait("vector", "tet", p.count("tet"))
        p.op("vector", lambda e, tt=tt, slot=slot: e.tensor_copy(
            glt[:, tt * 8:(tt + 1) * 8], slot[0:128, 0:8]), inc="evt")
    p.op("vector", lambda e: e.tensor_add(
        glt[:, 0:128], glt[:, 0:128],
        bass.AP(bb_sb, 0, [[8, 128], [0, 16], [1, 8]])), inc="rt")
    p.wait("scalar", "rt", p.count("rt"))
    p.op("scalar", lambda e: e.activation(glt[:, 0:128], glt[:, 0:128], AF.Sigmoid), inc="rta")
    p.wait("vector", "rta", p.count("rta"))
    for tt in range(16):
        sl = slice(tt * 8, (tt + 1) * 8)
        p.op("vector", lambda e, sl=sl: e.reduce_max(m1_t[:], glt[:, sl], AX.X), inc="rt")
        p.op("vector", lambda e, sl=sl: e.tensor_scalar(wcoef[:, sl], glt[:, sl], m1_t[:, 0:1], -1e30,
                                                        ALU.is_ge, ALU.mult), inc="rt")
        p.op("vector", lambda e, sl=sl: e.tensor_add(wcoef[:, sl], wcoef[:, sl], glt[:, sl]), inc="rt")
        p.op("vector", lambda e, sl=sl: e.reduce_max(m2_t[:], wcoef[:, sl], AX.X), inc="rt")
        p.op("vector", lambda e: e.tensor_add(den_t[:], m1_t[:], m2_t[:]), inc="rt")
        p.op("vector", lambda e: e.tensor_add(den_t[:], den_t[:], eps6[:]), inc="rt")
        p.op("vector", lambda e: e.reciprocal(den_t[:], den_t[:]), inc="rt")
        p.op("vector", lambda e, sl=sl: e.tensor_scalar(wcoef[:, sl], glt[:, sl], m2_t[:, 0:1], None,
                                                        ALU.is_ge), inc="rt")
        p.op("vector", lambda e, sl=sl: e.tensor_mul(wcoef[:, sl], wcoef[:, sl], glt[:, sl]), inc="rt")
        p.op("vector", lambda e, sl=sl: e.tensor_scalar_mul(wcoef[:, sl], wcoef[:, sl], den_t[:, 0:1]), inc="rt")
    p.wait("vector", "rt", p.count("rt"))
    wct = p.sb("wct", [128, 16 * 8], BF16)
    p.op("vector", lambda e: e.tensor_copy(wct[:], wcoef[:]), inc="wcb")
    p.wait("tensor", "wcb", p.count("wcb"))
    for tt in range(16):
        slot = psTs[tt % 2]
        if tt >= 2:
            p.wait("tensor", "evt2", tt - 1)
        p.op("tensor", lambda e, tt=tt, slot=slot: e.transpose(
            slot[0:8, :], wct[:, tt * 8:(tt + 1) * 8], identb[:]), inc="tet2")
        p.wait("vector", "tet2", p.count("tet2"))
        p.op("vector", lambda e, tt=tt, slot=slot: e.tensor_copy(
            wrowT[0:8, tt * 128:(tt + 1) * 128], slot[0:8, 0:128]), inc="evt2")
    p.wait("sync", "evt2", p.count("evt2"))
    p.dma("sync", lambda e: e.dma_start(wc_d.ap(), wrowT[0:8, :]), inc="stwc")

    p.wait("tensor", "ldn2", 8 * 16)
    p.wait("tensor", "evt2", 16)
    for unit in range(8):
        n = unit // 4
        h = unit % 4
        cch = [2 * h, 2 * h + 1]
        ncol = n * 1024
        for ct in range(8):
            for d2 in range(2):
                p.op("tensor", lambda e, ct=ct, d2=d2, cch=cch, ncol=ncol: e.transpose(
                    psT[:], xn2_ch[cch[d2]][:, ncol + ct * 128:ncol + (ct + 1) * 128], identb[:]),
                    inc="lt1")
                p.wait("vector", "lt1", p.count("lt1"))
                p.op("vector", lambda e, ct=ct, d2=d2: e.tensor_copy(
                    kvt[:, ct * 256 + d2 * 128:ct * 256 + (d2 + 1) * 128], psT[:]), inc="lev1")
                p.wait("tensor", "lev1", p.count("lev1"))
        for ct in range(8):
            par = ct % 2
            if par == 0:
                pG = psA[:, 0:256]; pG2 = psA[:, 512:768]; pH = psB[:, 0:256]
            else:
                pG = psB[:, 512:768]; pG2 = psC[:, 0:256]; pH = psD[:, 0:256]
            sg_t = sg_p[par]; dh0_t = dh0_p[par]
            if ct == 1:
                p.wait("tensor", "le5n", p.count("le5n"))
            if ct >= 2:
                p.wait("tensor", "lely", p.count("lely") - 1)
            for d2 in range(2):
                for m, pp in ((0, pG), (1, pG2), (2, pH)):
                    p.op("tensor", lambda e, m=m, pp=pp, d2=d2, ct=ct, cch=cch, ncol=ncol, h=h: e.matmul(
                        pp, xn2_ch[cch[d2]][:, ncol + ct * 128:ncol + (ct + 1) * 128],
                        lw_sl(h, m, d2), start=(d2 == 0), stop=(d2 == 1)),
                        inc=("lt2" if (d2 == 1 and m == 2) else None))
            p.wait("scalar", "lt2", p.count("lt2"))
            p.op("scalar", lambda e, pG=pG, sg_t=sg_t: e.activation(sg_t[:], pG, AF.Sigmoid), inc="la")
            p.op("scalar", lambda e, pH=pH, dh0_t=dh0_t: e.activation(dh0_t[:], pH, AF.Copy, scale=-1.0), inc="la")
            p.wait("vector", "la", p.count("la"))
            p.op("vector", lambda e, pG=pG: e.tensor_copy(gt_t[:], pG), inc="le")
            p.op("vector", lambda e, pG2=pG2: e.tensor_copy(g2_t[:], pG2), inc="le")
            sl2 = slice(ct * 256, (ct + 1) * 256)
            p.op("vector", lambda e, sg_t=sg_t: e.tensor_mul(u_t[:], gt_t[:], sg_t[:]), inc="le")
            p.op("vector", lambda e, sl2=sl2: e.tensor_mul(hid_sb[:, sl2], u_t[:], g2_t[:]), inc="le")
            p.op("vector", lambda e, sl2=sl2, dh0_t=dh0_t: e.tensor_mul(dg_sb[:, sl2], dh0_t[:], u_t[:]), inc="le")
            p.op("vector", lambda e, sg_t=sg_t: e.tensor_add(w_t[:], sg_t[:], u_t[:]), inc="le")
            p.op("vector", lambda e, sg_t=sg_t: e.tensor_mul(gt_t[:], u_t[:], sg_t[:]), inc="le")
            p.op("vector", lambda e: e.tensor_sub(w_t[:], w_t[:], gt_t[:]), inc="le")
            p.op("vector", lambda e, dh0_t=dh0_t: e.tensor_mul(gt_t[:], dh0_t[:], g2_t[:]), inc="le")
            p.op("vector", lambda e, sl2=sl2: e.tensor_mul(dpre_sb[:, sl2], gt_t[:], w_t[:]), inc="le",
                 n=1)
            p.op("vector", lambda e: e.engine_nop(), inc="lely")
        p.wait("tensor", "le", p.count("le"))
        p.wait("tensor", "le5n", p.count("le5n"))
        for d2 in range(2):
            for ct in range(8):
                p.op("tensor", lambda e, ct=ct, d2=d2: e.matmul(
                    psC[:, 0:256] if False else psA[:, d2 * 256:(d2 + 1) * 256],
                    kvt[:, ct * 256 + d2 * 128:ct * 256 + (d2 + 1) * 128],
                    dpre_sb[:, ct * 256:(ct + 1) * 256],
                    start=(ct == 0), stop=(ct == 7)), inc=("lt3" if ct == 7 else None))
                p.op("tensor", lambda e, ct=ct, d2=d2: e.matmul(
                    psB[:, d2 * 256:(d2 + 1) * 256],
                    kvt[:, ct * 256 + d2 * 128:ct * 256 + (d2 + 1) * 128],
                    dg_sb[:, ct * 256:(ct + 1) * 256],
                    start=(ct == 0), stop=(ct == 7)), inc=("lt3" if ct == 7 else None))
                p.op("tensor", lambda e, ct=ct, d2=d2: e.matmul(
                    psC[:, d2 * 256:(d2 + 1) * 256] if True else None,
                    hid_sb[:].rearrange("p (ct f) -> p ct f", ct=8)[:, ct, d2 * 128:(d2 + 1) * 128],
                    kvt[:, ct * 256:(ct + 1) * 256],
                    start=(ct == 0), stop=(ct == 7)), inc=("lt3" if ct == 7 else None))
        p.wait("vector", "lt3", p.count("lt3"))
        p.op("vector", lambda e, h=h: e.tensor_sub(wn0[:], lw[:, (h * 4 + 0) * 512:(h * 4 + 1) * 512], psA[:, 0:512]), inc="lwn")
        p.op("vector", lambda e, h=h: e.tensor_sub(wn2[:], lw[:, (h * 4 + 1) * 512:(h * 4 + 2) * 512], psB[:, 0:512]), inc="lwn")
        p.op("vector", lambda e, h=h: e.tensor_add(wn1[:], lw[:, (h * 4 + 3) * 512:(h * 4 + 4) * 512], psC[:, 0:512]), inc="lwn")
        p.wait("tensor", "lwn", p.count("lwn"))
        for es in range(2):
            for cs in range(2):
                for d2 in range(2):
                    p.op("tensor", lambda e, es=es, cs=cs, d2=d2, cch=cch, ncol=ncol: e.matmul(
                        psA[:, cs * 512:(cs + 1) * 512] if es == 0 else psA[:, cs * 512:(cs + 1) * 512],
                        wn0[:, es * 256 + 0:0] if False else wn0[:].rearrange("p (d2 f) -> p d2 f", d2=2)[:, d2, es * 128:(es + 1) * 128],
                        xn2_ch[cch[d2]][:, ncol + cs * 512:ncol + (cs + 1) * 512],
                        start=(d2 == 0), stop=(d2 == 1)),
                        inc=("lt4" if d2 == 1 else None))
                    p.op("tensor", lambda e, es=es, cs=cs, d2=d2, cch=cch, ncol=ncol: e.matmul(
                        psB[:, cs * 512:(cs + 1) * 512],
                        wn2[:].rearrange("p (d2 f) -> p d2 f", d2=2)[:, d2, es * 128:(es + 1) * 128],
                        xn2_ch[cch[d2]][:, ncol + cs * 512:ncol + (cs + 1) * 512],
                        start=(d2 == 0), stop=(d2 == 1)),
                        inc=("lt4" if d2 == 1 else None))
                p.wait("scalar", "lt4", p.count("lt4"))
                p.op("scalar", lambda e, es=es, cs=cs: e.activation(
                    hq_sb[:, es * 1024 + cs * 512:es * 1024 + (cs + 1) * 512],
                    psA[:, cs * 512:(cs + 1) * 512], AF.Silu), inc="la4")
                p.wait("vector", "la4", p.count("la4"))
                p.op("vector", lambda e, es=es, cs=cs: e.tensor_copy(
                    he_sb[:, es * 1024 + cs * 512:es * 1024 + (cs + 1) * 512],
                    psB[:, cs * 512:(cs + 1) * 512]), inc="le4")
                p.op("vector", lambda e, es=es, cs=cs: e.tensor_mul(
                    he_sb[:, es * 1024 + cs * 512:es * 1024 + (cs + 1) * 512],
                    he_sb[:, es * 1024 + cs * 512:es * 1024 + (cs + 1) * 512],
                    hq_sb[:, es * 1024 + cs * 512:es * 1024 + (cs + 1) * 512]), inc="le4")
                p.op("vector", lambda e: e.engine_nop(), inc="le4n")
                if es * 2 + cs >= 1:
                    p.wait("tensor", "le4n", p.count("le4n") - 1)
        p.wait("tensor", "le4n", p.count("le4n"))
        for ds in range(2):
            for cs in range(2):
                par5 = (ds * 2 + cs) % 2
                pOQ = psC[:, 0:512] if par5 == 0 else psD[:, 0:512]
                if ds * 2 + cs >= 2:
                    p.wait("tensor", "le5n", p.count("le5n") - 1)
                for es in range(2):
                    p.op("tensor", lambda e, ds=ds, cs=cs, es=es, pOQ=pOQ: e.matmul(
                        pOQ,
                        wn1[:].rearrange("p (es f) -> p es f", es=2)[:, es, ds * 128:(ds + 1) * 128],
                        he_sb[:, es * 1024 + cs * 512:es * 1024 + (cs + 1) * 512],
                        start=(es == 0), stop=(es == 1)),
                        inc=("lt5" if es == 1 else None))
                p.wait("vector", "lt5", p.count("lt5"))
                p.wait("vector", "ldy", 8 * 16)
                p.op("vector", lambda e, ds=ds, cs=cs, cch=cch, ncol=ncol, pOQ=pOQ: e.scalar_tensor_tensor(
                    y1_ch[cch[ds]][:, ncol + cs * 512:ncol + (cs + 1) * 512], pOQ, 1.0,
                    y1_ch[cch[ds]][:, ncol + cs * 512:ncol + (cs + 1) * 512], ALU.mult, ALU.add), inc="le5")
                p.op("vector", lambda e: e.engine_nop(), inc="le5n")

    mh = [he_sb[:, 0:1024], he_sb[:, 1024:2048]]
    p.wait("sync", "stwc", 16)
    p.wait("sync", "le5n", 32)
    p.wait("vector", "le5n", 32)
    for sp in range(2):
        for ee in range(8):
            ie = sp * 8 + ee
            for hc in range(16):
                it = ie * 16 + hc
                buf = it % 2
                dq = "sync"
                if sp == 0 and hc <= 1:
                    p.wait(dq, "gcw1", 16 * (ee + 1))
                if it >= 2:
                    p.wait(dq, "temh", it - 1)
                p.dma(dq, lambda e, ee=ee, hc=hc, buf=buf: e.dma_start(
                    mh[buf][:].rearrange("p (c f) -> p c f", c=NCH),
                    bass.AP(mw1b_d, ee * D * 2048 + hc * 128,
                            [[2048, 128], [128 * 2048, NCH], [1, 128]])), inc=f"mhd{buf}")
                p.wait("tensor", f"mhd{buf}", 16 * (it // 2 + 1))
                ph2 = [psC[:], psD[:]] if it % 2 == 0 else [psA[:, 0:512], psA[:, 512:1024]]
                if it >= 2:
                    p.wait("tensor", "evh", 2 * (it - 1))
                if it % 2 == 1:
                    p.wait("tensor", "evd", 2 * 8 * ie)
                for dc in range(NCH):
                    for t2 in range(2):
                        p.op("tensor", lambda e, sp=sp, dc=dc, t2=t2, buf=buf, ph2=ph2: e.matmul(
                            ph2[t2],
                            mh[buf][:, dc * 128:(dc + 1) * 128],
                            xn1_ch[dc][:, (sp * 2 + t2) * 512:(sp * 2 + t2 + 1) * 512],
                            start=(dc == 0), stop=(dc == NCH - 1)),
                            inc=("temh" if (dc == NCH - 1 and t2 == 1) else None))
                p.wait("scalar", "temh", p.count("temh"))
                p.wait("scalar", "temd", 8 * ie)
                if it == 0:
                    p.wait("scalar", "stwc", 16)
                for t2 in range(2):
                    p.op("scalar", lambda e, hc=hc, t2=t2, ph2=ph2: e.activation(
                        hst[:, hc * 1024 + t2 * 512:hc * 1024 + (t2 + 1) * 512],
                        ph2[t2], AF.Silu), inc="evh")
            p.dma("scalar", lambda e, ee=ee, sp=sp: e.dma_start(
                wbc, bass.AP(wc_d, ee * T + sp * 1024, [[0, 128], [1, 512]])), inc="wbcd")
            p.dma("scalar", lambda e, ee=ee, sp=sp: e.dma_start(
                wbc2, bass.AP(wc_d, ee * T + sp * 1024 + 512, [[0, 128], [1, 512]])), inc="wbcd")
            for dc in range(NCH):
                it2 = ie * 8 + dc
                buf = it2 % 2
                dq2 = "scalar"
                if sp == 0 and dc <= 1:
                    p.wait(dq2, "gcw2", 16 * (ee + 1))
                if it2 >= 2:
                    p.wait(dq2, "temd", it2 - 1)
                p.dma(dq2, lambda e, ee=ee, dc=dc, buf=buf: e.dma_start(
                    mw2_sb[buf][:].rearrange("p (c f) -> p c f", c=16),
                    bass.AP(mw2b_d, ee * 2048 * D + dc * 128,
                            [[D, 128], [128 * D, 16], [1, 128]])), inc=f"md{buf}")
                p.wait("tensor", f"md{buf}", 16 * (it2 // 2 + 1))
                p.wait("tensor", "evh", p.count("evh"))
                if it2 >= 2:
                    p.wait("tensor", "evd", 2 * (it2 - 1))
                pd2 = psA if dc % 2 == 0 else psB
                for hc in range(16):
                    for t2 in range(2):
                        p.op("tensor", lambda e, dc=dc, hc=hc, t2=t2, buf=buf, pd2=pd2: e.matmul(
                            pd2[:, t2 * 512:(t2 + 1) * 512],
                            mw2_sb[buf][:, hc * 128:(hc + 1) * 128],
                            hst[:, hc * 1024 + t2 * 512:hc * 1024 + (t2 + 1) * 512],
                            start=(hc == 0), stop=(hc == 15)),
                            inc=("temd" if (hc == 15 and t2 == 1) else None))
                p.wait("vector", "temd", p.count("temd"))
                p.wait("vector", "wbcd", p.count("wbcd"))
                for t2 in range(2):
                    p.op("vector", lambda e, t2=t2, pd2=pd2: e.tensor_mul(
                        hq_sb[:, 0:512] if t2 == 0 else hq_sb[:, 512:1024],
                        pd2[:, t2 * 512:(t2 + 1) * 512],
                        wbc if t2 == 0 else wbc2), inc="evd0")
                p.wait("gpsimd", "evd0", p.count("evd0"))
                for t2 in range(2):
                    p.op("gpsimd", lambda e, sp=sp, dc=dc, t2=t2: e.tensor_add(
                        y1_ch[dc][:, (sp * 2 + t2) * 512:(sp * 2 + t2 + 1) * 512],
                        y1_ch[dc][:, (sp * 2 + t2) * 512:(sp * 2 + t2 + 1) * 512],
                        hq_sb[:, 0:512] if t2 == 0 else hq_sb[:, 512:1024]), inc="evd")
    p.wait("sync", "evd", p.count("evd"))
    for c in range(NCH):
        p.dma("sync", lambda e, c=c: e.dma_start(out_d.ap()[c * 128:(c + 1) * 128, :], y1_ch[c]), inc="sto")




def build_l3b2():
    nc = bass.Bass("TRN2", target_bir_lowering=False, debug=False)
    p = Prog(nc)
    io = dict(
        y1=nc.dram_tensor("y1_fm", [D, T], F32, kind="ExternalInput"),
        xn1=nc.dram_tensor("xn1_fm", [D, T], BF16, kind="ExternalInput"),
        xn2=nc.dram_tensor("xn2_fm", [D, T], BF16, kind="ExternalInput"),
        w0T=nc.dram_tensor("w0T", [4 * 256, 256], F32, kind="ExternalInput"),
        w2T=nc.dram_tensor("w2T", [4 * 256, 256], F32, kind="ExternalInput"),
        w1l=nc.dram_tensor("w1l", [4 * 256, 256], F32, kind="ExternalInput"),
        w1T=nc.dram_tensor("w1T", [4 * 256, 256], F32, kind="ExternalInput"),
        gate_wT=nc.dram_tensor("gate_wT", [D, 8], F32, kind="ExternalInput"),
        bal_b=nc.dram_tensor("bal_b", [1, 8], F32, kind="ExternalInput"),
        moew1T=nc.dram_tensor("moew1T", [8 * D, 2048], F32, kind="ExternalInput"),
        moew2T=nc.dram_tensor("moew2T", [8 * 2048, D], F32, kind="ExternalInput"),
        ident_b=nc.dram_tensor("ident_b", [128, 128], BF16, kind="ExternalInput"),
        out=nc.dram_tensor("out_fm", [D, T], F32, kind="ExternalOutput"),
        mw1b=nc.dram_tensor("mw1b", [8 * D, 2048], BF16, kind="Internal"),
        mw2b=nc.dram_tensor("mw2b", [8 * 2048, D], BF16, kind="Internal"),
        wc=nc.dram_tensor("wc_dram", [8, T], F32, kind="Internal"),
    )
    PB1 = p.ps("PB1", [128, 2048], F32)
    PB2 = p.ps("PB2", [128, 2048], F32)
    pp = dict(psA=PB1[:, 0:1024], psB=PB1[:, 1024:2048],
              psC=PB2[:, 0:512], psD=PB2[:, 512:1024],
              psT=PB2[:, 1024:1088].bitcast(BF16),
              psT2=PB2[:, 1536:1600].bitcast(BF16))
    emit_p5(p, io, pp)
    p.finish_and_emit()
    return nc


T8 = 8192


def emit_p1(p, io, pp):
    nc = p.nc
    x_d, wgT_d, wuT_d, wdT_d = io["x"], io["wgT"], io["wuT"], io["wdT"]
    xn_d, acc_d = io["xn"], io["acc"]
    pbig1, pbig2 = pp["PB1"], pp["PB2"]

    x_sb = p.sb("x_sb", [128, NCH * T], F32)
    xn_sb = p.sb("xn_sb", [128, NCH * T], BF16)
    x_ch = [x_sb[:, c * T:(c + 1) * T] for c in range(NCH)]
    xn_ch = [xn_sb[:, c * T:(c + 1) * T] for c in range(NCH)]
    lnsc = make_ln_scratch(p)
    wg_sb = [p.sb(f"wg{i}", [128, NCH * 128], BF16) for i in range(2)]
    wu_sb = [p.sb(f"wu{i}", [128, NCH * 128], BF16) for i in range(2)]
    wd_sb = [p.sb(f"wd{i}", [128, NH0 * 128], BF16) for i in range(2)]
    g_sb = p.sb("g_sb", [128, 1024], BF16)
    hTall = p.sb("hTall", [128, NH0 * 1024], BF16)
    ps4a = [pbig1[:, i * 512:(i + 1) * 512] for i in range(4)]
    ps4b = [pbig2[:, i * 512:(i + 1) * 512] for i in range(4)]

    for s4 in range(4):
        co = s4 * T
        ldx_base = p.count("ld_x")
        for c in range(NCH):
            if s4 >= 1:
                p.wait("sync", "st_acc", 16 * (8 * (s4 - 1) + c + 1))
            p.dma("sync", lambda e, c=c, co=co: e.dma_start(
                x_ch[c], x_d.ap()[c * 128:(c + 1) * 128, co:co + T]), inc="ld_x")
        if s4 >= 1:
            p.wait("tensor", "dve_acc", 32 * s4)
        ln_done = emit_ln_fm(p, f"l0{s4}", x_ch, xn_ch, pbig1, pbig2, lnsc,
                             x_chunks=("ld_x", ldx_base),
                             bc_sb=hTall[:].bitcast(F32)[:, 0:2 * T])
        p.wait("sync", f"l0{s4}_dve", ln_done)
        for c in range(NCH):
            p.dma("sync", lambda e, c=c, co=co: e.dma_start(
                xn_d.ap()[c * 128:(c + 1) * 128, co:co + T], xn_ch[c]), inc="st_xn")
        for hv in range(2):
          ho = hv * 1024
          for hc in range(NH0):
            buf = hc % 2
            it = (s4 * 2 + hv) * NH0 + hc
            src_g = bass.AP(wgT_d, hc * 128, [[HID0, 128], [128 * HID0, 8], [1, 128]])
            src_u = bass.AP(wuT_d, hc * 128, [[HID0, 128], [128 * HID0, 8], [1, 128]])
            if it >= 2:
                p.wait("gpsimd", "te_gu", it - 1)
            p.dma("gpsimd", lambda e, s=src_g, buf=buf: e.dma_start(
                wg_sb[buf][:].rearrange("p (c f) -> p c f", c=8), s), inc=f"wgu{buf}")
            p.dma("gpsimd", lambda e, s=src_u, buf=buf: e.dma_start(
                wu_sb[buf][:].rearrange("p (c f) -> p c f", c=8), s), inc=f"wgu{buf}")
            p.wait("tensor", f"wgu{buf}", 32 * (it // 2 + 1))
            pg = pbig1[:, (it % 2) * 1024:(it % 2 + 1) * 1024]
            pu = pbig2[:, (it % 2) * 1024:(it % 2 + 1) * 1024]
            if it >= 2:
                p.wait("tensor", "evac_gu", 3 * (it - 1))
            if hc <= 1 and it >= 2:
                p.wait("tensor", "dve_acc", 2 * NCH * (s4 * 2 + hv))
            for dc in range(NCH):
                if hc == 0 and hv == 0:
                    p.wait("tensor", f"l0{s4}_dve", ln_done - 2 * (NCH - 1 - dc))
                for tq in range(2):
                    p.op("tensor", lambda e, buf=buf, dc=dc, tq=tq, ho=ho, pg=pg: e.matmul(
                        pg[:, tq * 512:(tq + 1) * 512], wg_sb[buf][:, dc * 128:(dc + 1) * 128],
                        xn_ch[dc][:, ho + tq * 512:ho + (tq + 1) * 512],
                        start=(dc == 0), stop=(dc == NCH - 1)))
            for dc in range(NCH):
                for tq in range(2):
                    p.op("tensor", lambda e, buf=buf, dc=dc, tq=tq, ho=ho, pu=pu: e.matmul(
                        pu[:, tq * 512:(tq + 1) * 512], wu_sb[buf][:, dc * 128:(dc + 1) * 128],
                        xn_ch[dc][:, ho + tq * 512:ho + (tq + 1) * 512],
                        start=(dc == 0), stop=(dc == NCH - 1)),
                        inc=("te_gu" if (dc == NCH - 1 and tq == 1) else None))
            p.wait("scalar", "te_gu", p.count("te_gu"))
            if hc == 0:
                p.wait("scalar", "te_d", (s4 * 2 + hv) * NCH)
            p.op("scalar", lambda e, pg=pg: e.activation(g_sb[:], pg, AF.Silu, scale=QSC),
                 inc="evac_gu")
            p.op("scalar", lambda e, hc=hc, pu=pu: e.activation(
                hTall[:, hc * 1024:(hc + 1) * 1024], pu, AF.Copy, scale=QSC),
                inc="evac_gu")
            p.wait("vector", "evac_gu", p.count("evac_gu"))
            p.op("vector", lambda e, hc=hc: e.tensor_mul(
                hTall[:, hc * 1024:(hc + 1) * 1024], g_sb[:],
                hTall[:, hc * 1024:(hc + 1) * 1024]), inc="evac_gu")
          for dc in range(NCH):
            buf = dc % 2
            it = (s4 * 2 + hv) * NCH + dc
            src_d = bass.AP(wdT_d, dc * 128, [[D, 128], [128 * D, NH0], [1, 128]])
            if it >= 2:
                p.wait("gpsimd", "te_d", it - 1)
            p.dma("gpsimd", lambda e, s=src_d, buf=buf: e.dma_start(
                wd_sb[buf][:].rearrange("p (c f) -> p c f", c=NH0), s), inc=f"wdd{buf}")
            p.wait("tensor", f"wdd{buf}", 16 * (it // 2 + 1))
            pd2 = pbig1[:, (it % 2) * 1024:(it % 2 + 1) * 1024]
            if dc == 0:
                p.wait("tensor", "evac_gu", p.count("evac_gu"))
            if it >= 2:
                p.wait("tensor", "dve_acc", 2 * (it - 1))
            for hc in range(NH0):
                for tq in range(2):
                    p.op("tensor", lambda e, buf=buf, hc=hc, tq=tq, pd2=pd2: e.matmul(
                        pd2[:, tq * 512:(tq + 1) * 512], wd_sb[buf][:, hc * 128:(hc + 1) * 128],
                        hTall[:, hc * 1024 + tq * 512:hc * 1024 + (tq + 1) * 512],
                        start=(hc == 0), stop=(hc == NH0 - 1)),
                        inc=("te_d" if (hc == NH0 - 1 and tq == 1) else None))
            p.wait("vector", "te_d", p.count("te_d"))
            for tq in range(2):
                p.op("vector", lambda e, dc=dc, tq=tq, ho=ho, pd2=pd2: e.scalar_tensor_tensor(
                    x_ch[dc][:, ho + tq * 512:ho + (tq + 1) * 512],
                    pd2[:, tq * 512:(tq + 1) * 512], QSC,
                    x_ch[dc][:, ho + tq * 512:ho + (tq + 1) * 512], ALU.mult, ALU.add),
                    inc="dve_acc")
        acc_base = 32 * s4
        for c in range(NCH):
            p.wait("sync", "dve_acc", acc_base + 16 + 2 * (c + 1))
            p.dma("sync", lambda e, c=c, co=co: e.dma_start(
                acc_d.ap()[c * 128:(c + 1) * 128, co:co + T], x_ch[c]), inc="st_acc")


def emit_p2_tile(p, io, pp, dt8):
    nc = p.nc
    xn_d, y_d = io["xn"], io["y"]
    ldt_d, B_d, C_d, Dv_d = io["log_dt"], io["Bmat"], io["Cmat"], io["Dvec"]
    A_d, iA_d = io["A_bc"], io["invA_bc"]
    idb_d, jrev_d, jv129_d, gm_d = io["ident_b"], io["jrev"], io["jv129"], io["gmask"]
    krep_d, urev_d, v_d = io["krep"], io["urev"], io["vdram"]
    PB1, PB2 = pp["PB1"], pp["PB2"]
    pY = PB1[:, 0:512]
    pP = PB1[0:64, 512:1024]
    pT = PB2[0:64, 0:128]
    pTRb = [PB2[:, 512:576].bitcast(BF16), PB2[:, 1024:1088].bitcast(BF16),
            PB2[:, 1536:1600].bitcast(BF16), PB2[:, 0:64].bitcast(BF16)]
    ro = dt8 * 128

    ldt = p.sb("ldt", [128, 1], F32)
    Bs = p.sb("Bs", [128, 64], F32)
    Cs = p.sb("Cs", [128, 64], F32)
    Ds = p.sb("Ds", [128, 1], F32)
    Abc = p.sb("Abc", [128, 64], F32)
    iAbc = p.sb("iAbc", [128, 64], F32)
    idb = p.sb("idb", [128, 128], BF16)
    jrev = p.sb("jrev", [128, 128], F32)
    jv129 = p.sb("jv129", [128, 129], F32)
    gm = p.sb("gm", [64, 4], F32)
    dt_t = p.sb("dt_t", [128, 1], F32)
    dtA = p.sb("dtA", [128, 64], F32)
    ones64 = p.sb("ones64", [128, 64], F32)
    CBt = p.sb("CBt", [128, 64], F32)
    grid = p.sb("grid", [128, 129 * 64], F32)
    urev_sb = p.sb("urev_sb", [128, 8192], BF16)
    v_sb = p.sb("v_sb", [128, 64 * 129], BF16)
    K_sb = p.sb("K_sb", [128, 256], F32)
    K_bf = p.sb("K_bf", [128, 256], BF16)
    lamQ = p.sb("lamQ", [128, 64], F32)
    lamQT = p.sb("lamQT", [64, 128], F32)
    lamQ16 = p.sb("lamQ16", [128, 64], F32)
    lamQ16T = p.sb("lamQ16T", [64, 128], F32)
    lqp = p.sb("lqp", [64, 128 * 16], F32)
    pwt = p.sb("pwt", [64, 128], F32)
    xfm = p.sb("xfm", [128, 8192], BF16)
    xraw = p.sb("xraw", [128, 8192], BF16)
    y_sb = p.sb("ysb", [128, 8192], BF16)
    P_all = p.sb("Pall", [64, 8192], BF16)
    S_shift = p.sb("S_shift", [64, 128 * 65], F32)
    S_bf = grid[:].bitcast(BF16)[0:64, 0:8320]
    EndA = grid[0:64, 4500:5012]
    Dd = grid[0:64, 5012:5524]
    Outq = grid[0:64, 5524:5652]
    ctmp = grid[0:64, 5652:7700]
    T8b = [p.sb(f"T8_{i}", [128, 8 * 128], BF16) for i in range(2)]
    W8 = [p.sb(f"W8_{i}", [128, 8 * 64], BF16) for i in range(2)]
    M8 = [p.sb(f"M8_{i}", [64, 8 * 128], BF16) for i in range(2)]

    for t, d in ((idb, idb_d), (jrev, jrev_d), (jv129, jv129_d), (gm, gm_d),
                 (Abc, A_d), (iAbc, iA_d)):
        p.dma("sync", lambda e, t=t, d=d: e.dma_start(t[:], d.ap()), inc="lp")
    for t, d in ((ldt, ldt_d), (Bs, B_d), (Cs, C_d), (Ds, Dv_d)):
        p.dma("sync", lambda e, t=t, d=d: e.dma_start(t[:], d.ap()[ro:ro + 128, :]), inc="lp")
    for pc in range(8):
        p.dma("scalar", lambda e, pc=pc: e.dma_start(
            xfm[:, pc * 1024:(pc + 1) * 1024],
            xn_d.ap()[ro:ro + 128, pc * 1024:(pc + 1) * 1024]), inc="lx")

    p.wait("scalar", "lp", 160)
    p.op("scalar", lambda e: e.activation(dt_t[:], ldt[:], AF.Exp), inc="a")
    p.wait("vector", "a", p.count("a"))
    p.op("vector", lambda e: e.memset(ones64[:], 1.0))
    p.op("vector", lambda e: e.tensor_scalar_mul(dtA[:], Abc[:], dt_t[:, 0:1]))
    p.op("vector", lambda e: e.tensor_copy(CBt[:], dtA[:]), inc="v")
    p.wait("scalar", "v", p.count("v"))
    p.op("scalar", lambda e: e.activation(CBt[:], CBt[:], AF.Exp), inc="a")
    p.wait("vector", "a", p.count("a"))
    p.op("vector", lambda e: e.tensor_sub(CBt[:], CBt[:], ones64[:]))
    p.op("vector", lambda e: e.tensor_mul(CBt[:], CBt[:], iAbc[:]))
    p.op("vector", lambda e: e.tensor_mul(CBt[:], CBt[:], Bs[:]))
    p.op("vector", lambda e: e.tensor_mul(CBt[:], CBt[:], Cs[:]))
    p.op("vector", lambda e: e.memset(K_sb[:], 0.0))
    for hf in range(2):
        p.op("vector", lambda e, hf=hf: e.tensor_tensor(
            bass.AP(grid, hf * 4096, [[8256, 128], [64, 64], [1, 64]]),
            bass.AP(dtA, 0, [[64, 128], [0, 64], [1, 64]]),
            bass.AP(jv129, hf * 64, [[129, 128], [1, 64], [0, 64]]), ALU.mult), inc="v")
        p.wait("scalar", "v", p.count("v"))
        p.op("scalar", lambda e, hf=hf: e.activation(
            grid[:, hf * 4096:(hf + 1) * 4096], grid[:, hf * 4096:(hf + 1) * 4096],
            AF.Exp), inc="a")
    for hf in range(2):
        p.wait("vector", "a", hf + 1)
        p.op("vector", lambda e, hf=hf: e.tensor_tensor(
            grid[:, hf * 4096:(hf + 1) * 4096], grid[:, hf * 4096:(hf + 1) * 4096],
            bass.AP(CBt, 0, [[64, 128], [0, 64], [1, 64]]), ALU.mult))
        p.op("vector", lambda e, hf=hf: e.tensor_reduce(
            K_sb[:, 127 + hf * 64:127 + (hf + 1) * 64],
            grid[:].rearrange("p (j n) -> p j n", n=64)[:, hf * 64:(hf + 1) * 64, :],
            AX.X, ALU.add))
    p.op("vector", lambda e: e.tensor_add(K_sb[:, 127:128], K_sb[:, 127:128], Ds[:]))
    p.op("vector", lambda e: e.tensor_copy(K_bf[:], K_sb[:]), inc="kv")
    p.wait("sync", "kv", 1)
    p.dma("sync", lambda e: e.dma_start(
        bass.AP(krep_d, ro * 32768, [[32768, 128], [1, 256]]), K_bf[:]), inc="kr")
    for s in range(7):
        n = 256 << s
        p.wait("sync", "kr", 16 * (s + 1))
        p.dma("sync", lambda e, n=n: e.dma_start(
            bass.AP(krep_d, ro * 32768 + n, [[32768, 128], [1, n]]),
            bass.AP(krep_d, ro * 32768, [[32768, 128], [1, n]])), inc="kr")
    for hf in range(2):
        p.op("vector", lambda e, hf=hf: e.tensor_tensor(
            bass.AP(grid, hf * 4096, [[8256, 128], [64, 64], [1, 64]]),
            bass.AP(dtA, 0, [[64, 128], [0, 64], [1, 64]]),
            bass.AP(jrev, hf * 64, [[128, 128], [1, 64], [0, 64]]), ALU.mult), inc="v")
        p.wait("scalar", "v", p.count("v"))
        p.op("scalar", lambda e, hf=hf: e.activation(
            urev_sb[:, hf * 4096:(hf + 1) * 4096], grid[:, hf * 4096:(hf + 1) * 4096],
            AF.Exp), inc="au")
        p.wait("gpsimd", "au", hf + 1)
        p.dma("gpsimd", lambda e, hf=hf: e.dma_start(
            bass.AP(urev_d, ro * 8192 + hf * 4096, [[8192, 128], [1, 4096]]),
            urev_sb[:, hf * 4096:(hf + 1) * 4096]), inc="g")
    for hf in range(2):
        p.wait("vector", "au", 2)
        p.op("vector", lambda e, hf=hf: e.tensor_tensor(
            bass.AP(grid, hf * 4128, [[8256, 128], [129, 32], [1, 129]]),
            bass.AP(dtA, hf * 32, [[64, 128], [1, 32], [0, 129]]),
            bass.AP(jv129, 0, [[129, 128], [0, 32], [1, 129]]), ALU.mult), inc="v")
        p.wait("scalar", "v", p.count("v"))
        p.op("scalar", lambda e, hf=hf: e.activation(
            grid[:, hf * 4128:(hf + 1) * 4128], grid[:, hf * 4128:(hf + 1) * 4128],
            AF.Exp), inc="a")
        p.wait("vector", "a", p.count("a"))
        p.op("vector", lambda e, hf=hf: e.tensor_tensor(
            v_sb[:, hf * 4128:(hf + 1) * 4128], grid[:, hf * 4128:(hf + 1) * 4128],
            bass.AP(CBt, hf * 32, [[64, 128], [1, 32], [0, 129]]), ALU.mult), inc="vv")
    p.wait("scalar", "vv", 2)
    p.op("scalar", lambda e: e.activation(lamQ[:], dtA[:], AF.Exp, scale=128.0), inc="a")
    p.op("scalar", lambda e: e.activation(lamQ16[:], dtA[:], AF.Exp, scale=2048.0), inc="a")
    p.wait("gpsimd", "vv", 2)
    p.dma("gpsimd", lambda e: e.dma_start(
        bass.AP(v_d, ro * 8256, [[8256, 128], [1, 8256]]), v_sb[:]), inc="g")
    p.wait("tensor", "lx", 128)
    for tt in range(64):
        pb = pTRb[tt % 4]
        if tt >= 4:
            p.wait("tensor", "ex", tt - 3)
        p.op("tensor", lambda e, tt=tt, pb=pb: e.transpose(
            pb, xfm[:, tt * 128:(tt + 1) * 128], idb[:]), inc="tx")
        p.wait("scalar", "tx", tt + 1)
        p.op("scalar", lambda e, tt=tt, pb=pb: e.activation(
            xraw[:, tt * 128:(tt + 1) * 128], pb, AF.Copy), inc="ex")

    idf = p.sb("idf", [128, 128], F32)
    p.wait("vector", "lp", 160)
    p.op("vector", lambda e: e.tensor_copy(idf[:], idb[:]))
    p.op("vector", lambda e: e.engine_nop(), inc="idr")
    p.wait("tensor", "idr", 1)
    p.wait("tensor", "a", p.count("a"))
    p.wait("tensor", "ex", 64)
    p.op("tensor", lambda e: e.transpose(pT[:], lamQ[:, 0:64], idf[:]), inc="t1")
    p.wait("vector", "t1", 1)
    p.op("vector", lambda e: e.tensor_copy(lamQT[:], pT[:]), inc="vq")
    p.wait("tensor", "vq", 1)
    p.op("tensor", lambda e: e.transpose(pT[:], lamQ16[:, 0:64], idf[:]), inc="t1")
    p.wait("vector", "t1", 2)
    p.op("vector", lambda e: e.tensor_copy(lamQ16T[:], pT[:]), inc="vq")
    p.op("vector", lambda e: e.memset(pwt[:], 1.0))
    for t in range(16):
        p.op("vector", lambda e, t=t: e.tensor_copy(
            bass.AP(lqp, t, [[2048, 64], [16, 128]]), pwt[:]))
        if t < 15:
            p.op("vector", lambda e: e.tensor_mul(pwt[:], pwt[:], lamQT[:]))
    p.op("vector", lambda e: e.engine_nop(), inc="tabv")

    def rhs_x(ch):
        return bass.AP(xraw, ch, [[8192, 128], [128, 64]])

    p.op("vector", lambda e: e.memset(S_shift[:], 0.0))
    p.wait("tensor", "ex", 64)
    p.wait("sync", "kr", 8 * 16)
    nblk = 16
    for blk in range(nblk):
        buf = blk % 2
        tq_ = "sync" if blk % 2 == 0 else "scalar"
        if blk >= 2:
            p.wait(tq_, "te1", blk - 1)
        p.dma(tq_, lambda e, blk=blk, buf=buf: e.dma_start(
            T8b[buf][:], bass.AP(krep_d, (ro + blk * 8) * 32768 + 127,
                                 [[255, 128], [32768, 8], [1, 128]])), inc=f"tw{buf}")
        if blk >= 2:
            p.wait("gpsimd", "te1", blk - 1)
        p.wait("gpsimd", "g", 32)
        p.dma("gpsimd", lambda e, blk=blk, buf=buf: e.dma_start(
            W8[buf][:], bass.AP(urev_d, (ro + blk * 8) * 8192,
                                [[64, 128], [8192, 8], [1, 64]])), inc=f"w8{buf}")
        p.wait("tensor", f"tw{buf}", 16 * (blk // 2 + 1))
        p.wait("tensor", f"w8{buf}", 16 * (blk // 2 + 1))
        if blk >= 1:
            p.wait("tensor", "ev1", 2 * blk)
        for i in range(8):
            ch = blk * 8 + i
            p.op("tensor", lambda e, i=i, buf=buf, ch=ch: e.matmul(
                pY[:, i * 64:(i + 1) * 64], T8b[buf][:, i * 128:(i + 1) * 128],
                rhs_x(ch), start=True, stop=True))
            p.op("tensor", lambda e, i=i, buf=buf, ch=ch: e.matmul(
                pP[:, i * 64:(i + 1) * 64], W8[buf][:, i * 64:(i + 1) * 64],
                rhs_x(ch), start=True, stop=True),
                inc=("te1" if i == 7 else None))
        p.wait("vector", "te1", blk + 1)
        p.op("vector", lambda e, blk=blk: e.tensor_copy(
            y_sb[:, blk * 512:(blk + 1) * 512], pY[:]), inc="ev1")
        p.op("vector", lambda e, blk=blk: e.tensor_copy(
            P_all[0:64, blk * 512:(blk + 1) * 512], pP[:]), inc="ev1")
        for d in range(blk * 8, blk * 8 + 8):
            p.op("vector", lambda e, d=d: e.tensor_tensor_scan(
                S_shift[0:64, d * 65 + 1:d * 65 + 65],
                lamQT[:, d:d + 1].broadcast_to((64, 64)),
                P_all[0:64, d * 64:(d + 1) * 64],
                0.0, ALU.mult, ALU.add))

    p.wait("vector", "tabv", 1)
    for j in range(4):
        p.op("vector", lambda e, j=j: e.tensor_copy(
            EndA[:, j * 128:(j + 1) * 128],
            bass.AP(S_shift, 16 * j + 16, [[8320, 64], [65, 128]])))
    p.op("vector", lambda e: e.memset(Dd[:], 0.0))
    for pas in range(2):
        if pas == 1:
            p.op("vector", lambda e: e.tensor_scalar_mul(
                Dd[:, 0:128], Outq[:], gm[:, 0:1]))
        for j in range(4):
            p.op("vector", lambda e, j=j: e.tensor_mul(
                Outq[:], lamQ16T[:], Dd[:, j * 128:(j + 1) * 128]))
            p.op("vector", lambda e, j=j: e.tensor_add(
                Outq[:], Outq[:], EndA[:, j * 128:(j + 1) * 128]))
            if j < 3:
                p.op("vector", lambda e, j=j: e.tensor_scalar_mul(
                    Dd[:, (j + 1) * 128:(j + 2) * 128], Outq[:], gm[:, j + 1:j + 2]))
                p.op("vector", lambda e, j=j: e.tensor_sub(
                    Dd[:, (j + 1) * 128:(j + 2) * 128],
                    Dd[:, (j + 1) * 128:(j + 2) * 128],
                    EndA[:, j * 128:(j + 1) * 128]))
    for j in range(4):
        p.op("vector", lambda e, j=j: e.tensor_tensor(
            ctmp[:, 0:2048], lqp[:],
            bass.AP(grid, 5012 + j * 128, [[8256, 64], [1, 128], [0, 16]]), ALU.mult))
        p.op("vector", lambda e, j=j: e.tensor_tensor(
            bass.AP(S_shift, 16 * j, [[8320, 64], [65, 128], [1, 16]]),
            bass.AP(S_shift, 16 * j, [[8320, 64], [65, 128], [1, 16]]),
            ctmp[:, 0:2048], ALU.add))
    p.op("vector", lambda e: e.tensor_copy(S_bf, S_shift[:]), inc="scan")

    for blk in range(nblk):
        buf = blk % 2
        if blk == 0:
            p.wait("scalar", "g", 48)
        if blk >= 2:
            p.wait("scalar", "teM", blk - 1)
        p.dma("scalar", lambda e, blk=blk, buf=buf: e.dma_start(
            M8[buf][:], bass.AP(v_d, (ro + blk * 8) * 8256 + 1,
                                [[129, 64], [8256, 8], [1, 128]])), inc=f"md{buf}")
        p.wait("tensor", f"md{buf}", 16 * (blk // 2 + 1))
        p.wait("tensor", "scan", 1)
        if blk >= 1:
            p.wait("tensor", "evM", blk)
        for i in range(8):
            ch = blk * 8 + i
            p.op("tensor", lambda e, i=i, buf=buf, ch=ch: e.matmul(
                pY[:, i * 64:(i + 1) * 64], M8[buf][:, i * 128:(i + 1) * 128],
                S_bf[0:64, ch * 65:ch * 65 + 64], start=True, stop=True),
                inc=("teM" if i == 7 else None))
        p.wait("vector", "teM", blk + 1)
        p.op("vector", lambda e, blk=blk: e.tensor_add(
            y_sb[:, blk * 512:(blk + 1) * 512],
            y_sb[:, blk * 512:(blk + 1) * 512], pY[:]), inc="evM")

    out_r = xraw[:].rearrange("p (c d) -> p c d", c=64)
    in_r = bass.AP(y_sb, 0, [[8192, 128], [1, 64], [64, 128]])
    p.wait("vector", "evM", nblk)
    p.op("vector", lambda e, o=out_r, i=in_r: e.tensor_copy(o, i), inc="rs")
    p.wait("sync", "rs", 1)
    dst = bass.AP(y_d, ro, [[1024, 128], [128 * 1024, 64], [1, 128]])
    p.dma("sync", lambda e, dst=dst: e.dma_start(
        dst, xraw[:].rearrange("p (c d) -> p c d", c=64)), inc="sty")


def emit_p3_stripe(p, io, pp, s4):
    nc = p.nc
    acc_d, y_d = io["acc"], io["y"]
    wT_d, ib_d, lw_d, lb_d = io["in_projT"], io["in_proj_b"], io["ln_w"], io["ln_b"]
    x1_d, xn1_d, xn2_d = io["x1"], io["xn1"], io["xn2"]
    q_d, K_d, V_d, idb_d = io["q"], io["K"], io["V"], io["ident_b"]
    pbig1, pbig2 = pp["PB1"], pp["PB2"]
    co = s4 * T

    x1_sb = p.sb("x1_sb", [128, NCH * T], F32)
    xn1_sb = p.sb("xn1_sb", [128, NCH * T], BF16)
    xn2_sb = p.sb("xn2_sb", [128, NCH * T], BF16)
    x1_ch = [x1_sb[:, c * T:(c + 1) * T] for c in range(NCH)]
    xn1_ch = [xn1_sb[:, c * T:(c + 1) * T] for c in range(NCH)]
    xn2_ch = [xn2_sb[:, c * T:(c + 1) * T] for c in range(NCH)]
    lw_sb = p.sb("lw_sb", [128, NCH], F32)
    lb_sb = p.sb("lb_sb", [128, NCH], F32)
    qb_sb = p.sb("qb_sb", [128, 16], F32)
    idb = p.sb("idb", [128, 128], BF16)
    yst = [p.sb(f"yst{i}", [128, 16 * 128], BF16) for i in range(2)]
    wqk = [p.sb(f"wqk{i}", [128, NCH * 128], BF16) for i in range(2)]
    wv_sb = p.sb("wv_sb", [128, NCH * D], BF16)
    ev = [p.sb(f"ev{i}", [128, 1024], BF16) for i in range(2)]
    pTRb = [pbig2[:, 0:64].bitcast(BF16), pbig2[:, 512:576].bitcast(BF16)]

    for c in range(NCH):
        p.dma("sync", lambda e, c=c: e.dma_start(
            x1_ch[c], acc_d.ap()[c * 128:(c + 1) * 128, co:co + T]), inc="ldx1")
    p.dma("sync", lambda e: e.dma_start(idb[:], idb_d.ap()), inc="ldb")
    for c in range(NCH):
        p.dma("sync", lambda e, c=c: e.dma_start(lw_sb[:, c:c + 1], lw_d.ap()[c * 128:(c + 1) * 128, :]), inc="ldb")
        p.dma("sync", lambda e, c=c: e.dma_start(lb_sb[:, c:c + 1], lb_d.ap()[c * 128:(c + 1) * 128, :]), inc="ldb")
    for c in range(16):
        p.dma("sync", lambda e, c=c: e.dma_start(qb_sb[:, c:c + 1], ib_d.ap()[c * 128:(c + 1) * 128, :]), inc="ldb")
    p.dma("gpsimd", lambda e: e.dma_start(
        wv_sb[:].rearrange("p (c f) -> p c f", c=NCH),
        bass.AP(wT_d, 2 * D, [[3 * D, 128], [128 * 3 * D, NCH], [1, D]])), inc="gwv")

    p.wait("vector", "ldx1", NCH * 16)
    p.wait("tensor", "ldb", 33 * 16)
    for c in range(NCH):
        buf = c % 2
        if c >= 2:
            p.wait("scalar", "jt", (c - 1) * 16)
        src = bass.AP(y_d, co * D + c * 128, [[D, 128], [128 * D, 16], [1, 128]])
        p.dma("scalar", lambda e, src=src, buf=buf: e.dma_start(
            yst[buf][:].rearrange("p (t f) -> p t f", t=16), src), inc=f"jy{buf}")
        p.wait("tensor", f"jy{buf}", 16 * (c // 2 + 1))
        for tt in range(16):
            nj = c * 16 + tt
            if nj >= 2:
                p.wait("tensor", "ja", nj - 1)
            p.op("tensor", lambda e, buf=buf, tt=tt, nj=nj: e.transpose(
                pTRb[nj % 2], yst[buf][:, tt * 128:(tt + 1) * 128], idb[:]), inc="jt")
            p.wait("vector", "jt", nj + 1)
            p.op("vector", lambda e, c=c, tt=tt, nj=nj: e.tensor_add(
                x1_ch[c][:, tt * 128:(tt + 1) * 128],
                x1_ch[c][:, tt * 128:(tt + 1) * 128],
                pTRb[nj % 2]), inc="ja")
    lnsc = make_ln_scratch(p)
    bc1 = xn2_sb[:].bitcast(F32)[:, 0:2 * T]
    ln1_done = emit_ln_fm(p, "n1", x1_ch, xn1_ch, pbig1, pbig2, lnsc,
                          x_ready=("ja", nj), bc_sb=bc1)
    lnw = [lw_sb[:, c:c + 1] for c in range(NCH)]
    lnb = [lb_sb[:, c:c + 1] for c in range(NCH)]
    p.wait("vector", "ldb", 33 * 16)
    bc2 = None
    ln2_done = emit_ln_fm(p, "n2", xn1_ch, xn2_ch, pbig1, pbig2, lnsc,
                          lnw=lnw, lnb=lnb, x_ready=("n1_dve", ln1_done),
                          x_prod=("n1_dve", ln1_done - 16, 2), bc_sb=bc2)

    if s4 == 0:
        p.wait("sync", "n1_dve", ln1_done)
        for c in range(NCH):
            p.dma("sync", lambda e, c=c: e.dma_start(
                xn1_d.ap()[c * 128:(c + 1) * 128, :], xn1_ch[c]), inc="st1")
            p.dma("sync", lambda e, c=c: e.dma_start(
                x1_d.ap()[c * 128:(c + 1) * 128, :], x1_ch[c]), inc="st1")
        p.wait("sync", "n2_dve", ln2_done)
        for c in range(NCH):
            p.dma("sync", lambda e, c=c: e.dma_start(
                xn2_d.ap()[c * 128:(c + 1) * 128, :], xn2_ch[c]), inc="st1")

    ps_qk = [pbig1[:, i * 512:(i + 1) * 512] for i in range(4)]
    oc_list = list(range(16)) if s4 == 0 else list(range(8, 16))
    for idx, oc in enumerate(oc_list):
        buf = idx % 2
        if idx >= 2:
            p.wait("gpsimd", "teqk", idx - 1)
        p.dma("gpsimd", lambda e, oc=oc, buf=buf: e.dma_start(
            wqk[buf][:].rearrange("p (c f) -> p c f", c=NCH),
            bass.AP(wT_d, oc * 128, [[3 * D, 128], [128 * 3 * D, NCH], [1, 128]])), inc=f"gqk{buf}")
        p.wait("tensor", f"gqk{buf}", 16 * (idx // 2 + 1))
        if (bc2 is None) or idx > 0:
            p.wait("tensor", "n2_dve", ln2_done)
        if idx >= 1:
            p.wait("tensor", "evqk", (idx - 1) * 4 + 4)
        for dc in range(NCH):
            if bc2 is not None and idx == 0:
                p.wait("tensor", "n2_dve", ln2_done - 3 * (NCH - 1 - dc))
            for s in range(4):
                p.op("tensor", lambda e, s=s, dc=dc, buf=buf: e.matmul(
                    ps_qk[s], wqk[buf][:, dc * 128:(dc + 1) * 128],
                    xn2_ch[dc][:, s * 512:(s + 1) * 512],
                    start=(dc == 0), stop=(dc == NCH - 1)),
                    inc=("teqk" if (dc == NCH - 1 and s == 3) else None))
        p.wait("vector", "teqk", p.count("teqk"))
        for s in range(4):
            nev = idx * 4 + s
            if nev >= 2:
                p.wait("vector", f"stq{nev % 2}", 16 * (nev // 2))
            p.op("vector", lambda e, oc=oc, s=s, buf=buf: e.tensor_scalar_add(
                ev[buf][:, (s % 2) * 512:(s % 2 + 1) * 512],
                ps_qk[s], qb_sb[:, oc:oc + 1]), inc="evqk")
            if oc < 8:
                tgt, row, coo = q_d, oc * 128, 0
            else:
                tgt, row, coo = K_d, (oc - 8) * 128, co
            p.wait("sync", "evqk", p.count("evqk"))
            p.dma("sync", lambda e, tgt=tgt, row=row, s=s, buf=buf, coo=coo: e.dma_start(
                tgt.ap()[row:row + 128, coo + s * 512:coo + (s + 1) * 512],
                ev[buf][:, (s % 2) * 512:(s % 2 + 1) * 512]), inc=f"stq{(idx * 4 + s) % 2}")

    for tt in range(16):
        buf = tt % 2
        p.wait("tensor", "gwv", 16)
        if tt >= 1:
            p.wait("tensor", "evv", tt)
        for dc in range(NCH):
            for h in range(2):
                p.op("tensor", lambda e, tt=tt, dc=dc, h=h: e.matmul(
                    pbig2[:, h * 512:(h + 1) * 512],
                    xn2_ch[dc][:, tt * 128:(tt + 1) * 128],
                    wv_sb[:, dc * D + h * 512: dc * D + (h + 1) * 512],
                    start=(dc == 0), stop=(dc == NCH - 1)),
                    inc=("tev" if dc == NCH - 1 and h == 1 else None))
        p.wait("vector", "tev", p.count("tev"))
        if tt >= 2:
            p.wait("vector", f"sv{buf}", 16 * (tt // 2))
        p.op("vector", lambda e, buf=buf: e.tensor_copy(ev[buf][:], pbig2[:, 0:1024]), inc="evv")
        p.wait("sync", "evv", p.count("evv"))
        p.dma("sync", lambda e, tt=tt, buf=buf: e.dma_start(
            V_d.ap()[co + tt * 128:co + (tt + 1) * 128, :], ev[buf][:]), inc=f"sv{buf}")


def build_fused(dbg=False):
    nc = bass.Bass("TRN2", target_bir_lowering=False, debug=False)
    p = Prog(nc)
    EI, INT = "ExternalInput", "Internal"
    io = dict(
        x=nc.dram_tensor("x_rot_fm", [D, T8], F32, kind=EI),
        wgT=nc.dram_tensor("wgT", [D, HID0], I32, kind=EI),
        wuT=nc.dram_tensor("wuT", [D, HID0], I32, kind=EI),
        wdT=nc.dram_tensor("wdT", [HID0, D], I32, kind=EI),
        log_dt=nc.dram_tensor("log_dt", [D, 1], F32, kind=EI),
        Bmat=nc.dram_tensor("Bmat", [D, 64], F32, kind=EI),
        Cmat=nc.dram_tensor("Cmat", [D, 64], F32, kind=EI),
        Dvec=nc.dram_tensor("Dvec", [D, 1], F32, kind=EI),
        A_bc=nc.dram_tensor("A_bc", [128, 64], F32, kind=EI),
        invA_bc=nc.dram_tensor("invA_bc", [128, 64], F32, kind=EI),
        ident_b=nc.dram_tensor("ident_b", [128, 128], BF16, kind=EI),
        jrev=nc.dram_tensor("jrev", [128, 128], F32, kind=EI),
        jv129=nc.dram_tensor("jv129", [128, 129], F32, kind=EI),
        gmask=nc.dram_tensor("gmask", [64, 4], F32, kind=EI),
        in_projT=nc.dram_tensor("in_projT", [D, 3 * D], F32, kind=EI),
        in_proj_b=nc.dram_tensor("in_proj_b", [3 * D, 1], F32, kind=EI),
        ln_w=nc.dram_tensor("ln_w", [D, 1], F32, kind=EI),
        ln_b=nc.dram_tensor("ln_b", [D, 1], F32, kind=EI),
        out_projT=nc.dram_tensor("out_projT", [D, D], F32, kind=EI),
        out_proj_b=nc.dram_tensor("out_proj_b", [D, 1], F32, kind=EI),
        v_bias=nc.dram_tensor("v_bias", [D, 1], F32, kind=EI),
        w0T=nc.dram_tensor("w0T", [4 * 256, 256], F32, kind=EI),
        w2T=nc.dram_tensor("w2T", [4 * 256, 256], F32, kind=EI),
        w1l=nc.dram_tensor("w1l", [4 * 256, 256], F32, kind=EI),
        w1T=nc.dram_tensor("w1T", [4 * 256, 256], F32, kind=EI),
        gate_wT=nc.dram_tensor("gate_wT", [D, 8], F32, kind=EI),
        bal_b=nc.dram_tensor("bal_b", [1, 8], F32, kind=EI),
        moew1T=nc.dram_tensor("moew1T", [8 * D, 2048], F32, kind=EI),
        moew2T=nc.dram_tensor("moew2T", [8 * 2048, D], F32, kind=EI),
        xn=nc.dram_tensor("xn_i", [D, T8], BF16, kind=INT),
        acc=nc.dram_tensor("acc_i", [D, T8], F32, kind=INT),
        krep=nc.dram_tensor("krep_i", [D, 32768], BF16, kind=INT),
        urev=nc.dram_tensor("urev_i", [D, 8192], BF16, kind=INT),
        vdram=nc.dram_tensor("vdram_i", [D, 8256], BF16, kind=INT),
        y=nc.dram_tensor("y_i", [T8, D], BF16, kind=INT),
        x1=nc.dram_tensor("x1_i", [D, T], F32, kind=INT),
        xn1=nc.dram_tensor("xn1_i", [D, T], BF16, kind=INT),
        xn2=nc.dram_tensor("xn2_i", [D, T], BF16, kind=INT),
        q=nc.dram_tensor("q_i", [D, T], BF16, kind=INT),
        K=nc.dram_tensor("K_i", [D, T8], BF16, kind=INT),
        V=nc.dram_tensor("V_i", [T8, D], BF16, kind=INT),
        y1=nc.dram_tensor("y1_i", [D, T], F32, kind=INT),
        mw1b=nc.dram_tensor("mw1b", [8 * D, 2048], BF16, kind=INT),
        mw2b=nc.dram_tensor("mw2b", [8 * 2048, D], BF16, kind=INT),
        wc=nc.dram_tensor("wc_dram", [8, T], F32, kind=INT),
        out=nc.dram_tensor("out_fm", [D, T], F32, kind="ExternalOutput"),
    )
    PB1 = p.ps("PB1", [128, 2048], F32)
    PB2 = p.ps("PB2", [128, 2048], F32)
    ppA = dict(PB1=PB1, PB2=PB2)
    pp4 = dict(psA=PB1[:, 0:1024], psB=PB1[:, 1024:2048],
               psE=PB2[0:65, 0:512], psF=PB2[0:65, 512:1024],
               psC=PB2[:, 1024:1536], psD=PB2[:, 1536:2048])
    pp5 = dict(psA=PB1[:, 0:1024], psB=PB1[:, 1024:2048],
               psC=PB2[:, 0:512], psD=PB2[:, 512:1024],
               psT=PB2[:, 1024:1088].bitcast(BF16),
               psT2=PB2[:, 1536:1600].bitcast(BF16))

    if dbg:
        dbg_keys = ["xn", "acc", "y", "x1", "xn1", "xn2", "q", "K", "V", "y1"]
        for kk in dbg_keys:
            t = io[kk]
            io["dbg_" + kk] = nc.dram_tensor("dbg_" + kk, list(t.shape), t.dtype,
                                             kind="ExternalOutput")
    emit_p1(p, io, ppA)
    p.barrier()
    for dt8 in range(NCH):
        emit_p2_tile(p, io, ppA, dt8)
        p.barrier()
    for s4 in range(4):
        emit_p3_stripe(p, io, ppA, s4)
        p.barrier()
    io4 = dict(io)
    emit_p4(p, io4, pp4)
    p.barrier()
    emit_p5(p, io, pp5)
    if dbg:
        p.barrier()
        for kk in dbg_keys:
            p.dma("sync", lambda e, kk=kk: e.dma_start(
                io["dbg_" + kk].ap(), io[kk].ap()), inc="dbgo")
    p.finish_and_emit()
    nins = sum(len(q) for q in p.q.values())
    return nc, nins


def kernel(**inputs):
    x = np.asarray(inputs["L0_attn_x"])
    wg = np.asarray(inputs["L0_ffn_w_gate"]); wu = np.asarray(inputs["L0_ffn_w_up"])
    wd = np.asarray(inputs["L0_ffn_w_down"])
    log_dt = np.asarray(inputs["L0_attn_log_dt"])
    Bm = np.asarray(inputs["L0_attn_B"]); Cm = np.asarray(inputs["L0_attn_C"])
    Dv = np.asarray(inputs["L0_attn_D"])
    lnw = np.asarray(inputs["L1_attn_ln_w"]); lnb = np.asarray(inputs["L1_attn_ln_b"])
    ipw = np.asarray(inputs["L1_attn_in_proj_w"]); ipb = np.asarray(inputs["L1_attn_in_proj_b"])
    opw = np.asarray(inputs["L1_attn_out_proj_w"]); opb = np.asarray(inputs["L1_attn_out_proj_b"])
    w0 = np.asarray(inputs["L1_attn_w0"]); w1 = np.asarray(inputs["L1_attn_w1"])
    w2 = np.asarray(inputs["L1_attn_w2"])
    gw = np.asarray(inputs["L1_ffn_gate_w"]); bal = np.asarray(inputs["L1_ffn_balance_bias"])
    mw1 = np.asarray(inputs["L1_ffn_w1"]); mw2 = np.asarray(inputs["L1_ffn_w2"])

    common = dict(
        wgT=np.ascontiguousarray(wg.T), wuT=np.ascontiguousarray(wu.T),
        wdT=np.ascontiguousarray(wd.T),
        log_dt=log_dt.reshape(-1, 1), Bmat=Bm, Cmat=Cm, Dvec=Dv.reshape(-1, 1),
        A_bc=np.tile((-0.5 * np.arange(1, 65)).astype(np.float32), (128, 1)),
        invA_bc=np.tile((1.0 / (-0.5 * np.arange(1, 65))).astype(np.float32), (128, 1)),
        ident_b=np.eye(128, dtype=BF16NP),
        jrev=np.tile((127.0 - np.arange(128)).astype(np.float32), (128, 1)),
        jv129=np.tile(np.arange(129, dtype=np.float32), (128, 1)),
        in_projT=np.ascontiguousarray(ipw.T), in_proj_b=ipb.reshape(-1, 1),
        ln_w=lnw.reshape(-1, 1), ln_b=lnb.reshape(-1, 1),
        out_projT=np.ascontiguousarray(opw.T), out_proj_b=opb.reshape(-1, 1),
        v_bias=ipb[2048:].reshape(-1, 1),
        w0T=np.ascontiguousarray(np.transpose(w0, (0, 2, 1))).reshape(4 * 256, 256),
        w2T=np.ascontiguousarray(np.transpose(w2, (0, 2, 1))).reshape(4 * 256, 256),
        w1l=np.ascontiguousarray(w1).reshape(4 * 256, 256),
        w1T=np.ascontiguousarray(np.transpose(w1, (0, 2, 1))).reshape(4 * 256, 256),
        gate_wT=np.ascontiguousarray(gw.T), bal_b=bal.reshape(1, 8),
        moew1T=np.ascontiguousarray(np.transpose(mw1, (0, 2, 1))).reshape(8 * 1024, 2048),
        moew2T=np.ascontiguousarray(np.transpose(mw2, (0, 2, 1))).reshape(8 * 2048, 1024),
    )
    in_maps = []
    for c in range(8):
        b, r = c // 4, c % 4
        xb = x[b]
        xrot = np.concatenate([xb[r * T:], xb[:r * T]], axis=0)
        g = np.ones(4, np.float32)
        g[(4 - r) % 4] = 0.0
        m = dict(common)
        m["x_rot_fm"] = np.ascontiguousarray(xrot.T)
        m["gmask"] = np.tile(g, (64, 1))
        in_maps.append(m)
    nc, nins = build_fused(dbg=bool(inputs.get("_dbg")))
    res = run_bass_kernel_spmd(nc, in_maps, core_ids=list(range(8)))
    if inputs.get("_dbg"):
        kernel._dbg_res = res
    out = np.zeros((2, T8, D), np.float32)
    for c in range(8):
        b, r = c // 4, c % 4
        out[b, r * T:(r + 1) * T] = np.asarray(res.results[c]["out_fm"]).T
    return np.ascontiguousarray(out)


# version 27
# speedup vs baseline: 1.1578x; 1.0169x over previous
import numpy as np
import ml_dtypes
import concourse.bass as bass
import concourse.mybir as mybir
from concourse.bass_utils import run_bass_kernel_spmd

F32 = mybir.dt.float32
BF16 = mybir.dt.bfloat16
I32 = mybir.dt.int32
AF = mybir.ActivationFunctionType
ALU = mybir.AluOpType
AX = mybir.AxisListType
BF16NP = ml_dtypes.bfloat16

D = 1024
T = 2048
NCH = 8
HID0 = 3072
NH0 = HID0 // 128
QSC = 1.0 / 127.0
EPS = 1e-5


class Prog:
    ENGINES = ("sync", "vector", "scalar", "tensor", "gpsimd")

    def __init__(self, nc):
        self.nc = nc
        self.q = {e: [] for e in self.ENGINES}
        self.pool = []
        self.lmap = {}
        self._tensors = {}
        self._uid = 0
        self.nbar = 0
        self.bar = None
        self.arena0 = None
        self.arena_off = 0
        self.arena_hwm = 0

    def sb(self, name, shape, dtype=F32):
        if self.arena0 is None:
            self.arena0 = (self.nc.sbuf_base + 63) & ~63
        esz = {F32: 4, BF16: 2, I32: 4}[dtype]
        nbytes = int(shape[1]) * esz
        off = (self.arena0 + self.arena_off + 63) & ~63
        self._uid += 1
        t = self.nc.alloc_sbuf_tensor_at(f"{name}_u{self._uid}", list(shape), dtype,
                                         offset=off)
        self.arena_off = off + nbytes - self.arena0
        self.arena_hwm = max(self.arena_hwm, self.arena_off)
        assert self.arena0 + self.arena_off <= self.nc.sbuf_top - 64, \
            f"arena overflow at {name}: {self.arena0 + self.arena_off}"
        self._tensors[name] = t
        return t

    def ps(self, name, shape, dtype=F32):
        t = self.nc.alloc_psum_tensor(name, list(shape), dtype)
        self._tensors[name] = t
        return t

    def _sem(self, name):
        if name not in self.lmap:
            idx = len(self.lmap)
            if idx >= len(self.pool):
                self.pool.append([self.nc.alloc_semaphore(f"ps{len(self.pool)}"), 0])
            self.lmap[name] = (idx, self.pool[idx][1])
        idx, base = self.lmap[name]
        return self.pool[idx], base

    def count(self, name):
        s, base = self._sem(name)
        return s[1] - base

    def wait(self, eng, semname, val=None):
        s, base = self._sem(semname)
        v = (s[1] - base) if val is None else val
        if v > 0:
            self.q[eng].append(("w", s[0], v + base))

    def op(self, eng, fn, inc=None, n=1):
        if inc is not None:
            s, _ = self._sem(inc)
            s[1] += n
            self.q[eng].append(("o", fn, s[0], n))
        else:
            self.q[eng].append(("o", fn, None, 0))

    def dma(self, eng, fn, inc):
        self.op(eng, fn, inc=inc, n=16)

    def wait_abs(self, eng, name):
        s, _ = self._sem(name)
        if s[1] > 0:
            self.q[eng].append(("w", s[0], s[1]))

    def barrier(self, skip=()):
        if self.bar is None:
            self.bar = self.nc.alloc_semaphore("bar")
        skip_idx = {self.lmap[n][0] for n in skip if n in self.lmap}
        for i, (h, c) in enumerate(self.pool):
            if c > 0 and i not in skip_idx:
                self.q["sync"].append(("w", h, c))
        self.nbar += 1
        for eng in self.ENGINES:
            self.q[eng].append(("o", (lambda e: e.nop()), self.bar, 1))
        for eng in self.ENGINES:
            self.q[eng].append(("w", self.bar, 5 * self.nbar))
        self.lmap = {}
        self.arena_off = 0

    def finish_and_emit(self):
        for h, c in self.pool:
            if c > 0:
                self.q["sync"].append(("w", h, c))
        nc = self.nc
        with nc.Block() as block:
            for eng_name, dec in (("sync", block.sync), ("vector", block.vector),
                                  ("scalar", block.scalar), ("tensor", block.tensor),
                                  ("gpsimd", block.gpsimd)):
                items = self.q[eng_name]
                if not items:
                    continue

                def body(e, items=items):
                    for it in items:
                        if it[0] == "w":
                            e.wait_ge(it[1], it[2])
                        else:
                            ins = it[1](e)
                            if it[2] is not None:
                                ins.then_inc(it[2], it[3])
                dec(body)


def make_ln_scratch(p, pre="lns"):
    rows1 = p.sb(pre + "_rows1", [33, T], F32)
    rows2 = p.sb(pre + "_rows2", [33, T], F32)
    return dict(
        ones_b=p.sb(pre + "_onesb", [128, 1], BF16),
        ones_f=p.sb(pre + "_onesf", [1, 128], F32),
        sq=[p.sb(pre + f"_sq{i}", [128, T], BF16) for i in range(2)],
        rowm=rows1[0:1, :],
        rowq=rows1[32:33, :],
        rowt=rows2[32:33, :],
        rowr=rows2[0:1, :],
        epst=p.sb(pre + "_eps", [1, 1], F32),
        init=[False],
    )


def emit_ln_fm(p, pre, x_ch, xn_ch, ps1, ps2, sc, lnw=None, lnb=None, x_ready=None,
               x_chunks=None, bc_sb=None, x_prod=None):
    nc = p.nc
    ones_b = sc["ones_b"]; ones_f = sc["ones_f"]; sq = sc["sq"]
    rowm = sc["rowm"]; rowq = sc["rowq"]; rowr = sc["rowr"]
    rowt = sc["rowt"]; epst = sc["epst"]
    s_te, s_act, s_dve = pre + "_te", pre + "_act", pre + "_dve"
    NS = T // 512
    in_bf = (x_ch[0].dtype == BF16)

    xb = x_ch if in_bf else xn_ch
    if not sc["init"][0]:
        sc["init"][0] = True
        p.op("vector", lambda e: e.memset(ones_b[:], 1.0), inc=s_dve)
        p.op("vector", lambda e: e.memset(ones_f[:], 1.0), inc=s_dve)
        p.op("vector", lambda e: e.memset(epst[:], EPS), inc=s_dve)
    cast_start = p.count(s_dve)
    if x_chunks is not None:
        xname, xbase = x_chunks
        if not in_bf:
            for c in range(NCH):
                p.wait("vector", xname, xbase + 16 * (c + 1))
                p.op("vector", lambda e, c=c: e.tensor_copy(xb[c], x_ch[c]), inc=s_dve)
    else:
        if x_ready is not None:
            p.wait("vector", *x_ready)
        if not in_bf:
            for c in range(NCH):
                p.op("vector", lambda e, c=c: e.tensor_copy(xb[c], x_ch[c]), inc=s_dve)
    casts_done = p.count(s_dve)

    if in_bf and x_ready is not None and x_prod is None:
        p.wait("tensor", *x_ready)
        p.wait("scalar", *x_ready)
    for s in range(NS):
        for c in range(NCH):
            if not in_bf:
                p.wait("tensor", s_dve, cast_start + c + 1)
            elif x_prod is not None and s == 0:
                p.wait("tensor", x_prod[0], x_prod[1] + x_prod[2] * (c + 1))
            p.op("tensor",
                 lambda e, s=s, c=c: e.matmul(ps1[0:1, s * 512:(s + 1) * 512],
                                              ones_b[:], xb[c][:, s * 512:(s + 1) * 512],
                                              start=(c == 0), stop=(c == NCH - 1)),
                 inc=(s_te if (s == NS - 1 and c == NCH - 1) else None))
    for c in range(NCH):
        if not in_bf:
            p.wait("scalar", s_dve, cast_start + c + 1)
        elif x_prod is not None:
            p.wait("scalar", x_prod[0], x_prod[1] + x_prod[2] * (c + 1))
        else:
            p.wait("scalar", s_dve, casts_done)
        if c >= 2:
            p.wait("scalar", s_te, 1 + (c - 1))
        p.op("scalar", lambda e, c=c: e.activation(sq[c % 2][:], xb[c], AF.Square),
             inc=s_act)
        p.wait("tensor", s_act, p.count(s_act))
        for s in range(NS):
            p.op("tensor",
                 lambda e, s=s, c=c: e.matmul(ps1[32:33, s * 512:(s + 1) * 512],
                                              ones_b[:], sq[c % 2][:, s * 512:(s + 1) * 512],
                                              start=(c == 0), stop=(c == NCH - 1)),
                 inc=(s_te if s == NS - 1 else None))
    p.wait("vector", s_te, p.count(s_te))
    p.op("vector", lambda e: e.tensor_scalar_mul(rowm, ps1[0:1, :], 1.0 / D), inc=s_dve)
    p.op("vector", lambda e: e.tensor_scalar_mul(rowq, ps1[32:33, :], 1.0 / D), inc=s_dve)
    p.wait("scalar", s_dve, p.count(s_dve))
    p.op("scalar", lambda e: e.activation(rowt, rowm, AF.Square), inc=s_act)
    p.wait("vector", s_act, p.count(s_act))
    p.op("vector", lambda e: e.tensor_sub(rowq, rowq, rowt), inc=s_dve)
    p.wait("scalar", s_dve, p.count(s_dve))
    p.op("scalar", lambda e: e.activation(rowt, rowq, AF.Sqrt, bias=epst[:]), inc=s_act)
    p.wait("vector", s_act, p.count(s_act))
    p.op("vector", lambda e: e.reciprocal(rowr, rowt), inc=s_dve)
    fin_done = p.count(s_dve)
    p.wait("tensor", s_dve, fin_done)
    for s in range(NS):
        p.op("tensor", lambda e, s=s: e.matmul(ps2[:, s * 512:(s + 1) * 512], ones_f[:],
                                               rowr[:, s * 512:(s + 1) * 512],
                                               start=True, stop=True),
             inc=(s_te if s == NS - 1 else None))
    for s in range(NS):
        p.op("tensor", lambda e, s=s: e.matmul(ps1[:, s * 512:(s + 1) * 512], ones_f[:],
                                               rowm[:, s * 512:(s + 1) * 512],
                                               start=True, stop=True),
             inc=(s_te if s == NS - 1 else None))
    bc_done = p.count(s_te)
    p.wait("vector", s_te, bc_done)
    if bc_sb is not None:
        bcm, bcr = bc_sb[:, 0:T], bc_sb[:, T:2 * T]
        p.op("vector", lambda e, bcm=bcm: e.tensor_copy(bcm, ps1[:]), inc=s_dve)
        p.op("vector", lambda e, bcr=bcr: e.tensor_copy(bcr, ps2[:]), inc=s_dve)
    else:
        bcm, bcr = ps1[:], ps2[:]
    for c in range(NCH):
        p.op("vector", lambda e, c=c, bcm=bcm: e.tensor_sub(xn_ch[c], x_ch[c], bcm), inc=s_dve)
        p.op("vector", lambda e, c=c, bcr=bcr: e.tensor_mul(xn_ch[c], xn_ch[c], bcr), inc=s_dve)
        if lnw is not None:
            p.op("vector", lambda e, c=c: e.tensor_scalar(xn_ch[c], xn_ch[c],
                                                          lnw[c], lnb[c],
                                                          ALU.mult, ALU.add), inc=s_dve)
    return p.count(s_dve)


def build_l1():
    nc = bass.Bass("TRN2", target_bir_lowering=False, debug=False)
    p = Prog(nc)
    x_d = nc.dram_tensor("x_fm", [D, T], F32, kind="ExternalInput")
    wgT_d = nc.dram_tensor("wgT", [D, HID0], I32, kind="ExternalInput")
    wuT_d = nc.dram_tensor("wuT", [D, HID0], I32, kind="ExternalInput")
    wdT_d = nc.dram_tensor("wdT", [HID0, D], I32, kind="ExternalInput")
    xn_d = nc.dram_tensor("xn_fm", [D, T], BF16, kind="ExternalOutput")
    acc_d = nc.dram_tensor("acc_fm", [D, T], F32, kind="ExternalOutput")

    pbig1 = p.ps("pbig1", [128, T], F32)
    pbig2 = p.ps("pbig2", [128, T], F32)
    x_sb = p.sb("x_sb", [128, NCH * T], F32)
    xn_sb = p.sb("xn_sb", [128, NCH * T], BF16)
    x_ch = [x_sb[:, c * T:(c + 1) * T] for c in range(NCH)]
    xn_ch = [xn_sb[:, c * T:(c + 1) * T] for c in range(NCH)]

    for c in range(NCH):
        p.dma("sync", lambda e, c=c: e.dma_start(x_ch[c], x_d.ap()[c * 128:(c + 1) * 128, :]),
              inc="ld_x")
    lnsc = make_ln_scratch(p)
    ln_done = emit_ln_fm(p, "ln0", x_ch, xn_ch, pbig1, pbig2, lnsc, x_ready=("ld_x", 8 * 16))

    wg_sb = [p.sb(f"wg{i}", [128, NCH * 128], BF16) for i in range(2)]
    wu_sb = [p.sb(f"wu{i}", [128, NCH * 128], BF16) for i in range(2)]
    wd_sb = [p.sb(f"wd{i}", [128, NH0 * 128], BF16) for i in range(2)]
    h_sb = p.sb("h_sb", [128, NH0 * 512], BF16)
    g_scr = p.sb("g_scr", [128, 512], BF16)
    ps_g = pbig1[:, 0:512]
    ps_u = pbig1[:, 512:1024]
    ps_d = [pbig1[:, 1024:1536], pbig1[:, 1536:2048]]

    n_wdma = 0

    for st in range(4):
        tq = slice(st * 512, (st + 1) * 512)
        for hc in range(NH0):
            buf = hc % 2
            src_g = bass.AP(wgT_d, hc * 128, [[HID0, 128], [128 * HID0, 8], [1, 128]])
            src_u = bass.AP(wuT_d, hc * 128, [[HID0, 128], [128 * HID0, 8], [1, 128]])
            dst_g = wg_sb[buf][:].rearrange("p (c f) -> p c f", c=8)
            dst_u = wu_sb[buf][:].rearrange("p (c f) -> p c f", c=8)
            it = st * NH0 + hc
            if it >= 2:
                p.wait("gpsimd", "te_gu", it - 1)
            p.dma("gpsimd", lambda e, s=src_g, d=dst_g: e.dma_start(d, s), inc=f"wgu{buf}")
            p.dma("gpsimd", lambda e, s=src_u, d=dst_u: e.dma_start(d, s), inc=f"wgu{buf}")
            p.wait("tensor", f"wgu{buf}", 32 * (it // 2 + 1))
            if st == 0 and hc == 0:
                p.wait("tensor", "ln0_dve", ln_done)
            if st * NH0 + hc >= 1:
                p.wait("tensor", "evac_gu", 2 * (st * NH0 + hc - 1) + 2)
            for dc in range(NCH):
                p.op("tensor", lambda e, buf=buf, dc=dc, tq=tq: e.matmul(
                    ps_g, wg_sb[buf][:, dc * 128:(dc + 1) * 128],
                    xn_ch[dc][:, tq], start=(dc == 0), stop=(dc == NCH - 1)))
                p.op("tensor", lambda e, buf=buf, dc=dc, tq=tq: e.matmul(
                    ps_u, wu_sb[buf][:, dc * 128:(dc + 1) * 128],
                    xn_ch[dc][:, tq], start=(dc == 0), stop=(dc == NCH - 1)),
                    inc=("te_gu" if dc == NCH - 1 else None))
            p.wait("scalar", "te_gu", p.count("te_gu"))
            if st >= 1:
                p.wait("scalar", "te_d", st * NCH)
            if st * NH0 + hc >= 1:
                p.wait("scalar", "dve_h", st * NH0 + hc)
            p.op("scalar", lambda e: e.activation(g_scr[:], ps_g[:], AF.Silu, scale=QSC),
                 inc="evac_gu")
            p.op("scalar", lambda e, hc=hc: e.activation(
                h_sb[:, hc * 512:(hc + 1) * 512], ps_u[:], AF.Copy, scale=QSC),
                inc="evac_gu")
            p.wait("vector", "evac_gu", p.count("evac_gu"))
            p.op("vector", lambda e, hc=hc: e.tensor_mul(
                h_sb[:, hc * 512:(hc + 1) * 512], g_scr[:],
                h_sb[:, hc * 512:(hc + 1) * 512]), inc="dve_h")
        for dc in range(NCH):
            buf = dc % 2
            src_d = bass.AP(wdT_d, dc * 128, [[D, 128], [128 * D, NH0], [1, 128]])
            dst_d = wd_sb[buf][:].rearrange("p (c f) -> p c f", c=NH0)
            it = st * NCH + dc
            if it >= 2:
                p.wait("gpsimd", "te_d", it - 1)
            p.dma("gpsimd", lambda e, s=src_d, d=dst_d: e.dma_start(d, s), inc=f"wdd{buf}")
            p.wait("tensor", f"wdd{buf}", 16 * (it // 2 + 1))
            p.wait("tensor", "dve_h", p.count("dve_h"))
            if st * NCH + dc >= 2:
                p.wait("tensor", "dve_acc", (st * NCH + dc - 1))
            for hc in range(NH0):
                p.op("tensor", lambda e, buf=buf, hc=hc, dc=dc: e.matmul(
                    ps_d[dc % 2], wd_sb[buf][:, hc * 128:(hc + 1) * 128],
                    h_sb[:, hc * 512:(hc + 1) * 512],
                    start=(hc == 0), stop=(hc == NH0 - 1)),
                    inc=("te_d" if hc == NH0 - 1 else None))
            p.wait("vector", "te_d", p.count("te_d"))
            p.op("vector", lambda e, dc=dc, st=st: e.scalar_tensor_tensor(
                x_ch[dc][:, st * 512:(st + 1) * 512], ps_d[dc % 2], QSC,
                x_ch[dc][:, st * 512:(st + 1) * 512], ALU.mult, ALU.add),
                inc="dve_acc")

    p.wait("sync", "ln0_dve", ln_done)
    for c in range(NCH):
        p.dma("sync", lambda e, c=c: e.dma_start(xn_d.ap()[c * 128:(c + 1) * 128, :], xn_ch[c]),
              inc="st_xn")
    p.wait("sync", "dve_acc", p.count("dve_acc"))
    for c in range(NCH):
        p.dma("sync", lambda e, c=c: e.dma_start(acc_d.ap()[c * 128:(c + 1) * 128, :], x_ch[c]),
              inc="st_acc")
    p.finish_and_emit()
    return nc


def _fm(a):
    return np.ascontiguousarray(a.T)


def kernel_unfused(**inputs):
    x = np.asarray(inputs["L0_attn_x"])
    wg = np.asarray(inputs["L0_ffn_w_gate"])
    wu = np.asarray(inputs["L0_ffn_w_up"])
    wd = np.asarray(inputs["L0_ffn_w_down"])
    log_dt = np.asarray(inputs["L0_attn_log_dt"])
    Bm = np.asarray(inputs["L0_attn_B"]); Cm = np.asarray(inputs["L0_attn_C"])
    Dv = np.asarray(inputs["L0_attn_D"])
    lnw = np.asarray(inputs["L1_attn_ln_w"]); lnb = np.asarray(inputs["L1_attn_ln_b"])
    ipw = np.asarray(inputs["L1_attn_in_proj_w"]); ipb = np.asarray(inputs["L1_attn_in_proj_b"])
    opw = np.asarray(inputs["L1_attn_out_proj_w"]); opb = np.asarray(inputs["L1_attn_out_proj_b"])
    w0 = np.asarray(inputs["L1_attn_w0"]); w1 = np.asarray(inputs["L1_attn_w1"])
    w2 = np.asarray(inputs["L1_attn_w2"])
    gw = np.asarray(inputs["L1_ffn_gate_w"]); bal = np.asarray(inputs["L1_ffn_balance_bias"])
    mw1 = np.asarray(inputs["L1_ffn_w1"]); mw2 = np.asarray(inputs["L1_ffn_w2"])

    xt = x.reshape(16384, D)
    shards = [xt[i * T:(i + 1) * T] for i in range(8)]
    cores = list(range(8))

    wgT = np.ascontiguousarray(wg.T); wuT = np.ascontiguousarray(wu.T)
    wdT = np.ascontiguousarray(wd.T)
    nc1 = build_l1()
    res = run_bass_kernel_spmd(nc1, [
        {"x_fm": _fm(s), "wgT": wgT, "wuT": wuT, "wdT": wdT} for s in shards],
        core_ids=cores)
    xn_fm = [np.asarray(r["xn_fm"]) for r in res.results]
    acc_fm = [np.asarray(r["acc_fm"]) for r in res.results]

    xn_tok = [np.concatenate([xn_fm[b * 4 + j].T for j in range(4)], axis=0)
              for b in range(2)]
    A = (-0.5 * np.arange(1, 65)).astype(np.float32)
    A_bc = np.tile(A, (128, 1))
    iA_bc = np.tile((1.0 / A).astype(np.float32), (128, 1))
    ident = np.eye(128, dtype=np.float32)
    nc2 = build_l2()
    in2 = []
    for c in range(8):
        sl = slice(c * 128, (c + 1) * 128)
        in2.append({
            "xn0": np.ascontiguousarray(xn_tok[0][:, sl]),
            "xn1": np.ascontiguousarray(xn_tok[1][:, sl]),
            "log_dt": log_dt[sl].reshape(128, 1), "Bmat": Bm[sl], "Cmat": Cm[sl],
            "Dvec": Dv[sl].reshape(128, 1), "A_bc": A_bc, "invA_bc": iA_bc,
            "ident_f": ident})
    res = run_bass_kernel_spmd(nc2, in2, core_ids=cores)
    yssm = [np.concatenate([np.asarray(res.results[c][f"y{b}"]) for c in range(8)], axis=1)
            for b in range(2)]

    ipT = np.ascontiguousarray(ipw.T)
    nc3 = build_l3a()
    in3 = []
    for c in range(8):
        b = c // 4
        sl = slice((c % 4) * T, (c % 4 + 1) * T)
        in3.append({
            "acc_fm": acc_fm[c],
            "yssm_fm": np.ascontiguousarray(yssm[b][sl].T),
            "in_projT": ipT, "in_proj_b": ipb.reshape(-1, 1),
            "ln_w": lnw.reshape(-1, 1), "ln_b": lnb.reshape(-1, 1)})
    res = run_bass_kernel_spmd(nc3, in3, core_ids=cores)
    x1_fm = [np.asarray(r["x1_fm"]) for r in res.results]
    xn1_fm = [np.asarray(r["xn1_fm"]) for r in res.results]
    xn2_fm = [np.asarray(r["xn2_fm"]) for r in res.results]
    q_fm = [np.asarray(r["q_fm"]) for r in res.results]
    k_fm = [np.asarray(r["k_fm"]) for r in res.results]
    v_tok = [np.asarray(r["v_tok"]) for r in res.results]

    K_full = [np.ascontiguousarray(np.concatenate([k_fm[b * 4 + j] for j in range(4)], axis=1))
              for b in range(2)]
    V_full = [np.ascontiguousarray(np.concatenate([v_tok[b * 4 + j] for j in range(4)], axis=0))
              for b in range(2)]
    opT = np.ascontiguousarray(opw.T)
    nc4 = build_l3b1()
    in4 = []
    for c in range(8):
        b = c // 4
        in4.append({
            "x1_fm": x1_fm[c], "q_fm": q_fm[c],
            "K_full": K_full[b], "V_full": V_full[b],
            "out_projT": opT, "out_proj_b": opb.reshape(-1, 1),
            "v_bias": ipb[2048:].reshape(-1, 1)})
    res = run_bass_kernel_spmd(nc4, in4, core_ids=cores)
    y1_fm = [np.asarray(r["y1_fm"]) for r in res.results]

    w0T = np.ascontiguousarray(np.transpose(w0, (0, 2, 1))).reshape(4 * 256, 256)
    w2T = np.ascontiguousarray(np.transpose(w2, (0, 2, 1))).reshape(4 * 256, 256)
    w1l = np.ascontiguousarray(w1).reshape(4 * 256, 256)
    w1T = np.ascontiguousarray(np.transpose(w1, (0, 2, 1))).reshape(4 * 256, 256)
    gwT = np.ascontiguousarray(gw.T)
    mw1T = np.ascontiguousarray(np.transpose(mw1, (0, 2, 1))).reshape(8 * 1024, 2048)
    mw2T = np.ascontiguousarray(np.transpose(mw2, (0, 2, 1))).reshape(8 * 2048, 1024)
    identb = np.eye(128, dtype=BF16NP)
    nc5 = build_l3b2()
    in5 = []
    for c in range(8):
        in5.append({
            "y1_fm": y1_fm[c], "xn1_fm": xn1_fm[c], "xn2_fm": xn2_fm[c],
            "w0T": w0T, "w2T": w2T, "w1l": w1l, "w1T": w1T,
            "gate_wT": gwT, "bal_b": bal.reshape(1, 8),
            "moew1T": mw1T, "moew2T": mw2T, "ident_b": identb})
    res = run_bass_kernel_spmd(nc5, in5, core_ids=cores)
    out = np.concatenate([np.asarray(r["out_fm"]).T for r in res.results], axis=0)
    return np.ascontiguousarray(out.reshape(2, 8192, D).astype(np.float32))


def build_l2(dbg=False):
    nc = bass.Bass("TRN2", target_bir_lowering=False, debug=False)
    p = Prog(nc)
    xn_d = [nc.dram_tensor(f"xn{b}", [8192, 128], BF16, kind="ExternalInput") for b in range(2)]
    ldt_d = nc.dram_tensor("log_dt", [128, 1], F32, kind="ExternalInput")
    B_d = nc.dram_tensor("Bmat", [128, 64], F32, kind="ExternalInput")
    C_d = nc.dram_tensor("Cmat", [128, 64], F32, kind="ExternalInput")
    Dv_d = nc.dram_tensor("Dvec", [128, 1], F32, kind="ExternalInput")
    A_d = nc.dram_tensor("A_bc", [128, 64], F32, kind="ExternalInput")
    iA_d = nc.dram_tensor("invA_bc", [128, 64], F32, kind="ExternalInput")
    id_d = nc.dram_tensor("ident_f", [128, 128], F32, kind="ExternalInput")
    y_d = [nc.dram_tensor(f"y{b}", [8192, 128], BF16, kind="ExternalOutput") for b in range(2)]
    if dbg:
        dbgK_d = nc.dram_tensor("dbgK", [128, 256], F32, kind="ExternalOutput")
        dbgX_d = nc.dram_tensor("dbgX", [128, 8192], BF16, kind="ExternalOutput")
        dbgP_d = nc.dram_tensor("dbgP", [64, 8192], BF16, kind="ExternalOutput")
        dbgS_d = nc.dram_tensor("dbgS", [64, 128 * 65], BF16, kind="ExternalOutput")
        dbgY_d = nc.dram_tensor("dbgY", [128, 8192], BF16, kind="ExternalOutput")
        dbgU_d = nc.dram_tensor("dbgU", [128, 8192], BF16, kind="ExternalOutput")
        dbgV_d = nc.dram_tensor("dbgV", [128, 64 * 129], BF16, kind="ExternalOutput")
        dbgQ_d = nc.dram_tensor("dbgQ", [64, 128], F32, kind="ExternalOutput")
    krep_d = nc.dram_tensor("krep", [128, 128 * 256], BF16, kind="Internal")
    urev_d = nc.dram_tensor("urev", [128, 8192], BF16, kind="Internal")
    v_d = nc.dram_tensor("vdram", [128, 64 * 129], BF16, kind="Internal")

    ldt = p.sb("ldt", [128, 1], F32)
    Bs = p.sb("Bs", [128, 64], F32)
    Cs = p.sb("Cs", [128, 64], F32)
    Ds = p.sb("Ds", [128, 1], F32)
    Abc = p.sb("Abc", [128, 64], F32)
    iAbc = p.sb("iAbc", [128, 64], F32)
    idf = p.sb("idf", [128, 128], F32)
    dt = p.sb("dt", [128, 1], F32)
    lam = p.sb("lam", [128, 64], F32)
    ones64 = p.sb("ones64", [128, 64], F32)
    CB = p.sb("CB", [128, 64], F32)
    ucur = p.sb("ucur", [128, 64], F32)
    tmp64 = p.sb("tmp64", [128, 64], F32)
    urev_sb = p.sb("urev_sb", [128, 8192], BF16)
    v_sb = p.sb("v_sb", [128, 64 * 129], BF16)
    K_sb = p.sb("K_sb", [128, 256], F32)
    K_bf = p.sb("K_bf", [128, 256], BF16)
    lamQT = p.sb("lamQT", [64, 128], F32)
    xraw = [p.sb(f"xraw{b}", [128, 8192], BF16) for b in range(2)]
    y_sb = [p.sb(f"ysb{b}", [128, 8192], BF16) for b in range(2)]
    P_all = [p.sb(f"Pall{b}", [64, 8192], BF16) for b in range(2)]
    S_shift = p.sb("S_shift", [64, 128 * 65], F32)
    S_bf = p.sb("S_bf", [64, 128 * 65], BF16)
    T8 = [p.sb(f"T8_{i}", [128, 8 * 128], BF16) for i in range(2)]
    W8 = [p.sb(f"W8_{i}", [128, 8 * 64], BF16) for i in range(2)]
    M8 = [p.sb(f"M8_{i}", [64, 8 * 128], BF16) for i in range(2)]
    pY = [p.ps(f"pY{b}", [128, 512], F32) for b in range(2)]
    pP = [p.ps(f"pP{b}", [64, 512], F32) for b in range(2)]
    pT = p.ps("pT", [64, 128], F32)

    for t, d in ((ldt, ldt_d), (Bs, B_d), (Cs, C_d), (Ds, Dv_d), (Abc, A_d),
                 (iAbc, iA_d), (idf, id_d)):
        p.dma("sync", lambda e, t=t, d=d: e.dma_start(t[:], d.ap()), inc="ld")
    p.wait("scalar", "ld", p.count("ld"))
    p.op("scalar", lambda e: e.activation(dt[:], ldt[:], AF.Exp), inc="act")
    p.wait("vector", "act", p.count("act"))
    p.op("vector", lambda e: e.tensor_scalar_mul(lam[:], Abc[:], dt[:, 0:1]))
    p.wait("scalar", "vchain", 0)
    p.op("vector", lambda e: e.memset(ones64[:], 1.0))
    p.op("vector", lambda e: e.tensor_copy(tmp64[:], lam[:]), inc="dve")
    p.wait("scalar", "dve", p.count("dve"))
    p.op("scalar", lambda e: e.activation(lam[:], tmp64[:], AF.Exp), inc="act")
    p.wait("vector", "act", p.count("act"))
    p.op("vector", lambda e: e.tensor_sub(CB[:], lam[:], ones64[:]))
    p.op("vector", lambda e: e.tensor_mul(CB[:], CB[:], iAbc[:]))
    p.op("vector", lambda e: e.tensor_mul(CB[:], CB[:], Bs[:]))
    p.op("vector", lambda e: e.tensor_mul(CB[:], CB[:], Cs[:]))
    p.op("vector", lambda e: e.memset(ucur[:], 1.0))
    p.op("vector", lambda e: e.memset(K_sb[:], 0.0))
    v_r = v_sb[:].rearrange("p (n j) -> p n j", j=129)
    for j in range(129):
        if j <= 127:
            p.op("vector", lambda e, j=j: e.tensor_copy(
                urev_sb[:, (127 - j) * 64:(128 - j) * 64], ucur[:]))
        p.op("vector", lambda e: e.tensor_mul(tmp64[:], ucur[:], CB[:]))
        p.op("vector", lambda e, j=j: e.tensor_copy(v_r[:, :, j], tmp64[:]))
        p.op("vector", lambda e, j=j: e.reduce_sum(K_sb[:, 127 + j:128 + j], tmp64[:], AX.X))
        if j <= 127:
            p.op("vector", lambda e: e.tensor_mul(ucur[:], ucur[:], lam[:]))
    p.op("vector", lambda e: e.tensor_add(K_sb[:, 127:128], K_sb[:, 127:128], Ds[:]))
    p.op("vector", lambda e: e.tensor_copy(K_bf[:], K_sb[:]), inc="chain")
    chain_done = p.count("chain")

    p.wait("gpsimd", "chain", chain_done)
    p.dma("gpsimd", lambda e: e.dma_start(urev_d.ap(), urev_sb[:]), inc="guv")
    p.dma("gpsimd", lambda e: e.dma_start(v_d.ap(), v_sb[:]), inc="guv")
    p.wait("sync", "chain", chain_done)
    for k in range(128):
        p.dma("sync", lambda e, k=k: e.dma_start(
            bass.AP(krep_d, k * 256, [[256 * 128, 128], [1, 256]]), K_bf[:]), inc="krep")
    p.wait("tensor", "chain", chain_done)
    p.op("tensor", lambda e: e.transpose(pT[:], ucur[:, 0:64], idf[:]), inc="te_t")
    p.wait("vector", "te_t", p.count("te_t"))
    p.op("vector", lambda e: e.tensor_copy(lamQT[:], pT[:]), inc="lamqt")

    for b in range(2):
        for piece in range(8):
            src = bass.AP(xn_d[b], piece * 8 * 16384,
                          [[128, 128], [16384, 8], [1, 128]])
            dst = xraw[b][:].rearrange("p (c d) -> p c d", c=64)[:, piece * 8:(piece + 1) * 8, :]
            p.dma("scalar", lambda e, s=src, d=dst: e.dma_start(d, s), inc=f"ldx{b}")

    if dbg:
        p.wait("sync", "ldx0", 8 * 16)
        p.dma("sync", lambda e: e.dma_start(dbgX_d.ap(), xraw[0][:]), inc="dbg")

    def rhs_x(b, ch):
        return bass.AP(xraw[b].handle if hasattr(xraw[b], "handle") else xraw[b],
                       ch, [[8192, 128], [128, 64]])

    nblk = 16
    for blk in range(nblk):
        buf = blk % 2
        p.wait("sync", "krep", 128 * 16)
        if blk >= 2:
            p.wait("sync", "te1", blk - 1)
        p.dma("sync", lambda e, blk=blk, buf=buf: e.dma_start(
            T8[buf][:], bass.AP(krep_d, blk * 8 * 32768 + 127,
                                [[255, 128], [32768, 8], [1, 128]])), inc=f"tw{buf}")
        p.wait("sync", "guv", 32)
        p.dma("sync", lambda e, blk=blk, buf=buf: e.dma_start(
            W8[buf][:], bass.AP(urev_d, blk * 8 * 8192,
                                [[64, 128], [8192, 8], [1, 64]])), inc=f"tw{buf}")
        p.wait("tensor", f"tw{buf}", 32 * (blk // 2 + 1))
        for b in range(2):
            p.wait("tensor", f"ldx{b}", 8 * 16)
            if blk >= 1:
                p.wait("tensor", "ev1", 4 * blk)
            for i in range(8):
                ch = blk * 8 + i
                p.op("tensor", lambda e, b=b, i=i, buf=buf, ch=ch: e.matmul(
                    pY[b][:, i * 64:(i + 1) * 64], T8[buf][:, i * 128:(i + 1) * 128],
                    rhs_x(b, ch), start=True, stop=True))
                p.op("tensor", lambda e, b=b, i=i, buf=buf, ch=ch: e.matmul(
                    pP[b][0:64, i * 64:(i + 1) * 64], W8[buf][:, i * 64:(i + 1) * 64],
                    rhs_x(b, ch), start=True, stop=True),
                    inc=("te1" if (b == 1 and i == 7) else None))
        p.wait("vector", "te1", blk + 1)
        for b in range(2):
            p.op("vector", lambda e, b=b, blk=blk: e.tensor_copy(
                y_sb[b][:, blk * 512:(blk + 1) * 512], pY[b][:]), inc="ev1")
            p.op("vector", lambda e, b=b, blk=blk: e.tensor_copy(
                P_all[b][0:64, blk * 512:(blk + 1) * 512], pP[b][0:64, :]), inc="ev1")

    for b in range(2):
        p.wait("vector", "lamqt", 1)
        p.op("vector", lambda e: e.memset(S_shift[:], 0.0))
        for d in range(128):
            p.op("vector", lambda e, b=b, d=d: e.tensor_tensor_scan(
                S_shift[0:64, d * 65 + 1:d * 65 + 65],
                lamQT[:, d:d + 1].broadcast_to((64, 64)),
                P_all[b][0:64, d * 64:(d + 1) * 64],
                0.0, ALU.mult, ALU.add))
        p.op("vector", lambda e: e.tensor_copy(S_bf[:], S_shift[:]), inc=f"scan{b}")
        for blk in range(nblk):
            buf = blk % 2
            if blk == 0:
                p.wait("scalar", "guv", 32)
            if blk >= 2:
                p.wait("scalar", f"teM{b}", blk - 1)
            p.dma("scalar", lambda e, blk=blk, buf=buf: e.dma_start(
                M8[buf][:], bass.AP(v_d, blk * 8 * 8256 + 1,
                                    [[129, 64], [8256, 8], [1, 128]])), inc=f"md{b}{buf}")
            p.wait("tensor", f"md{b}{buf}", 16 * (blk // 2 + 1))
            p.wait("tensor", f"scan{b}", 1)
            if blk >= 1:
                p.wait("tensor", f"evM{b}", blk)
            for i in range(8):
                ch = blk * 8 + i
                p.op("tensor", lambda e, b=b, i=i, buf=buf, ch=ch: e.matmul(
                    pY[b][:, i * 64:(i + 1) * 64], M8[buf][:, i * 128:(i + 1) * 128],
                    S_bf[0:64, ch * 65:ch * 65 + 64], start=True, stop=True),
                    inc=(f"teM{b}" if i == 7 else None))
            p.wait("vector", f"teM{b}", blk + 1)
            p.op("vector", lambda e, b=b, blk=blk: e.tensor_add(
                y_sb[b][:, blk * 512:(blk + 1) * 512],
                y_sb[b][:, blk * 512:(blk + 1) * 512], pY[b][:]), inc=f"evM{b}")
        out_r = xraw[b][:].rearrange("p (c d) -> p c d", c=64)
        in_r = bass.AP(y_sb[b].handle if hasattr(y_sb[b], "handle") else y_sb[b],
                       0, [[8192, 128], [1, 64], [64, 128]])
        p.op("vector", lambda e, o=out_r, i=in_r: e.tensor_copy(o, i), inc=f"rs{b}")
        p.wait("sync", f"rs{b}", 1)
        dst = bass.AP(y_d[b], 0, [[128, 128], [128 * 128, 64], [1, 128]])
        p.dma("sync", lambda e, b=b, dst=dst: e.dma_start(
            dst, xraw[b][:].rearrange("p (c d) -> p c d", c=64)), inc="sty")

    if dbg:
        p.wait("sync", "chain", chain_done)
        p.dma("sync", lambda e: e.dma_start(dbgK_d.ap(), K_sb[:]), inc="dbg")
        p.dma("sync", lambda e: e.dma_start(dbgU_d.ap(), urev_sb[:]), inc="dbg")
        p.dma("sync", lambda e: e.dma_start(dbgV_d.ap(), v_sb[:]), inc="dbg")
        p.wait("sync", "lamqt", 1)
        p.dma("sync", lambda e: e.dma_start(dbgQ_d.ap(), lamQT[:]), inc="dbg")
        p.wait("sync", "scan0", 1)
        p.dma("sync", lambda e: e.dma_start(dbgP_d.ap(), P_all[0][:]), inc="dbg")
        p.dma("sync", lambda e: e.dma_start(dbgS_d.ap(), S_bf[:]), inc="dbg")
        p.wait("sync", "evM0", 16)
        p.dma("sync", lambda e: e.dma_start(dbgY_d.ap(), y_sb[0][:]), inc="dbg")
    p.finish_and_emit()
    return nc


def build_l3a():
    nc = bass.Bass("TRN2", target_bir_lowering=False, debug=False)
    p = Prog(nc)
    acc_d = nc.dram_tensor("acc_fm", [D, T], F32, kind="ExternalInput")
    ys_d = nc.dram_tensor("yssm_fm", [D, T], BF16, kind="ExternalInput")
    wT_d = nc.dram_tensor("in_projT", [D, 3 * D], F32, kind="ExternalInput")
    ib_d = nc.dram_tensor("in_proj_b", [3 * D, 1], F32, kind="ExternalInput")
    lw_d = nc.dram_tensor("ln_w", [D, 1], F32, kind="ExternalInput")
    lb_d = nc.dram_tensor("ln_b", [D, 1], F32, kind="ExternalInput")
    x1_d = nc.dram_tensor("x1_fm", [D, T], F32, kind="ExternalOutput")
    xn1_d = nc.dram_tensor("xn1_fm", [D, T], BF16, kind="ExternalOutput")
    xn2_d = nc.dram_tensor("xn2_fm", [D, T], BF16, kind="ExternalOutput")
    q_d = nc.dram_tensor("q_fm", [D, T], BF16, kind="ExternalOutput")
    k_d = nc.dram_tensor("k_fm", [D, T], BF16, kind="ExternalOutput")
    v_d = nc.dram_tensor("v_tok", [T, D], BF16, kind="ExternalOutput")

    x1_sb = p.sb("x1_sb", [128, NCH * T], F32)
    ys_st = [p.sb(f"ys_st{i}", [128, T], BF16) for i in range(4)]
    xn1_sb = p.sb("xn1_sb", [128, NCH * T], BF16)
    xn2_sb = p.sb("xn2_sb", [128, NCH * T], BF16)
    x1_ch = [x1_sb[:, c * T:(c + 1) * T] for c in range(NCH)]
    xn1_ch = [xn1_sb[:, c * T:(c + 1) * T] for c in range(NCH)]
    xn2_ch = [xn2_sb[:, c * T:(c + 1) * T] for c in range(NCH)]
    lw_sb = p.sb("lw_sb", [128, NCH], F32)
    lb_sb = p.sb("lb_sb", [128, NCH], F32)
    qb_sb = p.sb("qb_sb", [128, 16], F32)
    wqk = [p.sb(f"wqk{i}", [128, NCH * 128], BF16) for i in range(2)]
    wv_sb = p.sb("wv_sb", [128, NCH * D], BF16)
    ev = [p.sb(f"ev{i}", [128, 1024], BF16) for i in range(2)]
    pbig1 = p.ps("pbig1", [128, T], F32)
    pbig2 = p.ps("pbig2", [128, T], F32)

    for c in range(NCH):
        p.dma("sync", lambda e, c=c: e.dma_start(x1_ch[c], acc_d.ap()[c * 128:(c + 1) * 128, :]), inc="ldx1")
    for c in range(NCH):
        if c >= 4:
            p.wait("sync", "addx", c - 3)
        p.dma("sync", lambda e, c=c: e.dma_start(ys_st[c % 4][:], ys_d.ap()[c * 128:(c + 1) * 128, :]), inc=f"ldys{c % 4}")
    for c in range(NCH):
        p.dma("sync", lambda e, c=c: e.dma_start(lw_sb[:, c:c + 1], lw_d.ap()[c * 128:(c + 1) * 128, :]), inc="ldb")
        p.dma("sync", lambda e, c=c: e.dma_start(lb_sb[:, c:c + 1], lb_d.ap()[c * 128:(c + 1) * 128, :]), inc="ldb")
    for c in range(16):
        p.dma("sync", lambda e, c=c: e.dma_start(qb_sb[:, c:c + 1], ib_d.ap()[c * 128:(c + 1) * 128, :]), inc="ldb")
    p.dma("gpsimd", lambda e: e.dma_start(
        wv_sb[:].rearrange("p (c f) -> p c f", c=NCH),
        bass.AP(wT_d, 2 * D, [[3 * D, 128], [128 * 3 * D, NCH], [1, D]])), inc="gwv")

    p.wait("vector", "ldx1", NCH * 16)
    for c in range(NCH):
        p.wait("vector", f"ldys{c % 4}", 16 * (c // 4 + 1))
        p.op("vector", lambda e, c=c: e.tensor_add(x1_ch[c], x1_ch[c], ys_st[c % 4][:]),
             inc="addx")
    lnsc = make_ln_scratch(p)
    ln1_done = emit_ln_fm(p, "ln1", x1_ch, xn1_ch, pbig1, pbig2, lnsc,
                          x_ready=("addx", NCH))
    lnw = [lw_sb[:, c:c + 1] for c in range(NCH)]
    lnb = [lb_sb[:, c:c + 1] for c in range(NCH)]
    p.wait("vector", "ldb", 32 * 16)
    ln2_done = emit_ln_fm(p, "ln2", xn1_ch, xn2_ch, pbig1, pbig2, lnsc,
                          lnw=lnw, lnb=lnb, x_ready=("ln1_dve", ln1_done))

    ps_qk = [pbig1[:, i * 512:(i + 1) * 512] for i in range(4)]
    nw = 0
    for oc in range(16):
        buf = oc % 2
        if oc >= 2:
            p.wait("gpsimd", "teqk", (oc - 1) * 4)
        p.dma("gpsimd", lambda e, oc=oc, buf=buf: e.dma_start(
            wqk[buf][:].rearrange("p (c f) -> p c f", c=NCH),
            bass.AP(wT_d, oc * 128, [[3 * D, 128], [128 * 3 * D, NCH], [1, 128]])), inc=f"gqk{buf}")
        p.wait("tensor", f"gqk{buf}", 16 * (oc // 2 + 1))
        p.wait("tensor", "ln2_dve", ln2_done)
        if oc >= 1:
            p.wait("tensor", "evqk", (oc - 1) * 4 + 4)
        for s in range(4):
            for dc in range(NCH):
                p.op("tensor", lambda e, oc=oc, s=s, dc=dc, buf=buf: e.matmul(
                    ps_qk[s], wqk[buf][:, dc * 128:(dc + 1) * 128],
                    xn2_ch[dc][:, s * 512:(s + 1) * 512],
                    start=(dc == 0), stop=(dc == NCH - 1)),
                    inc=("teqk" if dc == NCH - 1 else None))
        p.wait("vector", "teqk", p.count("teqk"))
        p.wait("vector", "ldb", 32 * 16)
        for s in range(4):
            nev = oc * 4 + s
            if nev >= 2:
                p.wait("vector", f"stq{nev % 2}", 16 * (nev // 2))
            p.op("vector", lambda e, oc=oc, s=s, buf=buf: e.tensor_scalar_add(
                ev[buf][:, (s % 2) * 512:(s % 2 + 1) * 512],
                ps_qk[s], qb_sb[:, oc:oc + 1]), inc="evqk")
            tgt = (q_d if oc < 8 else k_d)
            row = (oc % 8) * 128
            p.wait("sync", "evqk", p.count("evqk"))
            p.dma("sync", lambda e, tgt=tgt, row=row, s=s, buf=buf: e.dma_start(
                tgt.ap()[row:row + 128, s * 512:(s + 1) * 512],
                ev[buf][:, (s % 2) * 512:(s % 2 + 1) * 512]), inc=f"stq{(oc * 4 + s) % 2}")

    for tt in range(16):
        buf = tt % 2
        p.wait("tensor", "gwv", 16)
        if tt >= 1:
            p.wait("tensor", "evv", tt)
        for dc in range(NCH):
            for h in range(2):
                p.op("tensor", lambda e, tt=tt, dc=dc, h=h: e.matmul(
                    pbig2[:, h * 512:(h + 1) * 512],
                    xn2_ch[dc][:, tt * 128:(tt + 1) * 128],
                    wv_sb[:, dc * D + h * 512: dc * D + (h + 1) * 512],
                    start=(dc == 0), stop=(dc == NCH - 1)),
                    inc=("tev" if dc == NCH - 1 and h == 1 else None))
        p.wait("vector", "tev", p.count("tev"))
        if tt >= 2:
            p.wait("vector", f"sv{buf}", 16 * (tt // 2))
        p.op("vector", lambda e, buf=buf: e.tensor_copy(ev[buf][:], pbig2[:, 0:1024]), inc="evv")
        p.wait("sync", "evv", p.count("evv"))
        p.dma("sync", lambda e, tt=tt, buf=buf: e.dma_start(
            v_d.ap()[tt * 128:(tt + 1) * 128, :], ev[buf][:]), inc=f"sv{buf}")

    p.wait("sync", "ln1_dve", ln1_done)
    for c in range(NCH):
        p.dma("sync", lambda e, c=c: e.dma_start(x1_d.ap()[c * 128:(c + 1) * 128, :], x1_ch[c]), inc="stx")
        p.dma("sync", lambda e, c=c: e.dma_start(xn1_d.ap()[c * 128:(c + 1) * 128, :], xn1_ch[c]), inc="stx")
    p.wait("sync", "ln2_dve", ln2_done)
    for c in range(NCH):
        p.dma("sync", lambda e, c=c: e.dma_start(xn2_d.ap()[c * 128:(c + 1) * 128, :], xn2_ch[c]), inc="stx")
    p.finish_and_emit()
    return nc


def emit_p4(p, io, pp):
    nc = p.nc
    x1_d = io["x1"]; q_d = io["q"]; K_d = io["K"]; V_d = io["V"]
    opT_d = io["out_projT"]; opb_d = io["out_proj_b"]; vb_d = io["v_bias"]
    y1_d = io["y1"]
    psA, psB, psC, psD, psE, psF = (pp["psA"], pp["psB"], pp["psC"],
                                     pp["psD"], pp["psE"], pp["psF"])
    pS = [psA, psB]
    pV = [[psE, psF], [psC[0:65, :], psD[0:65, :]]]

    x1_sb = p.sb("x1_sb", [128, NCH * T], F32)
    x1_ch = [x1_sb[:, c * T:(c + 1) * T] for c in range(NCH)]
    q_sb = p.sb("q_sb", [128, NCH * T], BF16)
    q_ch = [q_sb[:, c * T:(c + 1) * T] for c in range(NCH)]
    ao_sb = p.sb("ao_sb", [128, NCH * T], BF16)
    ao_ch = [ao_sb[:, c * T:(c + 1) * T] for c in range(NCH)]
    kpair = p.sb("kpair", [128, 8192], BF16)
    vsb = p.sb("vsb", [128, 64 * 130], BF16)
    pexp = [[p.sb(f"pexp{s}{i}", [128, 1024], BF16) for i in range(2)] for s in range(2)]
    pvacc = [[p.sb(f"pvacc{s}{h}", [65, 512], F32) for h in range(2)] for s in range(2)]
    opw = [p.sb(f"opw{i}", [128, NCH * 128], BF16) for i in range(2)]
    opb_sb = p.sb("opb_sb", [128, NCH], F32)
    vb_sb = p.sb("vb_sb", [128, NCH], F32)
    tmp_h = [p.sb(f"tmph{h}", [64, 512], F32) for h in range(2)]
    rd_sb = p.sb("rd_sb", [1, 512], F32)
    ones_f = p.sb("ones_f", [1, 128], F32)

    for c in range(NCH):
        p.dma("sync", lambda e, c=c: e.dma_start(x1_ch[c], x1_d.ap()[c * 128:(c + 1) * 128, :]), inc="ldx")
        p.dma("sync", lambda e, c=c: e.dma_start(q_ch[c], q_d.ap()[c * 128:(c + 1) * 128, :]), inc="ldq")
        p.dma("sync", lambda e, c=c: e.dma_start(opb_sb[:, c:c + 1], opb_d.ap()[c * 128:(c + 1) * 128, :]), inc="ldb")
        p.dma("sync", lambda e, c=c: e.dma_start(vb_sb[:, c:c + 1], vb_d.ap()[c * 128:(c + 1) * 128, :]), inc="ldb")
    p.op("vector", lambda e: e.memset(ones_f[:], 1.0), inc="ones")

    NKV = 64
    for hp in range(NCH):
        if hp >= 1:
            p.wait("sync", "pvev", 32 * hp)
        p.dma("sync", lambda e, hp=hp: e.dma_start(
            kpair[:], K_d.ap()[hp * 128:(hp + 1) * 128, :]), inc="kv_ld")
        for pc in range(4):
            for h in range(2):
                srcv = bass.AP(V_d, (pc * 16 * 128) * D + hp * 128 + h * 64,
                               [[D, 128], [128 * D, 16], [1, 64]])
                dstv = bass.AP(vsb, pc * 16 * 130 + h * 65,
                               [[8320, 128], [130, 16], [1, 64]])
                p.dma("sync", lambda e, s=srcv, d=dstv: e.dma_start(d, s), inc="kv_ld")
        p.wait("vector", "kv_ld", p.count("kv_ld"))
        p.op("vector", lambda e: e.memset(
            vsb[:].rearrange("p (kv f) -> p kv f", kv=128)[:, :, 64:65], 1.0), inc="vones")
        p.wait("tensor", "vones", p.count("vones"))
        p.wait("tensor", "kv_ld", p.count("kv_ld"))
        if hp == 0:
            p.wait("tensor", "ldq", 8 * 16)
        for sp in range(2):
        
            tqs = [slice((2 * sp + s) * 512, (2 * sp + s + 1) * 512) for s in range(2)]
            for kv in range(NKV + 1):
                for s in range(2):
                    if kv < NKV:
                        p.wait("tensor", f"aex{s}", p.count(f"aex{s}"))
                        for h in range(2):
                            bse = h * 64
                            p.op("tensor", lambda e, hp=hp, kv=kv, h=h, bse=bse, s=s, tq=tqs[s]: e.matmul(
                                pS[s][:, h * 512:(h + 1) * 512],
                                kpair[bse:bse + 64, kv * 128:(kv + 1) * 128],
                                q_ch[hp][bse:bse + 64, tq], start=True, stop=True),
                                inc=(f"tes{s}" if h == 1 else None))
                for s in range(2):
                    if kv < NKV:
                        p.wait("scalar", f"tes{s}", p.count(f"tes{s}"))
                        if kv >= 2:
                            p.wait("scalar", f"tepv{s}", p.count(f"tepv{s}"))
                        p.op("scalar", lambda e, s=s, kv=kv: e.activation(
                            pexp[s][kv % 2][:], pS[s][:], AF.Exp, scale=0.125), inc=f"aex{s}")
                for s in range(2):
                    if kv >= 1:
                        k2 = kv - 1
                        p.wait("tensor", f"aex{s}", p.count(f"aex{s}") - (1 if kv < NKV else 0))
                        if k2 % 16 == 0:
                            p.wait("tensor", f"pvev{s}x" if False else "pvev",
                                   p.count("pvev"))
                        for h in range(2):
                            p.op("tensor", lambda e, s=s, k2=k2, h=h: e.matmul(
                                pV[s][h][:], vsb[:, k2 * 130 + h * 65:k2 * 130 + (h + 1) * 65],
                                pexp[s][k2 % 2][:, h * 512:(h + 1) * 512],
                                start=(k2 % 16 == 0), stop=(k2 % 16 == 15)),
                                inc=(f"tepv{s}" if h == 1 else None))
                        if k2 % 16 == 15:
                            p.wait("vector", f"tepv{s}", p.count(f"tepv{s}"))
                            for h in range(2):
                                if k2 == 15:
                                    p.op("vector", lambda e, s=s, h=h: e.tensor_copy(
                                        pvacc[s][h][:], pV[s][h][:]), inc="pvev")
                                else:
                                    p.op("vector", lambda e, s=s, h=h: e.tensor_add(
                                        pvacc[s][h][:], pvacc[s][h][:], pV[s][h][:]), inc="pvev")
            p.wait("vector", "ones", 1)
            p.wait("vector", "ldb", 16 * 16)
            for s in range(2):
                for h in range(2):
                    p.op("vector", lambda e, s=s, h=h: e.reciprocal(
                        rd_sb[:], pvacc[s][h][64:65, :]), inc="fin")
                    p.wait("tensor", "fin", p.count("fin"))
                    p.op("tensor", lambda e, s=s: e.matmul(
                        pV[s][0][0:64, :], ones_f[:, 0:64], rd_sb[:],
                        start=True, stop=True), inc="finb")
                    p.wait("vector", "finb", p.count("finb"))
                    p.op("vector", lambda e, s=s, h=h: e.tensor_mul(
                        tmp_h[h][:], pvacc[s][h][0:64, :], pV[s][0][0:64, :]), inc="fin")
                    p.op("vector", lambda e, s=s, h=h, hp=hp, tq=tqs[s]: e.tensor_scalar_add(
                        ao_ch[hp][h * 64:h * 64 + 64, tq],
                        tmp_h[h][:], vb_sb[h * 64:h * 64 + 64, hp:hp + 1]), inc="fin")

    ao_done = p.count("fin")
    for oc in range(NCH):
        buf = oc % 2
        if oc >= 2:
            p.wait("gpsimd", "teop", oc - 1)
        p.dma("gpsimd", lambda e, oc=oc, buf=buf: e.dma_start(
            opw[buf][:].rearrange("p (c f) -> p c f", c=NCH),
            bass.AP(opT_d, oc * 128, [[D, 128], [128 * D, NCH], [1, 128]])), inc=f"gop{buf}")
        p.wait("tensor", f"gop{buf}", 16 * (oc // 2 + 1))
        p.wait("tensor", "fin", ao_done)
        if oc >= 1:
            p.wait("tensor", "evop", 4 * oc)
        for s in range(4):
            ps_s = (psA if s < 2 else psB)[:, (s % 2) * 512:(s % 2 + 1) * 512]
            for dc in range(NCH):
                p.op("tensor", lambda e, oc=oc, s=s, dc=dc, buf=buf, ps_s=ps_s: e.matmul(
                    ps_s, opw[buf][:, dc * 128:(dc + 1) * 128],
                    ao_ch[dc][:, s * 512:(s + 1) * 512],
                    start=(dc == 0), stop=(dc == NCH - 1)),
                    inc=("teop" if (dc == NCH - 1 and s == 3) else None))
        p.wait("vector", "teop", oc + 1)
        p.wait("vector", "ldx", NCH * 16)
        for s in range(4):
            ps_s = (psA if s < 2 else psB)[:, (s % 2) * 512:(s % 2 + 1) * 512]
            p.op("vector", lambda e, oc=oc, s=s, ps_s=ps_s: e.scalar_tensor_tensor(
                x1_ch[oc][:, s * 512:(s + 1) * 512], ps_s, 1.0,
                x1_ch[oc][:, s * 512:(s + 1) * 512], ALU.mult, ALU.add), inc="evop")
    p.wait("vector", "ldb", 16 * 16)
    for c in range(NCH):
        p.op("vector", lambda e, c=c: e.tensor_scalar_add(
            x1_ch[c], x1_ch[c], opb_sb[:, c:c + 1]), inc="bias")
    p.wait("sync", "bias", p.count("bias"))
    for c in range(NCH):
        p.dma("sync", lambda e, c=c: e.dma_start(y1_d.ap()[c * 128:(c + 1) * 128, :], x1_ch[c]), inc="sty")


def build_l3b1():
    nc = bass.Bass("TRN2", target_bir_lowering=False, debug=False)
    p = Prog(nc)
    io = dict(
        x1=nc.dram_tensor("x1_fm", [D, T], F32, kind="ExternalInput"),
        q=nc.dram_tensor("q_fm", [D, T], BF16, kind="ExternalInput"),
        K=nc.dram_tensor("K_full", [D, 8192], BF16, kind="ExternalInput"),
        V=nc.dram_tensor("V_full", [8192, D], BF16, kind="ExternalInput"),
        out_projT=nc.dram_tensor("out_projT", [D, D], F32, kind="ExternalInput"),
        out_proj_b=nc.dram_tensor("out_proj_b", [D, 1], F32, kind="ExternalInput"),
        v_bias=nc.dram_tensor("v_bias", [D, 1], F32, kind="ExternalInput"),
        y1=nc.dram_tensor("y1_fm", [D, T], F32, kind="ExternalOutput"),
    )
    PB1 = p.ps("PB1", [128, 2048], F32)
    PB2 = p.ps("PB2", [128, 2048], F32)
    pp = dict(psA=PB1[:, 0:1024], psB=PB1[:, 1024:2048],
              psE=PB2[0:65, 0:512], psF=PB2[0:65, 512:1024],
              psC=PB2[:, 1024:1536], psD=PB2[:, 1536:2048])
    emit_p4(p, io, pp)
    p.finish_and_emit()
    return nc


def emit_p5(p, io, pp):
    nc = p.nc
    y1_d = io["y1"]; xn1_d = io["xn1"]; xn2_d = io["xn2"]
    w0T_d = io["w0T"]; w2T_d = io["w2T"]; w1_d = io["w1l"]; w1T_d = io["w1T"]
    gwT_d = io["gate_wT"]; bal_d = io["bal_b"]
    mw1_d = io["moew1T"]; mw2_d = io["moew2T"]
    out_d = io["out"]
    mw1b_d = io["mw1b"]; mw2b_d = io["mw2b"]; wc_d = io["wc"]

    y1_sb = p.sb("y1_sb", [128, NCH * T], F32)
    y1_ch = [y1_sb[:, c * T:(c + 1) * T] for c in range(NCH)]
    xn1_sb = p.sb("xn1_sb", [128, NCH * T], BF16)
    xn1_ch = [xn1_sb[:, c * T:(c + 1) * T] for c in range(NCH)]
    xn2_sb = p.sb("xn2_sb", [128, NCH * T], BF16)
    xn2_ch = [xn2_sb[:, c * T:(c + 1) * T] for c in range(NCH)]
    lw = p.sb("lw", [128, 4 * 4 * 512], BF16)
    def lw_sl(h, m, dc2):
        return lw[:, (h * 4 + m) * 512 + dc2 * 256:(h * 4 + m) * 512 + (dc2 + 1) * 256]
    kvt = p.sb("kvt", [128, 8 * 256], BF16)
    hid_sb = p.sb("hid_sb", [128, 8 * 256], BF16)
    dpre_sb = p.sb("dpre_sb", [128, 8 * 256], BF16)
    dg_sb = None
    gt_t = g2_t = u_t = w_t = wn0 = wn2 = wn1 = None
    hq_sb = p.sb("hq_sb", [128, 2 * 1024], BF16)
    he_sb = p.sb("he_sb", [128, 2 * 1024], BF16)
    dg_sb = he_sb
    gw_sb = p.sb("gw_sb", [128, NCH * 8], BF16)
    glt = p.sb("glt", [128, 16 * 8], F32)
    wcoef = p.sb("wcoef", [128, 16 * 8], F32)
    wrowT = None
    m1_t = p.sb("m1_t", [128, 1], F32)
    m2_t = p.sb("m2_t", [128, 1], F32)
    den_t = p.sb("den_t", [128, 1], F32)
    eps6 = p.sb("eps6", [128, 1], F32)
    bb_sb = p.sb("bb_sb", [128, 8], F32)
    ones_f = p.sb("ones_f", [1, 128], F32)
    identb = p.sb("identb", [128, 128], BF16)
    wrowT = None
    mw2_sb = [p.sb(f"mw2_sb{i}", [128, 2048], BF16) for i in range(2)]
    hst = p.sb("hst", [128, 16 * 1024], BF16)
    wrowT = hst[:].bitcast(F32)[0:8, 0:2048]
    sg_p = [hst[:, 8192:8448], hst[:, 11264:11520]]
    dh0_p = [hst[:, 8960:9216], hst[:, 11520:11776]]
    gt_t = hst[:, 8448:8704]; g2_t = hst[:, 8704:8960]
    u_t = hst[:, 9216:9472]; w_t = hst[:, 9472:9728]
    wn0 = hst[:, 9728:10240]; wn2 = hst[:, 10240:10752]; wn1 = hst[:, 10752:11264]
    wbc = None

    wbc = kvt[:].bitcast(F32)[:, 0:512]
    wbc2 = kvt[:].bitcast(F32)[:, 512:1024]
    psA, psB, psC, psD, psT = pp["psA"], pp["psB"], pp["psC"], pp["psD"], pp["psT"]

    for c in range(NCH):
        p.dma("sync", lambda e, c=c: e.dma_start(y1_ch[c], y1_d.ap()[c * 128:(c + 1) * 128, :]), inc="ldy")
        p.dma("sync", lambda e, c=c: e.dma_start(xn1_ch[c], xn1_d.ap()[c * 128:(c + 1) * 128, :]), inc="ldn1")
        p.dma("sync", lambda e, c=c: e.dma_start(xn2_ch[c], xn2_d.ap()[c * 128:(c + 1) * 128, :]), inc="ldn2")
    for h in range(4):
        for m, wd in ((0, w0T_d), (1, w2T_d), (2, w1_d), (3, w1T_d)):
            src = bass.AP(wd, h * 256 * 256, [[256, 128], [128 * 256, 2], [1, 256]])
            dst = bass.AP(lw, (h * 4 + m) * 512, [[8192, 128], [256, 2], [1, 256]])
            p.dma("gpsimd", lambda e, s=src, d=dst: e.dma_start(d, s), inc="glw")
    p.dma("gpsimd", lambda e: e.dma_start(
        gw_sb[:].rearrange("p (c f) -> p c f", c=NCH),
        bass.AP(gwT_d, 0, [[8, 128], [128 * 8, NCH], [1, 8]])), inc="glw")
    p.dma("sync", lambda e: e.dma_start(identb[:], io["ident_b"].ap()), inc="ldi")
    p.dma("sync", lambda e: e.dma_start(bb_sb[0:1, :], bal_d.ap()), inc="ldi")
    for ee in range(8):
        p.dma("gpsimd", lambda e, ee=ee: e.dma_start(
            mw1b_d.ap()[ee * D:(ee + 1) * D, :], mw1_d.ap()[ee * D:(ee + 1) * D, :]), inc="gcw1")
        p.dma("gpsimd", lambda e, ee=ee: e.dma_start(
            mw2b_d.ap()[ee * 2048:(ee + 1) * 2048, :], mw2_d.ap()[ee * 2048:(ee + 1) * 2048, :]), inc="gcw2")

    p.op("vector", lambda e: e.memset(ones_f[:], 1.0), inc="ones")
    p.op("vector", lambda e: e.memset(eps6[:], 1e-6), inc="ones")
    p.wait("tensor", "glw", 17 * 16)
    p.wait("tensor", "ldn1", 8 * 16)
    for s in range(4):
        pr = (psA if s < 2 else psB)[0:8, (s % 2) * 512:(s % 2 + 1) * 512]
        for dc in range(NCH):
            p.op("tensor", lambda e, s=s, dc=dc, pr=pr: e.matmul(
                pr, gw_sb[:, dc * 8:(dc + 1) * 8],
                xn1_ch[dc][:, s * 512:(s + 1) * 512],
                start=(dc == 0), stop=(dc == NCH - 1)),
                inc=("ter" if dc == NCH - 1 else None))
    p.wait("tensor", "ldi", 32)
    p.op("tensor", lambda e: e.matmul(psC[:, 0:8], ones_f[:], bb_sb[0:1, :],
                                      start=True, stop=True), inc="ter")
    gl8 = hq_sb[0:8, 0:T]
    p.wait("vector", "ter", p.count("ter"))
    for s in range(4):
        pr = (psA if s < 2 else psB)[0:8, (s % 2) * 512:(s % 2 + 1) * 512]
        p.op("vector", lambda e, s=s, pr=pr: e.tensor_copy(gl8[:, s * 512:(s + 1) * 512], pr), inc="evr")
    p.op("vector", lambda e: e.tensor_copy(bb_sb[:], psC[:, 0:8]), inc="evr")
    p.wait("tensor", "evr", p.count("evr"))
    psTs = [psT, pp["psT2"]]
    for tt in range(16):
        slot = psTs[tt % 2]
        if tt >= 2:
            p.wait("tensor", "evt", tt - 1)
        p.op("tensor", lambda e, tt=tt, slot=slot: e.transpose(
            slot[:, 0:8], gl8[0:8, tt * 128:(tt + 1) * 128], identb[0:8, 0:8]), inc="tet")
        p.wait("vector", "tet", p.count("tet"))
        p.op("vector", lambda e, tt=tt, slot=slot: e.tensor_copy(
            glt[:, tt * 8:(tt + 1) * 8], slot[0:128, 0:8]), inc="evt")
    p.op("vector", lambda e: e.tensor_add(
        glt[:, 0:128], glt[:, 0:128],
        bass.AP(bb_sb, 0, [[8, 128], [0, 16], [1, 8]])), inc="rt")
    p.wait("scalar", "rt", p.count("rt"))
    p.op("scalar", lambda e: e.activation(glt[:, 0:128], glt[:, 0:128], AF.Sigmoid), inc="rta")
    p.wait("vector", "rta", p.count("rta"))
    for tt in range(16):
        sl = slice(tt * 8, (tt + 1) * 8)
        p.op("vector", lambda e, sl=sl: e.reduce_max(m1_t[:], glt[:, sl], AX.X), inc="rt")
        p.op("vector", lambda e, sl=sl: e.tensor_scalar(wcoef[:, sl], glt[:, sl], m1_t[:, 0:1], -1e30,
                                                        ALU.is_ge, ALU.mult), inc="rt")
        p.op("vector", lambda e, sl=sl: e.tensor_add(wcoef[:, sl], wcoef[:, sl], glt[:, sl]), inc="rt")
        p.op("vector", lambda e, sl=sl: e.reduce_max(m2_t[:], wcoef[:, sl], AX.X), inc="rt")
        p.op("vector", lambda e: e.tensor_add(den_t[:], m1_t[:], m2_t[:]), inc="rt")
        p.op("vector", lambda e: e.tensor_add(den_t[:], den_t[:], eps6[:]), inc="rt")
        p.op("vector", lambda e: e.reciprocal(den_t[:], den_t[:]), inc="rt")
        p.op("vector", lambda e, sl=sl: e.tensor_scalar(wcoef[:, sl], glt[:, sl], m2_t[:, 0:1], None,
                                                        ALU.is_ge), inc="rt")
        p.op("vector", lambda e, sl=sl: e.tensor_mul(wcoef[:, sl], wcoef[:, sl], glt[:, sl]), inc="rt")
        p.op("vector", lambda e, sl=sl: e.tensor_scalar_mul(wcoef[:, sl], wcoef[:, sl], den_t[:, 0:1]), inc="rt")
    p.wait("vector", "rt", p.count("rt"))
    wct = p.sb("wct", [128, 16 * 8], BF16)
    p.op("vector", lambda e: e.tensor_copy(wct[:], wcoef[:]), inc="wcb")
    p.wait("tensor", "wcb", p.count("wcb"))
    for tt in range(16):
        slot = psTs[tt % 2]
        if tt >= 2:
            p.wait("tensor", "evt2", tt - 1)
        p.op("tensor", lambda e, tt=tt, slot=slot: e.transpose(
            slot[0:8, :], wct[:, tt * 8:(tt + 1) * 8], identb[:]), inc="tet2")
        p.wait("vector", "tet2", p.count("tet2"))
        p.op("vector", lambda e, tt=tt, slot=slot: e.tensor_copy(
            wrowT[0:8, tt * 128:(tt + 1) * 128], slot[0:8, 0:128]), inc="evt2")
    p.wait("sync", "evt2", p.count("evt2"))
    p.dma("sync", lambda e: e.dma_start(wc_d.ap(), wrowT[0:8, :]), inc="stwc")

    p.wait("tensor", "ldn2", 8 * 16)
    p.wait("tensor", "evt2", 16)
    for unit in range(8):
        n = unit // 4
        h = unit % 4
        cch = [2 * h, 2 * h + 1]
        ncol = n * 1024
        for ct in range(8):
            for d2 in range(2):
                p.op("tensor", lambda e, ct=ct, d2=d2, cch=cch, ncol=ncol: e.transpose(
                    psT[:], xn2_ch[cch[d2]][:, ncol + ct * 128:ncol + (ct + 1) * 128], identb[:]),
                    inc="lt1")
                p.wait("vector", "lt1", p.count("lt1"))
                p.op("vector", lambda e, ct=ct, d2=d2: e.tensor_copy(
                    kvt[:, ct * 256 + d2 * 128:ct * 256 + (d2 + 1) * 128], psT[:]), inc="lev1")
                p.wait("tensor", "lev1", p.count("lev1"))
        for ct in range(8):
            par = ct % 2
            if par == 0:
                pG = psA[:, 0:256]; pG2 = psA[:, 512:768]; pH = psB[:, 0:256]
            else:
                pG = psB[:, 512:768]; pG2 = psC[:, 0:256]; pH = psD[:, 0:256]
            sg_t = sg_p[par]; dh0_t = dh0_p[par]
            if ct == 1:
                p.wait("tensor", "le5n", p.count("le5n"))
            if ct >= 2:
                p.wait("tensor", "lely", p.count("lely") - 1)
            for d2 in range(2):
                for m, pp in ((0, pG), (1, pG2), (2, pH)):
                    p.op("tensor", lambda e, m=m, pp=pp, d2=d2, ct=ct, cch=cch, ncol=ncol, h=h: e.matmul(
                        pp, xn2_ch[cch[d2]][:, ncol + ct * 128:ncol + (ct + 1) * 128],
                        lw_sl(h, m, d2), start=(d2 == 0), stop=(d2 == 1)),
                        inc=("lt2" if (d2 == 1 and m == 2) else None))
            p.wait("scalar", "lt2", p.count("lt2"))
            p.op("scalar", lambda e, pG=pG, sg_t=sg_t: e.activation(sg_t[:], pG, AF.Sigmoid), inc="la")
            p.op("scalar", lambda e, pH=pH, dh0_t=dh0_t: e.activation(dh0_t[:], pH, AF.Copy, scale=-1.0), inc="la")
            p.wait("vector", "la", p.count("la"))
            p.op("vector", lambda e, pG=pG: e.tensor_copy(gt_t[:], pG), inc="le")
            p.op("vector", lambda e, pG2=pG2: e.tensor_copy(g2_t[:], pG2), inc="le")
            sl2 = slice(ct * 256, (ct + 1) * 256)
            p.op("vector", lambda e, sg_t=sg_t: e.tensor_mul(u_t[:], gt_t[:], sg_t[:]), inc="le")
            p.op("vector", lambda e, sl2=sl2: e.tensor_mul(hid_sb[:, sl2], u_t[:], g2_t[:]), inc="le")
            p.op("vector", lambda e, sl2=sl2, dh0_t=dh0_t: e.tensor_mul(dg_sb[:, sl2], dh0_t[:], u_t[:]), inc="le")
            p.op("vector", lambda e, sg_t=sg_t: e.tensor_add(w_t[:], sg_t[:], u_t[:]), inc="le")
            p.op("vector", lambda e, sg_t=sg_t: e.tensor_mul(gt_t[:], u_t[:], sg_t[:]), inc="le")
            p.op("vector", lambda e: e.tensor_sub(w_t[:], w_t[:], gt_t[:]), inc="le")
            p.op("vector", lambda e, dh0_t=dh0_t: e.tensor_mul(gt_t[:], dh0_t[:], g2_t[:]), inc="le")
            p.op("vector", lambda e, sl2=sl2: e.tensor_mul(dpre_sb[:, sl2], gt_t[:], w_t[:]), inc="le",
                 n=1)
            p.op("vector", lambda e: e.engine_nop(), inc="lely")
        p.wait("tensor", "le", p.count("le"))
        p.wait("tensor", "le5n", p.count("le5n"))
        for d2 in range(2):
            for ct in range(8):
                p.op("tensor", lambda e, ct=ct, d2=d2: e.matmul(
                    psC[:, 0:256] if False else psA[:, d2 * 256:(d2 + 1) * 256],
                    kvt[:, ct * 256 + d2 * 128:ct * 256 + (d2 + 1) * 128],
                    dpre_sb[:, ct * 256:(ct + 1) * 256],
                    start=(ct == 0), stop=(ct == 7)), inc=("lt3" if ct == 7 else None))
                p.op("tensor", lambda e, ct=ct, d2=d2: e.matmul(
                    psB[:, d2 * 256:(d2 + 1) * 256],
                    kvt[:, ct * 256 + d2 * 128:ct * 256 + (d2 + 1) * 128],
                    dg_sb[:, ct * 256:(ct + 1) * 256],
                    start=(ct == 0), stop=(ct == 7)), inc=("lt3" if ct == 7 else None))
                p.op("tensor", lambda e, ct=ct, d2=d2: e.matmul(
                    psC[:, d2 * 256:(d2 + 1) * 256] if True else None,
                    hid_sb[:].rearrange("p (ct f) -> p ct f", ct=8)[:, ct, d2 * 128:(d2 + 1) * 128],
                    kvt[:, ct * 256:(ct + 1) * 256],
                    start=(ct == 0), stop=(ct == 7)), inc=("lt3" if ct == 7 else None))
        p.wait("vector", "lt3", p.count("lt3"))
        p.op("vector", lambda e, h=h: e.tensor_sub(wn0[:], lw[:, (h * 4 + 0) * 512:(h * 4 + 1) * 512], psA[:, 0:512]), inc="lwn")
        p.op("vector", lambda e, h=h: e.tensor_sub(wn2[:], lw[:, (h * 4 + 1) * 512:(h * 4 + 2) * 512], psB[:, 0:512]), inc="lwn")
        p.op("vector", lambda e, h=h: e.tensor_add(wn1[:], lw[:, (h * 4 + 3) * 512:(h * 4 + 4) * 512], psC[:, 0:512]), inc="lwn")
        p.wait("tensor", "lwn", p.count("lwn"))
        for es in range(2):
            for cs in range(2):
                for d2 in range(2):
                    p.op("tensor", lambda e, es=es, cs=cs, d2=d2, cch=cch, ncol=ncol: e.matmul(
                        psA[:, cs * 512:(cs + 1) * 512] if es == 0 else psA[:, cs * 512:(cs + 1) * 512],
                        wn0[:, es * 256 + 0:0] if False else wn0[:].rearrange("p (d2 f) -> p d2 f", d2=2)[:, d2, es * 128:(es + 1) * 128],
                        xn2_ch[cch[d2]][:, ncol + cs * 512:ncol + (cs + 1) * 512],
                        start=(d2 == 0), stop=(d2 == 1)),
                        inc=("lt4" if d2 == 1 else None))
                    p.op("tensor", lambda e, es=es, cs=cs, d2=d2, cch=cch, ncol=ncol: e.matmul(
                        psB[:, cs * 512:(cs + 1) * 512],
                        wn2[:].rearrange("p (d2 f) -> p d2 f", d2=2)[:, d2, es * 128:(es + 1) * 128],
                        xn2_ch[cch[d2]][:, ncol + cs * 512:ncol + (cs + 1) * 512],
                        start=(d2 == 0), stop=(d2 == 1)),
                        inc=("lt4" if d2 == 1 else None))
                p.wait("scalar", "lt4", p.count("lt4"))
                p.op("scalar", lambda e, es=es, cs=cs: e.activation(
                    hq_sb[:, es * 1024 + cs * 512:es * 1024 + (cs + 1) * 512],
                    psA[:, cs * 512:(cs + 1) * 512], AF.Silu), inc="la4")
                p.wait("vector", "la4", p.count("la4"))
                p.op("vector", lambda e, es=es, cs=cs: e.tensor_copy(
                    he_sb[:, es * 1024 + cs * 512:es * 1024 + (cs + 1) * 512],
                    psB[:, cs * 512:(cs + 1) * 512]), inc="le4")
                p.op("vector", lambda e, es=es, cs=cs: e.tensor_mul(
                    he_sb[:, es * 1024 + cs * 512:es * 1024 + (cs + 1) * 512],
                    he_sb[:, es * 1024 + cs * 512:es * 1024 + (cs + 1) * 512],
                    hq_sb[:, es * 1024 + cs * 512:es * 1024 + (cs + 1) * 512]), inc="le4")
                p.op("vector", lambda e: e.engine_nop(), inc="le4n")
                if es * 2 + cs >= 1:
                    p.wait("tensor", "le4n", p.count("le4n") - 1)
        p.wait("tensor", "le4n", p.count("le4n"))
        for ds in range(2):
            for cs in range(2):
                par5 = (ds * 2 + cs) % 2
                pOQ = psC[:, 0:512] if par5 == 0 else psD[:, 0:512]
                if ds * 2 + cs >= 2:
                    p.wait("tensor", "le5n", p.count("le5n") - 1)
                for es in range(2):
                    p.op("tensor", lambda e, ds=ds, cs=cs, es=es, pOQ=pOQ: e.matmul(
                        pOQ,
                        wn1[:].rearrange("p (es f) -> p es f", es=2)[:, es, ds * 128:(ds + 1) * 128],
                        he_sb[:, es * 1024 + cs * 512:es * 1024 + (cs + 1) * 512],
                        start=(es == 0), stop=(es == 1)),
                        inc=("lt5" if es == 1 else None))
                p.wait("vector", "lt5", p.count("lt5"))
                p.wait("vector", "ldy", 8 * 16)
                p.op("vector", lambda e, ds=ds, cs=cs, cch=cch, ncol=ncol, pOQ=pOQ: e.scalar_tensor_tensor(
                    y1_ch[cch[ds]][:, ncol + cs * 512:ncol + (cs + 1) * 512], pOQ, 1.0,
                    y1_ch[cch[ds]][:, ncol + cs * 512:ncol + (cs + 1) * 512], ALU.mult, ALU.add), inc="le5")
                p.op("vector", lambda e: e.engine_nop(), inc="le5n")

    mh = [he_sb[:, 0:1024], he_sb[:, 1024:2048]]
    p.wait("sync", "stwc", 16)
    p.wait("sync", "le5n", 32)
    p.wait("vector", "le5n", 32)
    for sp in range(2):
        for ee in range(8):
            ie = sp * 8 + ee
            for hc in range(16):
                it = ie * 16 + hc
                buf = it % 2
                dq = "sync"
                if sp == 0 and hc <= 1:
                    p.wait(dq, "gcw1", 16 * (ee + 1))
                if it >= 2:
                    p.wait(dq, "temh", it - 1)
                p.dma(dq, lambda e, ee=ee, hc=hc, buf=buf: e.dma_start(
                    mh[buf][:].rearrange("p (c f) -> p c f", c=NCH),
                    bass.AP(mw1b_d, ee * D * 2048 + hc * 128,
                            [[2048, 128], [128 * 2048, NCH], [1, 128]])), inc=f"mhd{buf}")
                p.wait("tensor", f"mhd{buf}", 16 * (it // 2 + 1))
                ph2 = [psC[:], psD[:]] if it % 2 == 0 else [psA[:, 0:512], psA[:, 512:1024]]
                if it >= 2:
                    p.wait("tensor", "evh", 2 * (it - 1))
                if it % 2 == 1:
                    p.wait("tensor", "evd", 2 * 8 * ie)
                for dc in range(NCH):
                    for t2 in range(2):
                        p.op("tensor", lambda e, sp=sp, dc=dc, t2=t2, buf=buf, ph2=ph2: e.matmul(
                            ph2[t2],
                            mh[buf][:, dc * 128:(dc + 1) * 128],
                            xn1_ch[dc][:, (sp * 2 + t2) * 512:(sp * 2 + t2 + 1) * 512],
                            start=(dc == 0), stop=(dc == NCH - 1)),
                            inc=("temh" if (dc == NCH - 1 and t2 == 1) else None))
                p.wait("scalar", "temh", p.count("temh"))
                p.wait("scalar", "temd", 8 * ie)
                if it == 0:
                    p.wait("scalar", "stwc", 16)
                for t2 in range(2):
                    p.op("scalar", lambda e, hc=hc, t2=t2, ph2=ph2: e.activation(
                        hst[:, hc * 1024 + t2 * 512:hc * 1024 + (t2 + 1) * 512],
                        ph2[t2], AF.Silu), inc="evh")
            p.dma("scalar", lambda e, ee=ee, sp=sp: e.dma_start(
                wbc, bass.AP(wc_d, ee * T + sp * 1024, [[0, 128], [1, 512]])), inc="wbcd")
            p.dma("scalar", lambda e, ee=ee, sp=sp: e.dma_start(
                wbc2, bass.AP(wc_d, ee * T + sp * 1024 + 512, [[0, 128], [1, 512]])), inc="wbcd")
            for dc in range(NCH):
                it2 = ie * 8 + dc
                buf = it2 % 2
                dq2 = "scalar"
                if sp == 0 and dc <= 1:
                    p.wait(dq2, "gcw2", 16 * (ee + 1))
                if it2 >= 2:
                    p.wait(dq2, "temd", it2 - 1)
                p.dma(dq2, lambda e, ee=ee, dc=dc, buf=buf: e.dma_start(
                    mw2_sb[buf][:].rearrange("p (c f) -> p c f", c=16),
                    bass.AP(mw2b_d, ee * 2048 * D + dc * 128,
                            [[D, 128], [128 * D, 16], [1, 128]])), inc=f"md{buf}")
                p.wait("tensor", f"md{buf}", 16 * (it2 // 2 + 1))
                p.wait("tensor", "evh", p.count("evh"))
                if it2 >= 2:
                    p.wait("tensor", "evd", 2 * (it2 - 1))
                pd2 = psA if dc % 2 == 0 else psB
                for hc in range(16):
                    for t2 in range(2):
                        p.op("tensor", lambda e, dc=dc, hc=hc, t2=t2, buf=buf, pd2=pd2: e.matmul(
                            pd2[:, t2 * 512:(t2 + 1) * 512],
                            mw2_sb[buf][:, hc * 128:(hc + 1) * 128],
                            hst[:, hc * 1024 + t2 * 512:hc * 1024 + (t2 + 1) * 512],
                            start=(hc == 0), stop=(hc == 15)),
                            inc=("temd" if (hc == 15 and t2 == 1) else None))
                p.wait("vector", "temd", p.count("temd"))
                p.wait("vector", "wbcd", p.count("wbcd"))
                for t2 in range(2):
                    p.op("vector", lambda e, t2=t2, pd2=pd2: e.tensor_mul(
                        hq_sb[:, 0:512] if t2 == 0 else hq_sb[:, 512:1024],
                        pd2[:, t2 * 512:(t2 + 1) * 512],
                        wbc if t2 == 0 else wbc2), inc="evd0")
                p.wait("gpsimd", "evd0", p.count("evd0"))
                for t2 in range(2):
                    p.op("gpsimd", lambda e, sp=sp, dc=dc, t2=t2: e.tensor_add(
                        y1_ch[dc][:, (sp * 2 + t2) * 512:(sp * 2 + t2 + 1) * 512],
                        y1_ch[dc][:, (sp * 2 + t2) * 512:(sp * 2 + t2 + 1) * 512],
                        hq_sb[:, 0:512] if t2 == 0 else hq_sb[:, 512:1024]), inc="evd")
    p.wait("sync", "evd", p.count("evd"))
    for c in range(NCH):
        p.dma("sync", lambda e, c=c: e.dma_start(out_d.ap()[c * 128:(c + 1) * 128, :], y1_ch[c]), inc="sto")




def build_l3b2():
    nc = bass.Bass("TRN2", target_bir_lowering=False, debug=False)
    p = Prog(nc)
    io = dict(
        y1=nc.dram_tensor("y1_fm", [D, T], F32, kind="ExternalInput"),
        xn1=nc.dram_tensor("xn1_fm", [D, T], BF16, kind="ExternalInput"),
        xn2=nc.dram_tensor("xn2_fm", [D, T], BF16, kind="ExternalInput"),
        w0T=nc.dram_tensor("w0T", [4 * 256, 256], F32, kind="ExternalInput"),
        w2T=nc.dram_tensor("w2T", [4 * 256, 256], F32, kind="ExternalInput"),
        w1l=nc.dram_tensor("w1l", [4 * 256, 256], F32, kind="ExternalInput"),
        w1T=nc.dram_tensor("w1T", [4 * 256, 256], F32, kind="ExternalInput"),
        gate_wT=nc.dram_tensor("gate_wT", [D, 8], F32, kind="ExternalInput"),
        bal_b=nc.dram_tensor("bal_b", [1, 8], F32, kind="ExternalInput"),
        moew1T=nc.dram_tensor("moew1T", [8 * D, 2048], F32, kind="ExternalInput"),
        moew2T=nc.dram_tensor("moew2T", [8 * 2048, D], F32, kind="ExternalInput"),
        ident_b=nc.dram_tensor("ident_b", [128, 128], BF16, kind="ExternalInput"),
        out=nc.dram_tensor("out_fm", [D, T], F32, kind="ExternalOutput"),
        mw1b=nc.dram_tensor("mw1b", [8 * D, 2048], BF16, kind="Internal"),
        mw2b=nc.dram_tensor("mw2b", [8 * 2048, D], BF16, kind="Internal"),
        wc=nc.dram_tensor("wc_dram", [8, T], F32, kind="Internal"),
    )
    PB1 = p.ps("PB1", [128, 2048], F32)
    PB2 = p.ps("PB2", [128, 2048], F32)
    pp = dict(psA=PB1[:, 0:1024], psB=PB1[:, 1024:2048],
              psC=PB2[:, 0:512], psD=PB2[:, 512:1024],
              psT=PB2[:, 1024:1088].bitcast(BF16),
              psT2=PB2[:, 1536:1600].bitcast(BF16))
    emit_p5(p, io, pp)
    p.finish_and_emit()
    return nc


T8 = 8192


def emit_p1(p, io, pp):
    nc = p.nc
    x_d, wgT_d, wuT_d, wdT_d = io["x"], io["wgT"], io["wuT"], io["wdT"]
    xn_d, acc_d = io["xn"], io["acc"]
    pbig1, pbig2 = pp["PB1"], pp["PB2"]

    x_sb = p.sb("x_sb", [128, NCH * T], F32)
    xn_sb = p.sb("xn_sb", [128, NCH * T], BF16)
    x_ch = [x_sb[:, c * T:(c + 1) * T] for c in range(NCH)]
    xn_ch = [xn_sb[:, c * T:(c + 1) * T] for c in range(NCH)]
    lnsc = make_ln_scratch(p)
    wg_sb = [p.sb(f"wg{i}", [128, NCH * 128], BF16) for i in range(2)]
    wu_sb = [p.sb(f"wu{i}", [128, NCH * 128], BF16) for i in range(2)]
    wd_sb = [p.sb(f"wd{i}", [128, NH0 * 128], BF16) for i in range(2)]
    g_sb = p.sb("g_sb", [128, 1024], BF16)
    hTall = p.sb("hTall", [128, NH0 * 1024], BF16)
    ps4a = [pbig1[:, i * 512:(i + 1) * 512] for i in range(4)]
    ps4b = [pbig2[:, i * 512:(i + 1) * 512] for i in range(4)]

    for s4 in range(4):
        co = s4 * T
        ldx_base = p.count("ld_x")
        for c in range(NCH):
            if s4 >= 1:
                p.wait("sync", "st_acc", 16 * (8 * (s4 - 1) + c + 1))
            p.dma("sync", lambda e, c=c, co=co: e.dma_start(
                x_ch[c], x_d.ap()[c * 128:(c + 1) * 128, co:co + T]), inc="ld_x")
        if s4 >= 1:
            p.wait("tensor", "dve_acc", 32 * s4)
        ln_done = emit_ln_fm(p, f"l0{s4}", x_ch, xn_ch, pbig1, pbig2, lnsc,
                             x_chunks=("ld_x", ldx_base),
                             bc_sb=hTall[:].bitcast(F32)[:, 0:2 * T])
        p.wait("sync", f"l0{s4}_dve", ln_done)
        for c in range(NCH):
            p.dma("sync", lambda e, c=c, co=co: e.dma_start(
                xn_d.ap()[c * 128:(c + 1) * 128, co:co + T], xn_ch[c]), inc="st_xn")
        for hv in range(2):
          ho = hv * 1024
          for hc in range(NH0):
            buf = hc % 2
            it = (s4 * 2 + hv) * NH0 + hc
            src_g = bass.AP(wgT_d, hc * 128, [[HID0, 128], [128 * HID0, 8], [1, 128]])
            src_u = bass.AP(wuT_d, hc * 128, [[HID0, 128], [128 * HID0, 8], [1, 128]])
            if it >= 2:
                p.wait("gpsimd", "te_gu", it - 1)
            p.dma("gpsimd", lambda e, s=src_g, buf=buf: e.dma_start(
                wg_sb[buf][:].rearrange("p (c f) -> p c f", c=8), s), inc=f"wgu{buf}")
            p.dma("gpsimd", lambda e, s=src_u, buf=buf: e.dma_start(
                wu_sb[buf][:].rearrange("p (c f) -> p c f", c=8), s), inc=f"wgu{buf}")
            p.wait("tensor", f"wgu{buf}", 32 * (it // 2 + 1))
            pg = pbig1[:, (it % 2) * 1024:(it % 2 + 1) * 1024]
            pu = pbig2[:, (it % 2) * 1024:(it % 2 + 1) * 1024]
            if it >= 2:
                p.wait("tensor", "evac_gu", 3 * (it - 1))
            if hc <= 1 and it >= 2:
                p.wait("tensor", "dve_acc", 2 * NCH * (s4 * 2 + hv))
            for dc in range(NCH):
                if hc == 0 and hv == 0:
                    p.wait("tensor", f"l0{s4}_dve", ln_done - 2 * (NCH - 1 - dc))
                for tq in range(2):
                    p.op("tensor", lambda e, buf=buf, dc=dc, tq=tq, ho=ho, pg=pg: e.matmul(
                        pg[:, tq * 512:(tq + 1) * 512], wg_sb[buf][:, dc * 128:(dc + 1) * 128],
                        xn_ch[dc][:, ho + tq * 512:ho + (tq + 1) * 512],
                        start=(dc == 0), stop=(dc == NCH - 1)))
            for dc in range(NCH):
                for tq in range(2):
                    p.op("tensor", lambda e, buf=buf, dc=dc, tq=tq, ho=ho, pu=pu: e.matmul(
                        pu[:, tq * 512:(tq + 1) * 512], wu_sb[buf][:, dc * 128:(dc + 1) * 128],
                        xn_ch[dc][:, ho + tq * 512:ho + (tq + 1) * 512],
                        start=(dc == 0), stop=(dc == NCH - 1)),
                        inc=("te_gu" if (dc == NCH - 1 and tq == 1) else None))
            p.wait("scalar", "te_gu", p.count("te_gu"))
            if hc == 0:
                p.wait("scalar", "te_d", (s4 * 2 + hv) * NCH)
            p.op("scalar", lambda e, pg=pg: e.activation(g_sb[:], pg, AF.Silu, scale=QSC),
                 inc="evac_gu")
            p.op("scalar", lambda e, hc=hc, pu=pu: e.activation(
                hTall[:, hc * 1024:(hc + 1) * 1024], pu, AF.Copy, scale=QSC),
                inc="evac_gu")
            p.wait("vector", "evac_gu", p.count("evac_gu"))
            p.op("vector", lambda e, hc=hc: e.tensor_mul(
                hTall[:, hc * 1024:(hc + 1) * 1024], g_sb[:],
                hTall[:, hc * 1024:(hc + 1) * 1024]), inc="evac_gu")
          for dc in range(NCH):
            buf = dc % 2
            it = (s4 * 2 + hv) * NCH + dc
            src_d = bass.AP(wdT_d, dc * 128, [[D, 128], [128 * D, NH0], [1, 128]])
            if it >= 2:
                p.wait("gpsimd", "te_d", it - 1)
            p.dma("gpsimd", lambda e, s=src_d, buf=buf: e.dma_start(
                wd_sb[buf][:].rearrange("p (c f) -> p c f", c=NH0), s), inc=f"wdd{buf}")
            p.wait("tensor", f"wdd{buf}", 16 * (it // 2 + 1))
            pd2 = pbig1[:, (it % 2) * 1024:(it % 2 + 1) * 1024]
            if dc == 0:
                p.wait("tensor", "evac_gu", p.count("evac_gu"))
            if it >= 2:
                p.wait("tensor", "dve_acc", 2 * (it - 1))
            for hc in range(NH0):
                for tq in range(2):
                    p.op("tensor", lambda e, buf=buf, hc=hc, tq=tq, pd2=pd2: e.matmul(
                        pd2[:, tq * 512:(tq + 1) * 512], wd_sb[buf][:, hc * 128:(hc + 1) * 128],
                        hTall[:, hc * 1024 + tq * 512:hc * 1024 + (tq + 1) * 512],
                        start=(hc == 0), stop=(hc == NH0 - 1)),
                        inc=("te_d" if (hc == NH0 - 1 and tq == 1) else None))
            p.wait("vector", "te_d", p.count("te_d"))
            for tq in range(2):
                p.op("vector", lambda e, dc=dc, tq=tq, ho=ho, pd2=pd2: e.scalar_tensor_tensor(
                    x_ch[dc][:, ho + tq * 512:ho + (tq + 1) * 512],
                    pd2[:, tq * 512:(tq + 1) * 512], QSC,
                    x_ch[dc][:, ho + tq * 512:ho + (tq + 1) * 512], ALU.mult, ALU.add),
                    inc="dve_acc")
        acc_base = 32 * s4
        for c in range(NCH):
            p.wait("sync", "dve_acc", acc_base + 16 + 2 * (c + 1))
            p.dma("sync", lambda e, c=c, co=co: e.dma_start(
                acc_d.ap()[c * 128:(c + 1) * 128, co:co + T], x_ch[c]), inc="st_acc")


def emit_p2_tile(p, io, pp, dt8):
    nc = p.nc
    xn_d, y_d = io["xn"], io["y"]
    ldt_d, B_d, C_d, Dv_d = io["log_dt"], io["Bmat"], io["Cmat"], io["Dvec"]
    A_d, iA_d = io["A_bc"], io["invA_bc"]
    idb_d, jrev_d, jv129_d, gm_d = io["ident_b"], io["jrev"], io["jv129"], io["gmask"]
    krep_d, urev_d, v_d = io["krep"], io["urev"], io["vdram"]
    PB1, PB2 = pp["PB1"], pp["PB2"]
    pY = PB1[:, 0:512]
    pP = PB1[0:64, 512:1024]
    pT = PB2[0:64, 0:128]
    pTRb = [PB2[:, 512:576].bitcast(BF16), PB2[:, 1024:1088].bitcast(BF16),
            PB2[:, 1536:1600].bitcast(BF16), PB2[:, 0:64].bitcast(BF16)]
    ro = dt8 * 128

    ldt = p.sb("ldt", [128, 1], F32)
    Bs = p.sb("Bs", [128, 64], F32)
    Cs = p.sb("Cs", [128, 64], F32)
    Ds = p.sb("Ds", [128, 1], F32)
    Abc = p.sb("Abc", [128, 64], F32)
    iAbc = p.sb("iAbc", [128, 64], F32)
    idb = p.sb("idb", [128, 128], BF16)
    jrev = p.sb("jrev", [128, 128], F32)
    jv129 = p.sb("jv129", [128, 129], F32)
    gm = p.sb("gm", [64, 4], F32)
    dt_t = p.sb("dt_t", [128, 1], F32)
    dtA = p.sb("dtA", [128, 64], F32)
    ones64 = p.sb("ones64", [128, 64], F32)
    CBt = p.sb("CBt", [128, 64], F32)
    grid = p.sb("grid", [128, 129 * 64], F32)
    urev_sb = p.sb("urev_sb", [128, 8192], BF16)
    v_sb = p.sb("v_sb", [128, 64 * 129], BF16)
    K_sb = p.sb("K_sb", [128, 256], F32)
    K_bf = p.sb("K_bf", [128, 256], BF16)
    lamQ = p.sb("lamQ", [128, 64], F32)
    lamQT = p.sb("lamQT", [64, 128], F32)
    lamQ16 = p.sb("lamQ16", [128, 64], F32)
    lamQ16T = p.sb("lamQ16T", [64, 128], F32)
    lqp = p.sb("lqp", [64, 128 * 16], F32)
    pwt = p.sb("pwt", [64, 128], F32)
    xfm = p.sb("xfm", [128, 8192], BF16)
    xraw = p.sb("xraw", [128, 8192], BF16)
    y_sb = p.sb("ysb", [128, 8192], BF16)
    P_all = p.sb("Pall", [64, 8192], BF16)
    S_shift = p.sb("S_shift", [64, 128 * 65], F32)
    S_bf = grid[:].bitcast(BF16)[0:64, 0:8320]
    EndA = grid[0:64, 4500:5012]
    Dd = grid[0:64, 5012:5524]
    Outq = grid[0:64, 5524:5652]
    ctmp = grid[0:64, 5652:7700]
    T8b = [p.sb(f"T8_{i}", [128, 8 * 128], BF16) for i in range(2)]
    W8 = [p.sb(f"W8_{i}", [128, 8 * 64], BF16) for i in range(2)]
    M8 = [p.sb(f"M8_{i}", [64, 8 * 128], BF16) for i in range(2)]

    for t, d in ((idb, idb_d), (jrev, jrev_d), (jv129, jv129_d), (gm, gm_d),
                 (Abc, A_d), (iAbc, iA_d)):
        p.dma("sync", lambda e, t=t, d=d: e.dma_start(t[:], d.ap()), inc="lp")
    for t, d in ((ldt, ldt_d), (Bs, B_d), (Cs, C_d), (Ds, Dv_d)):
        p.dma("sync", lambda e, t=t, d=d: e.dma_start(t[:], d.ap()[ro:ro + 128, :]), inc="lp")
    for pc in range(8):
        p.dma("scalar", lambda e, pc=pc: e.dma_start(
            xfm[:, pc * 1024:(pc + 1) * 1024],
            xn_d.ap()[ro:ro + 128, pc * 1024:(pc + 1) * 1024]), inc="lx")

    p.wait("scalar", "lp", 160)
    p.op("scalar", lambda e: e.activation(dt_t[:], ldt[:], AF.Exp), inc="a")
    p.wait("vector", "a", p.count("a"))
    p.op("vector", lambda e: e.memset(ones64[:], 1.0))
    p.op("vector", lambda e: e.tensor_scalar_mul(dtA[:], Abc[:], dt_t[:, 0:1]))
    p.op("vector", lambda e: e.tensor_copy(CBt[:], dtA[:]), inc="v")
    p.wait("scalar", "v", p.count("v"))
    p.op("scalar", lambda e: e.activation(CBt[:], CBt[:], AF.Exp), inc="a")
    p.wait("vector", "a", p.count("a"))
    p.op("vector", lambda e: e.tensor_sub(CBt[:], CBt[:], ones64[:]))
    p.op("vector", lambda e: e.tensor_mul(CBt[:], CBt[:], iAbc[:]))
    p.op("vector", lambda e: e.tensor_mul(CBt[:], CBt[:], Bs[:]))
    p.op("vector", lambda e: e.tensor_mul(CBt[:], CBt[:], Cs[:]))
    p.op("vector", lambda e: e.memset(K_sb[:], 0.0))
    for hf in range(2):
        p.op("vector", lambda e, hf=hf: e.tensor_tensor(
            bass.AP(grid, hf * 4096, [[8256, 128], [64, 64], [1, 64]]),
            bass.AP(dtA, 0, [[64, 128], [0, 64], [1, 64]]),
            bass.AP(jv129, hf * 64, [[129, 128], [1, 64], [0, 64]]), ALU.mult), inc="v")
        p.wait("scalar", "v", p.count("v"))
        p.op("scalar", lambda e, hf=hf: e.activation(
            grid[:, hf * 4096:(hf + 1) * 4096], grid[:, hf * 4096:(hf + 1) * 4096],
            AF.Exp), inc="a")
    for hf in range(2):
        p.wait("vector", "a", hf + 1)
        p.op("vector", lambda e, hf=hf: e.tensor_tensor(
            grid[:, hf * 4096:(hf + 1) * 4096], grid[:, hf * 4096:(hf + 1) * 4096],
            bass.AP(CBt, 0, [[64, 128], [0, 64], [1, 64]]), ALU.mult))
        p.op("vector", lambda e, hf=hf: e.tensor_reduce(
            K_sb[:, 127 + hf * 64:127 + (hf + 1) * 64],
            grid[:].rearrange("p (j n) -> p j n", n=64)[:, hf * 64:(hf + 1) * 64, :],
            AX.X, ALU.add))
    p.op("vector", lambda e: e.tensor_add(K_sb[:, 127:128], K_sb[:, 127:128], Ds[:]))
    p.op("vector", lambda e: e.tensor_copy(K_bf[:], K_sb[:]), inc="kv")
    p.wait("sync", "kv", 1)
    p.dma("sync", lambda e: e.dma_start(
        bass.AP(krep_d, ro * 32768, [[32768, 128], [1, 256]]), K_bf[:]), inc="kr")
    for s in range(7):
        n = 256 << s
        p.wait("sync", "kr", 16 * (s + 1))
        p.dma("sync", lambda e, n=n: e.dma_start(
            bass.AP(krep_d, ro * 32768 + n, [[32768, 128], [1, n]]),
            bass.AP(krep_d, ro * 32768, [[32768, 128], [1, n]])), inc="kr")
    for hf in range(2):
        p.op("vector", lambda e, hf=hf: e.tensor_tensor(
            bass.AP(grid, hf * 4096, [[8256, 128], [64, 64], [1, 64]]),
            bass.AP(dtA, 0, [[64, 128], [0, 64], [1, 64]]),
            bass.AP(jrev, hf * 64, [[128, 128], [1, 64], [0, 64]]), ALU.mult), inc="v")
        p.wait("scalar", "v", p.count("v"))
        p.op("scalar", lambda e, hf=hf: e.activation(
            urev_sb[:, hf * 4096:(hf + 1) * 4096], grid[:, hf * 4096:(hf + 1) * 4096],
            AF.Exp), inc="au")
        p.wait("gpsimd", "au", hf + 1)
        p.dma("gpsimd", lambda e, hf=hf: e.dma_start(
            bass.AP(urev_d, ro * 8192 + hf * 4096, [[8192, 128], [1, 4096]]),
            urev_sb[:, hf * 4096:(hf + 1) * 4096]), inc="g")
    for hf in range(2):
        p.wait("vector", "au", 2)
        p.op("vector", lambda e, hf=hf: e.tensor_tensor(
            bass.AP(grid, hf * 4128, [[8256, 128], [129, 32], [1, 129]]),
            bass.AP(dtA, hf * 32, [[64, 128], [1, 32], [0, 129]]),
            bass.AP(jv129, 0, [[129, 128], [0, 32], [1, 129]]), ALU.mult), inc="v")
        p.wait("scalar", "v", p.count("v"))
        p.op("scalar", lambda e, hf=hf: e.activation(
            grid[:, hf * 4128:(hf + 1) * 4128], grid[:, hf * 4128:(hf + 1) * 4128],
            AF.Exp), inc="a")
        p.wait("vector", "a", p.count("a"))
        p.op("vector", lambda e, hf=hf: e.tensor_tensor(
            v_sb[:, hf * 4128:(hf + 1) * 4128], grid[:, hf * 4128:(hf + 1) * 4128],
            bass.AP(CBt, hf * 32, [[64, 128], [1, 32], [0, 129]]), ALU.mult), inc="vv")
    p.wait("scalar", "vv", 2)
    p.op("scalar", lambda e: e.activation(lamQ[:], dtA[:], AF.Exp, scale=128.0), inc="a")
    p.op("scalar", lambda e: e.activation(lamQ16[:], dtA[:], AF.Exp, scale=2048.0), inc="a")
    p.wait("gpsimd", "vv", 2)
    p.dma("gpsimd", lambda e: e.dma_start(
        bass.AP(v_d, ro * 8256, [[8256, 128], [1, 8256]]), v_sb[:]), inc="g")
    p.wait("tensor", "lx", 128)
    p.wait_abs("scalar", "sty")
    for tt in range(64):
        pb = pTRb[tt % 4]
        if tt >= 4:
            p.wait("tensor", "ex", tt - 3)
        p.op("tensor", lambda e, tt=tt, pb=pb: e.transpose(
            pb, xfm[:, tt * 128:(tt + 1) * 128], idb[:]), inc="tx")
        p.wait("scalar", "tx", tt + 1)
        p.op("scalar", lambda e, tt=tt, pb=pb: e.activation(
            xraw[:, tt * 128:(tt + 1) * 128], pb, AF.Copy), inc="ex")

    idf = p.sb("idf", [128, 128], F32)
    p.wait("vector", "lp", 160)
    p.op("vector", lambda e: e.tensor_copy(idf[:], idb[:]))
    p.op("vector", lambda e: e.engine_nop(), inc="idr")
    p.wait("tensor", "idr", 1)
    p.wait("tensor", "a", p.count("a"))
    p.wait("tensor", "ex", 64)
    p.op("tensor", lambda e: e.transpose(pT[:], lamQ[:, 0:64], idf[:]), inc="t1")
    p.wait("vector", "t1", 1)
    p.op("vector", lambda e: e.tensor_copy(lamQT[:], pT[:]), inc="vq")
    p.wait("tensor", "vq", 1)
    p.op("tensor", lambda e: e.transpose(pT[:], lamQ16[:, 0:64], idf[:]), inc="t1")
    p.wait("vector", "t1", 2)
    p.op("vector", lambda e: e.tensor_copy(lamQ16T[:], pT[:]), inc="vq")
    p.op("vector", lambda e: e.memset(pwt[:], 1.0))
    for t in range(16):
        p.op("vector", lambda e, t=t: e.tensor_copy(
            bass.AP(lqp, t, [[2048, 64], [16, 128]]), pwt[:]))
        if t < 15:
            p.op("vector", lambda e: e.tensor_mul(pwt[:], pwt[:], lamQT[:]))
    p.op("vector", lambda e: e.engine_nop(), inc="tabv")

    def rhs_x(ch):
        return bass.AP(xraw, ch, [[8192, 128], [128, 64]])

    p.op("vector", lambda e: e.memset(S_shift[:], 0.0))
    p.wait("tensor", "ex", 64)
    p.wait("sync", "kr", 8 * 16)
    nblk = 16
    for blk in range(nblk):
        buf = blk % 2
        tq_ = "sync" if blk % 2 == 0 else "scalar"
        if blk >= 2:
            p.wait(tq_, "te1", blk - 1)
        p.dma(tq_, lambda e, blk=blk, buf=buf: e.dma_start(
            T8b[buf][:], bass.AP(krep_d, (ro + blk * 8) * 32768 + 127,
                                 [[255, 128], [32768, 8], [1, 128]])), inc=f"tw{buf}")
        if blk >= 2:
            p.wait("gpsimd", "te1", blk - 1)
        p.wait("gpsimd", "g", 32)
        p.dma("gpsimd", lambda e, blk=blk, buf=buf: e.dma_start(
            W8[buf][:], bass.AP(urev_d, (ro + blk * 8) * 8192,
                                [[64, 128], [8192, 8], [1, 64]])), inc=f"w8{buf}")
        p.wait("tensor", f"tw{buf}", 16 * (blk // 2 + 1))
        p.wait("tensor", f"w8{buf}", 16 * (blk // 2 + 1))
        if blk >= 1:
            p.wait("tensor", "ev1", 2 * blk)
        for i in range(8):
            ch = blk * 8 + i
            p.op("tensor", lambda e, i=i, buf=buf, ch=ch: e.matmul(
                pY[:, i * 64:(i + 1) * 64], T8b[buf][:, i * 128:(i + 1) * 128],
                rhs_x(ch), start=True, stop=True))
            p.op("tensor", lambda e, i=i, buf=buf, ch=ch: e.matmul(
                pP[:, i * 64:(i + 1) * 64], W8[buf][:, i * 64:(i + 1) * 64],
                rhs_x(ch), start=True, stop=True),
                inc=("te1" if i == 7 else None))
        p.wait("vector", "te1", blk + 1)
        p.op("vector", lambda e, blk=blk: e.tensor_copy(
            y_sb[:, blk * 512:(blk + 1) * 512], pY[:]), inc="ev1")
        p.op("vector", lambda e, blk=blk: e.tensor_copy(
            P_all[0:64, blk * 512:(blk + 1) * 512], pP[:]), inc="ev1")
        for d in range(blk * 8, blk * 8 + 8):
            p.op("vector", lambda e, d=d: e.tensor_tensor_scan(
                S_shift[0:64, d * 65 + 1:d * 65 + 65],
                lamQT[:, d:d + 1].broadcast_to((64, 64)),
                P_all[0:64, d * 64:(d + 1) * 64],
                0.0, ALU.mult, ALU.add))

    p.wait("vector", "tabv", 1)
    for j in range(4):
        p.op("vector", lambda e, j=j: e.tensor_copy(
            EndA[:, j * 128:(j + 1) * 128],
            bass.AP(S_shift, 16 * j + 16, [[8320, 64], [65, 128]])))
    p.op("vector", lambda e: e.memset(Dd[:], 0.0))
    for pas in range(2):
        if pas == 1:
            p.op("vector", lambda e: e.tensor_scalar_mul(
                Dd[:, 0:128], Outq[:], gm[:, 0:1]))
        for j in range(4):
            p.op("vector", lambda e, j=j: e.tensor_mul(
                Outq[:], lamQ16T[:], Dd[:, j * 128:(j + 1) * 128]))
            p.op("vector", lambda e, j=j: e.tensor_add(
                Outq[:], Outq[:], EndA[:, j * 128:(j + 1) * 128]))
            if j < 3:
                p.op("vector", lambda e, j=j: e.tensor_scalar_mul(
                    Dd[:, (j + 1) * 128:(j + 2) * 128], Outq[:], gm[:, j + 1:j + 2]))
                p.op("vector", lambda e, j=j: e.tensor_sub(
                    Dd[:, (j + 1) * 128:(j + 2) * 128],
                    Dd[:, (j + 1) * 128:(j + 2) * 128],
                    EndA[:, j * 128:(j + 1) * 128]))
    for dq in range(4):
        for j in range(4):
            p.op("vector", lambda e, j=j, dq=dq: e.tensor_tensor(
                ctmp[:, 0:512],
                bass.AP(lqp, dq * 32 * 16, [[2048, 64], [16, 32], [1, 16]]),
                bass.AP(grid, 5012 + j * 128 + dq * 32, [[8256, 64], [1, 32], [0, 16]]),
                ALU.mult))
            p.op("vector", lambda e, j=j, dq=dq: e.tensor_tensor(
                bass.AP(S_shift, 16 * j + dq * 32 * 65, [[8320, 64], [65, 32], [1, 16]]),
                bass.AP(S_shift, 16 * j + dq * 32 * 65, [[8320, 64], [65, 32], [1, 16]]),
                ctmp[:, 0:512], ALU.add))
        p.op("vector", lambda e, dq=dq: e.tensor_copy(
            S_bf[:, dq * 32 * 65:(dq + 1) * 32 * 65],
            S_shift[:, dq * 32 * 65:(dq + 1) * 32 * 65]), inc="scan")

    for blk in range(nblk):
        buf = blk % 2
        if blk == 0:
            p.wait("scalar", "g", 48)
        if blk >= 2:
            p.wait("scalar", "teM", blk - 1)
        p.dma("scalar", lambda e, blk=blk, buf=buf: e.dma_start(
            M8[buf][:], bass.AP(v_d, (ro + blk * 8) * 8256 + 1,
                                [[129, 64], [8256, 8], [1, 128]])), inc=f"md{buf}")
        p.wait("tensor", f"md{buf}", 16 * (blk // 2 + 1))
        p.wait("tensor", "scan", blk // 4 + 1)
        if blk >= 1:
            p.wait("tensor", "evM", blk)
        for i in range(8):
            ch = blk * 8 + i
            p.op("tensor", lambda e, i=i, buf=buf, ch=ch: e.matmul(
                pY[:, i * 64:(i + 1) * 64], M8[buf][:, i * 128:(i + 1) * 128],
                S_bf[0:64, ch * 65:ch * 65 + 64], start=True, stop=True),
                inc=("teM" if i == 7 else None))
        p.wait("vector", "teM", blk + 1)
        p.op("vector", lambda e, blk=blk: e.tensor_add(
            y_sb[:, blk * 512:(blk + 1) * 512],
            y_sb[:, blk * 512:(blk + 1) * 512], pY[:]), inc="evM")

    out_r = xraw[:].rearrange("p (c d) -> p c d", c=64)
    in_r = bass.AP(y_sb, 0, [[8192, 128], [1, 64], [64, 128]])
    p.wait("vector", "evM", nblk)
    p.op("vector", lambda e, o=out_r, i=in_r: e.tensor_copy(o, i), inc="rs")
    p.wait("sync", "rs", 1)
    dst = bass.AP(y_d, ro, [[1024, 128], [128 * 1024, 64], [1, 128]])
    p.dma("sync", lambda e, dst=dst: e.dma_start(
        dst, xraw[:].rearrange("p (c d) -> p c d", c=64)), inc="sty")


def emit_p3_stripe(p, io, pp, s4):
    nc = p.nc
    acc_d, y_d = io["acc"], io["y"]
    wT_d, ib_d, lw_d, lb_d = io["in_projT"], io["in_proj_b"], io["ln_w"], io["ln_b"]
    x1_d, xn1_d, xn2_d = io["x1"], io["xn1"], io["xn2"]
    q_d, K_d, V_d, idb_d = io["q"], io["K"], io["V"], io["ident_b"]
    pbig1, pbig2 = pp["PB1"], pp["PB2"]
    co = s4 * T

    x1_sb = p.sb("x1_sb", [128, NCH * T], F32)
    xn1_sb = p.sb("xn1_sb", [128, NCH * T], BF16)
    xn2_sb = p.sb("xn2_sb", [128, NCH * T], BF16)
    x1_ch = [x1_sb[:, c * T:(c + 1) * T] for c in range(NCH)]
    xn1_ch = [xn1_sb[:, c * T:(c + 1) * T] for c in range(NCH)]
    xn2_ch = [xn2_sb[:, c * T:(c + 1) * T] for c in range(NCH)]
    lw_sb = p.sb("lw_sb", [128, NCH], F32)
    lb_sb = p.sb("lb_sb", [128, NCH], F32)
    qb_sb = p.sb("qb_sb", [128, 16], F32)
    idb = p.sb("idb", [128, 128], BF16)
    yst = [p.sb(f"yst{i}", [128, 16 * 128], BF16) for i in range(2)]
    wqk = [p.sb(f"wqk{i}", [128, NCH * 128], BF16) for i in range(2)]
    wv_sb = p.sb("wv_sb", [128, NCH * D], BF16)
    ev = [p.sb(f"ev{i}", [128, 1024], BF16) for i in range(2)]
    pTRb = [pbig2[:, 0:64].bitcast(BF16), pbig2[:, 512:576].bitcast(BF16)]

    for c in range(NCH):
        p.dma("sync", lambda e, c=c: e.dma_start(
            x1_ch[c], acc_d.ap()[c * 128:(c + 1) * 128, co:co + T]), inc="ldx1")
    p.dma("sync", lambda e: e.dma_start(idb[:], idb_d.ap()), inc="ldb")
    for c in range(NCH):
        p.dma("sync", lambda e, c=c: e.dma_start(lw_sb[:, c:c + 1], lw_d.ap()[c * 128:(c + 1) * 128, :]), inc="ldb")
        p.dma("sync", lambda e, c=c: e.dma_start(lb_sb[:, c:c + 1], lb_d.ap()[c * 128:(c + 1) * 128, :]), inc="ldb")
    for c in range(16):
        p.dma("sync", lambda e, c=c: e.dma_start(qb_sb[:, c:c + 1], ib_d.ap()[c * 128:(c + 1) * 128, :]), inc="ldb")
    p.dma("gpsimd", lambda e: e.dma_start(
        wv_sb[:].rearrange("p (c f) -> p c f", c=NCH),
        bass.AP(wT_d, 2 * D, [[3 * D, 128], [128 * 3 * D, NCH], [1, D]])), inc="gwv")

    p.wait("vector", "ldx1", NCH * 16)
    p.wait("tensor", "ldb", 33 * 16)
    for c in range(NCH):
        buf = c % 2
        if c >= 2:
            p.wait("scalar", "jt", (c - 1) * 16)
        src = bass.AP(y_d, co * D + c * 128, [[D, 128], [128 * D, 16], [1, 128]])
        p.dma("scalar", lambda e, src=src, buf=buf: e.dma_start(
            yst[buf][:].rearrange("p (t f) -> p t f", t=16), src), inc=f"jy{buf}")
        p.wait("tensor", f"jy{buf}", 16 * (c // 2 + 1))
        for tt in range(16):
            nj = c * 16 + tt
            if nj >= 2:
                p.wait("tensor", "ja", nj - 1)
            p.op("tensor", lambda e, buf=buf, tt=tt, nj=nj: e.transpose(
                pTRb[nj % 2], yst[buf][:, tt * 128:(tt + 1) * 128], idb[:]), inc="jt")
            p.wait("vector", "jt", nj + 1)
            p.op("vector", lambda e, c=c, tt=tt, nj=nj: e.tensor_add(
                x1_ch[c][:, tt * 128:(tt + 1) * 128],
                x1_ch[c][:, tt * 128:(tt + 1) * 128],
                pTRb[nj % 2]), inc="ja")
    lnsc = make_ln_scratch(p)
    bc1 = xn2_sb[:].bitcast(F32)[:, 0:2 * T]
    ln1_done = emit_ln_fm(p, "n1", x1_ch, xn1_ch, pbig1, pbig2, lnsc,
                          x_ready=("ja", nj), bc_sb=bc1)
    lnw = [lw_sb[:, c:c + 1] for c in range(NCH)]
    lnb = [lb_sb[:, c:c + 1] for c in range(NCH)]
    p.wait("vector", "ldb", 33 * 16)
    bc2 = None
    ln2_done = emit_ln_fm(p, "n2", xn1_ch, xn2_ch, pbig1, pbig2, lnsc,
                          lnw=lnw, lnb=lnb, x_ready=("n1_dve", ln1_done),
                          x_prod=("n1_dve", ln1_done - 16, 2), bc_sb=bc2)

    if s4 == 0:
        p.wait("sync", "n1_dve", ln1_done)
        for c in range(NCH):
            p.dma("sync", lambda e, c=c: e.dma_start(
                xn1_d.ap()[c * 128:(c + 1) * 128, :], xn1_ch[c]), inc="st1")
            p.dma("sync", lambda e, c=c: e.dma_start(
                x1_d.ap()[c * 128:(c + 1) * 128, :], x1_ch[c]), inc="st1")
        p.wait("sync", "n2_dve", ln2_done)
        for c in range(NCH):
            p.dma("sync", lambda e, c=c: e.dma_start(
                xn2_d.ap()[c * 128:(c + 1) * 128, :], xn2_ch[c]), inc="st1")

    ps_qk = [pbig1[:, i * 512:(i + 1) * 512] for i in range(4)]
    oc_list = list(range(16)) if s4 == 0 else list(range(8, 16))
    for idx, oc in enumerate(oc_list):
        buf = idx % 2
        if idx >= 2:
            p.wait("gpsimd", "teqk", idx - 1)
        p.dma("gpsimd", lambda e, oc=oc, buf=buf: e.dma_start(
            wqk[buf][:].rearrange("p (c f) -> p c f", c=NCH),
            bass.AP(wT_d, oc * 128, [[3 * D, 128], [128 * 3 * D, NCH], [1, 128]])), inc=f"gqk{buf}")
        p.wait("tensor", f"gqk{buf}", 16 * (idx // 2 + 1))
        if (bc2 is None) or idx > 0:
            p.wait("tensor", "n2_dve", ln2_done)
        if idx >= 1:
            p.wait("tensor", "evqk", (idx - 1) * 4 + 4)
        for dc in range(NCH):
            if bc2 is not None and idx == 0:
                p.wait("tensor", "n2_dve", ln2_done - 3 * (NCH - 1 - dc))
            for s in range(4):
                p.op("tensor", lambda e, s=s, dc=dc, buf=buf: e.matmul(
                    ps_qk[s], wqk[buf][:, dc * 128:(dc + 1) * 128],
                    xn2_ch[dc][:, s * 512:(s + 1) * 512],
                    start=(dc == 0), stop=(dc == NCH - 1)),
                    inc=("teqk" if (dc == NCH - 1 and s == 3) else None))
        p.wait("vector", "teqk", p.count("teqk"))
        for s in range(4):
            nev = idx * 4 + s
            if nev >= 2:
                p.wait("vector", f"stq{nev % 2}", 16 * (nev // 2))
            p.op("vector", lambda e, oc=oc, s=s, buf=buf: e.tensor_scalar_add(
                ev[buf][:, (s % 2) * 512:(s % 2 + 1) * 512],
                ps_qk[s], qb_sb[:, oc:oc + 1]), inc="evqk")
            if oc < 8:
                tgt, row, coo = q_d, oc * 128, 0
            else:
                tgt, row, coo = K_d, (oc - 8) * 128, co
            p.wait("sync", "evqk", p.count("evqk"))
            p.dma("sync", lambda e, tgt=tgt, row=row, s=s, buf=buf, coo=coo: e.dma_start(
                tgt.ap()[row:row + 128, coo + s * 512:coo + (s + 1) * 512],
                ev[buf][:, (s % 2) * 512:(s % 2 + 1) * 512]), inc=f"stq{(idx * 4 + s) % 2}")

    for tt in range(16):
        buf = tt % 2
        p.wait("tensor", "gwv", 16)
        if tt >= 1:
            p.wait("tensor", "evv", tt)
        for dc in range(NCH):
            for h in range(2):
                p.op("tensor", lambda e, tt=tt, dc=dc, h=h: e.matmul(
                    pbig2[:, h * 512:(h + 1) * 512],
                    xn2_ch[dc][:, tt * 128:(tt + 1) * 128],
                    wv_sb[:, dc * D + h * 512: dc * D + (h + 1) * 512],
                    start=(dc == 0), stop=(dc == NCH - 1)),
                    inc=("tev" if dc == NCH - 1 and h == 1 else None))
        p.wait("vector", "tev", p.count("tev"))
        if tt >= 2:
            p.wait("vector", f"sv{buf}", 16 * (tt // 2))
        p.op("vector", lambda e, buf=buf: e.tensor_copy(ev[buf][:], pbig2[:, 0:1024]), inc="evv")
        p.wait("sync", "evv", p.count("evv"))
        p.dma("sync", lambda e, tt=tt, buf=buf: e.dma_start(
            V_d.ap()[co + tt * 128:co + (tt + 1) * 128, :], ev[buf][:]), inc=f"sv{buf}")


def build_fused(dbg=False):
    nc = bass.Bass("TRN2", target_bir_lowering=False, debug=False)
    p = Prog(nc)
    EI, INT = "ExternalInput", "Internal"
    io = dict(
        x=nc.dram_tensor("x_rot_fm", [D, T8], F32, kind=EI),
        wgT=nc.dram_tensor("wgT", [D, HID0], I32, kind=EI),
        wuT=nc.dram_tensor("wuT", [D, HID0], I32, kind=EI),
        wdT=nc.dram_tensor("wdT", [HID0, D], I32, kind=EI),
        log_dt=nc.dram_tensor("log_dt", [D, 1], F32, kind=EI),
        Bmat=nc.dram_tensor("Bmat", [D, 64], F32, kind=EI),
        Cmat=nc.dram_tensor("Cmat", [D, 64], F32, kind=EI),
        Dvec=nc.dram_tensor("Dvec", [D, 1], F32, kind=EI),
        A_bc=nc.dram_tensor("A_bc", [128, 64], F32, kind=EI),
        invA_bc=nc.dram_tensor("invA_bc", [128, 64], F32, kind=EI),
        ident_b=nc.dram_tensor("ident_b", [128, 128], BF16, kind=EI),
        jrev=nc.dram_tensor("jrev", [128, 128], F32, kind=EI),
        jv129=nc.dram_tensor("jv129", [128, 129], F32, kind=EI),
        gmask=nc.dram_tensor("gmask", [64, 4], F32, kind=EI),
        in_projT=nc.dram_tensor("in_projT", [D, 3 * D], F32, kind=EI),
        in_proj_b=nc.dram_tensor("in_proj_b", [3 * D, 1], F32, kind=EI),
        ln_w=nc.dram_tensor("ln_w", [D, 1], F32, kind=EI),
        ln_b=nc.dram_tensor("ln_b", [D, 1], F32, kind=EI),
        out_projT=nc.dram_tensor("out_projT", [D, D], F32, kind=EI),
        out_proj_b=nc.dram_tensor("out_proj_b", [D, 1], F32, kind=EI),
        v_bias=nc.dram_tensor("v_bias", [D, 1], F32, kind=EI),
        w0T=nc.dram_tensor("w0T", [4 * 256, 256], F32, kind=EI),
        w2T=nc.dram_tensor("w2T", [4 * 256, 256], F32, kind=EI),
        w1l=nc.dram_tensor("w1l", [4 * 256, 256], F32, kind=EI),
        w1T=nc.dram_tensor("w1T", [4 * 256, 256], F32, kind=EI),
        gate_wT=nc.dram_tensor("gate_wT", [D, 8], F32, kind=EI),
        bal_b=nc.dram_tensor("bal_b", [1, 8], F32, kind=EI),
        moew1T=nc.dram_tensor("moew1T", [8 * D, 2048], F32, kind=EI),
        moew2T=nc.dram_tensor("moew2T", [8 * 2048, D], F32, kind=EI),
        xn=nc.dram_tensor("xn_i", [D, T8], BF16, kind=INT),
        acc=nc.dram_tensor("acc_i", [D, T8], F32, kind=INT),
        krep=nc.dram_tensor("krep_i", [D, 32768], BF16, kind=INT),
        urev=nc.dram_tensor("urev_i", [D, 8192], BF16, kind=INT),
        vdram=nc.dram_tensor("vdram_i", [D, 8256], BF16, kind=INT),
        y=nc.dram_tensor("y_i", [T8, D], BF16, kind=INT),
        x1=nc.dram_tensor("x1_i", [D, T], F32, kind=INT),
        xn1=nc.dram_tensor("xn1_i", [D, T], BF16, kind=INT),
        xn2=nc.dram_tensor("xn2_i", [D, T], BF16, kind=INT),
        q=nc.dram_tensor("q_i", [D, T], BF16, kind=INT),
        K=nc.dram_tensor("K_i", [D, T8], BF16, kind=INT),
        V=nc.dram_tensor("V_i", [T8, D], BF16, kind=INT),
        y1=nc.dram_tensor("y1_i", [D, T], F32, kind=INT),
        mw1b=nc.dram_tensor("mw1b", [8 * D, 2048], BF16, kind=INT),
        mw2b=nc.dram_tensor("mw2b", [8 * 2048, D], BF16, kind=INT),
        wc=nc.dram_tensor("wc_dram", [8, T], F32, kind=INT),
        out=nc.dram_tensor("out_fm", [D, T], F32, kind="ExternalOutput"),
    )
    PB1 = p.ps("PB1", [128, 2048], F32)
    PB2 = p.ps("PB2", [128, 2048], F32)
    ppA = dict(PB1=PB1, PB2=PB2)
    pp4 = dict(psA=PB1[:, 0:1024], psB=PB1[:, 1024:2048],
               psE=PB2[0:65, 0:512], psF=PB2[0:65, 512:1024],
               psC=PB2[:, 1024:1536], psD=PB2[:, 1536:2048])
    pp5 = dict(psA=PB1[:, 0:1024], psB=PB1[:, 1024:2048],
               psC=PB2[:, 0:512], psD=PB2[:, 512:1024],
               psT=PB2[:, 1024:1088].bitcast(BF16),
               psT2=PB2[:, 1536:1600].bitcast(BF16))

    if dbg:
        dbg_keys = ["xn", "acc", "y", "x1", "xn1", "xn2", "q", "K", "V", "y1"]
        for kk in dbg_keys:
            t = io[kk]
            io["dbg_" + kk] = nc.dram_tensor("dbg_" + kk, list(t.shape), t.dtype,
                                             kind="ExternalOutput")
    emit_p1(p, io, ppA)
    p.barrier()
    for dt8 in range(NCH):
        emit_p2_tile(p, io, ppA, dt8)
        p.barrier(skip=("sty",) if dt8 < NCH - 1 else ())
    for s4 in range(4):
        emit_p3_stripe(p, io, ppA, s4)
        p.barrier()
    io4 = dict(io)
    emit_p4(p, io4, pp4)
    p.barrier()
    emit_p5(p, io, pp5)
    if dbg:
        p.barrier()
        for kk in dbg_keys:
            p.dma("sync", lambda e, kk=kk: e.dma_start(
                io["dbg_" + kk].ap(), io[kk].ap()), inc="dbgo")
    p.finish_and_emit()
    nins = sum(len(q) for q in p.q.values())
    return nc, nins


def kernel(**inputs):
    x = np.asarray(inputs["L0_attn_x"])
    wg = np.asarray(inputs["L0_ffn_w_gate"]); wu = np.asarray(inputs["L0_ffn_w_up"])
    wd = np.asarray(inputs["L0_ffn_w_down"])
    log_dt = np.asarray(inputs["L0_attn_log_dt"])
    Bm = np.asarray(inputs["L0_attn_B"]); Cm = np.asarray(inputs["L0_attn_C"])
    Dv = np.asarray(inputs["L0_attn_D"])
    lnw = np.asarray(inputs["L1_attn_ln_w"]); lnb = np.asarray(inputs["L1_attn_ln_b"])
    ipw = np.asarray(inputs["L1_attn_in_proj_w"]); ipb = np.asarray(inputs["L1_attn_in_proj_b"])
    opw = np.asarray(inputs["L1_attn_out_proj_w"]); opb = np.asarray(inputs["L1_attn_out_proj_b"])
    w0 = np.asarray(inputs["L1_attn_w0"]); w1 = np.asarray(inputs["L1_attn_w1"])
    w2 = np.asarray(inputs["L1_attn_w2"])
    gw = np.asarray(inputs["L1_ffn_gate_w"]); bal = np.asarray(inputs["L1_ffn_balance_bias"])
    mw1 = np.asarray(inputs["L1_ffn_w1"]); mw2 = np.asarray(inputs["L1_ffn_w2"])

    common = dict(
        wgT=np.ascontiguousarray(wg.T), wuT=np.ascontiguousarray(wu.T),
        wdT=np.ascontiguousarray(wd.T),
        log_dt=log_dt.reshape(-1, 1), Bmat=Bm, Cmat=Cm, Dvec=Dv.reshape(-1, 1),
        A_bc=np.tile((-0.5 * np.arange(1, 65)).astype(np.float32), (128, 1)),
        invA_bc=np.tile((1.0 / (-0.5 * np.arange(1, 65))).astype(np.float32), (128, 1)),
        ident_b=np.eye(128, dtype=BF16NP),
        jrev=np.tile((127.0 - np.arange(128)).astype(np.float32), (128, 1)),
        jv129=np.tile(np.arange(129, dtype=np.float32), (128, 1)),
        in_projT=np.ascontiguousarray(ipw.T), in_proj_b=ipb.reshape(-1, 1),
        ln_w=lnw.reshape(-1, 1), ln_b=lnb.reshape(-1, 1),
        out_projT=np.ascontiguousarray(opw.T), out_proj_b=opb.reshape(-1, 1),
        v_bias=ipb[2048:].reshape(-1, 1),
        w0T=np.ascontiguousarray(np.transpose(w0, (0, 2, 1))).reshape(4 * 256, 256),
        w2T=np.ascontiguousarray(np.transpose(w2, (0, 2, 1))).reshape(4 * 256, 256),
        w1l=np.ascontiguousarray(w1).reshape(4 * 256, 256),
        w1T=np.ascontiguousarray(np.transpose(w1, (0, 2, 1))).reshape(4 * 256, 256),
        gate_wT=np.ascontiguousarray(gw.T), bal_b=bal.reshape(1, 8),
        moew1T=np.ascontiguousarray(np.transpose(mw1, (0, 2, 1))).reshape(8 * 1024, 2048),
        moew2T=np.ascontiguousarray(np.transpose(mw2, (0, 2, 1))).reshape(8 * 2048, 1024),
    )
    in_maps = []
    for c in range(8):
        b, r = c // 4, c % 4
        xb = x[b]
        xrot = np.concatenate([xb[r * T:], xb[:r * T]], axis=0)
        g = np.ones(4, np.float32)
        g[(4 - r) % 4] = 0.0
        m = dict(common)
        m["x_rot_fm"] = np.ascontiguousarray(xrot.T)
        m["gmask"] = np.tile(g, (64, 1))
        in_maps.append(m)
    nc, nins = build_fused(dbg=bool(inputs.get("_dbg")))
    res = run_bass_kernel_spmd(nc, in_maps, core_ids=list(range(8)))
    if inputs.get("_dbg"):
        kernel._dbg_res = res
    out = np.zeros((2, T8, D), np.float32)
    for c in range(8):
        b, r = c // 4, c % 4
        out[b, r * T:(r + 1) * T] = np.asarray(res.results[c]["out_fm"]).T
    return np.ascontiguousarray(out)
